# version 01
# baseline (speedup 1.0000x reference)
import numpy as np
import ml_dtypes
import concourse.bass as bass
import concourse.mybir as mybir
from concourse.bass_utils import run_bass_kernel_spmd
from bass_rust import IndirectOffsetOnAxis

F32 = mybir.dt.float32
BF16 = mybir.dt.bfloat16
U32 = mybir.dt.uint32
AF = mybir.ActivationFunctionType
OP = mybir.AluOpType
AX = mybir.AxisListType
NBF = ml_dtypes.bfloat16

T = 4096
D = 1024
G = 8
NT = T // 128
NCH = T // 512
EPS = 1e-5


class Cnt:
    def __init__(self, nc, name):
        self.h = nc.alloc_semaphore(name)
        self.v = 0

    def inc(self, instr, n=1):
        instr.then_inc(self.h, n)
        self.v += n


def build_ln(with_xhat):
    nc = bass.Bass("TRN2", target_bir_lowering=False, detect_race_conditions=False)
    x = nc.dram_tensor("x", [T, D], F32, kind="ExternalInput").ap()
    iden = nc.dram_tensor("iden", [128, 128], F32, kind="ExternalInput").ap()
    xn_fm32 = nc.dram_tensor("xn_fm32", [D, T], F32, kind="ExternalOutput").ap()
    xn_fmb = nc.dram_tensor("xn_fmb", [D, T], BF16, kind="ExternalOutput").ap()
    xn_tok = nc.dram_tensor("xn_tok", [T, D], F32, kind="ExternalOutput").ap()
    if with_xhat:
        xh_fmb = nc.dram_tensor("xh_fmb", [D, T], BF16, kind="ExternalOutput").ap()
        xh_tok = nc.dram_tensor("xh_tok", [T, D], BF16, kind="ExternalOutput").ap()

    sx = [nc.alloc_sbuf_tensor(f"sx{i}", [128, D], F32) for i in range(2)]
    s_sq = nc.alloc_sbuf_tensor("s_sq", [128, D], F32)
    smc = nc.alloc_sbuf_tensor("smc", [128, NT], F32)
    ssc = nc.alloc_sbuf_tensor("ssc", [128, NT], F32)
    mean = nc.alloc_sbuf_tensor("mean", [128, NT], F32)
    rstd = nc.alloc_sbuf_tensor("rstd", [128, NT], F32)
    cc = nc.alloc_sbuf_tensor("cc", [128, NT], F32)
    tmp = [nc.alloc_sbuf_tensor(f"tmp{i}", [128, NT], F32) for i in range(4)]
    s_iden = nc.alloc_sbuf_tensor("s_iden", [128, 128], F32)
    s_idenb = nc.alloc_sbuf_tensor("s_idenb", [128, 128], BF16)
    sxn = [nc.alloc_sbuf_tensor(f"sxn{i}", [128, D], F32) for i in range(2)]
    sxh = [nc.alloc_sbuf_tensor(f"sxh{i}", [128, D], BF16) for i in range(2)]
    tf32 = [nc.alloc_sbuf_tensor(f"tf32_{i}", [128, D], F32) for i in range(2)]
    tfb = [nc.alloc_sbuf_tensor(f"tfb_{i}", [128, D], BF16) for i in range(2)]
    thb = [nc.alloc_sbuf_tensor(f"thb_{i}", [128, D], BF16) for i in range(2)]
    pt = [nc.alloc_psum_tensor(f"pt{i}", [128, 128], F32) for i in range(2)]
    ptb = [nc.alloc_psum_tensor(f"ptb{i}", [128, 128], BF16) for i in range(2)]

    eps_col = nc.alloc_sbuf_tensor("eps_col", [128, 1], F32)
    eps2_col = nc.alloc_sbuf_tensor("eps2_col", [128, 1], F32)
    ld = Cnt(nc, "ld"); su = Cnt(nc, "su"); sv = Cnt(nc, "sv")
    ld2 = Cnt(nc, "ld2"); nrm = Cnt(nc, "nrm")
    tp = Cnt(nc, "tp"); tph = Cnt(nc, "tph"); ev = Cnt(nc, "ev"); evh = Cnt(nc, "evh")
    st = Cnt(nc, "st"); stok = Cnt(nc, "stok")
    c1 = Cnt(nc, "c1"); c2 = Cnt(nc, "c2")

    _all_sems = [ld, su, sv, ld2, nrm, tp, tph, ev, evh, st, stok, c1, c2]
    with nc.Block() as blk0:
        @blk0.sync
        def _(q):
            for _s in _all_sems:
                q.sem_clear(_s.h)

    with nc.Block() as blk:
        @blk.sync
        def _(q):
            q.dma_start(out=s_iden[:], in_=iden).then_inc(ld.h, 16)
            for i in range(NT):
                if i >= 2:
                    q.wait_ge(su.h, i - 1)
                    q.wait_ge(sv.h, i - 1)
                q.dma_start(out=sx[i % 2][:], in_=x[128 * i:128 * (i + 1), :]).then_inc(ld.h, 16)
        ld.v = 16 * (NT + 1)

        @blk.scalar
        def _(s):
            for i in range(NT):
                s.wait_ge(ld.h, 16 * (i + 2))
                s.activation(out=s_sq[:], in_=sx[i % 2][:], func=AF.Square,
                             accum_out=ssc[:, i:i + 1]).then_inc(su.h, 1)
        su.v = NT

        @blk.vector
        def _(v):
            v.memset(eps_col[:], EPS)
            for i in range(NT):
                v.wait_ge(ld.h, 16 * (i + 2))
                v.tensor_reduce(out=smc[:, i:i + 1], in_=sx[i % 2][:], op=OP.add, axis=AX.X).then_inc(sv.h, 1)
        sv.v = NT

    with nc.Block() as blk:
        @blk.vector
        def _(v):
            v.tensor_copy(out=s_idenb[:], in_=s_iden[:])
            v.memset(eps2_col[:], EPS * EPS)
            v.tensor_scalar_mul(out=mean[:], in0=smc[:], scalar1=1.0 / D)
            v.tensor_scalar_mul(out=tmp[0][:], in0=ssc[:], scalar1=1.0 / D)
            v.tensor_mul(out=tmp[1][:], in0=mean[:], in1=mean[:])
            v.tensor_sub(out=tmp[2][:], in0=tmp[0][:], in1=tmp[1][:]).then_inc(c1.h, 1)
        c1.v = 1

        @blk.scalar
        def _(s):
            s.wait_ge(c1.h, 1)
            s.activation(out=tmp[0][:], in_=tmp[2][:], func=AF.Ln, bias=eps_col[:]).then_inc(c2.h, 1)
            s.activation(out=tmp[1][:], in_=tmp[2][:], func=AF.Ln, bias=eps2_col[:], scale=1.0 + EPS).then_inc(c2.h, 1)
            s.wait_ge(c2.h, 2)
            s.activation(out=rstd[:], in_=tmp[0][:], func=AF.Exp, scale=-0.5)
            s.activation(out=cc[:], in_=tmp[1][:], func=AF.Exp, scale=-0.5).then_inc(c2.h, 1)
        c2.v = 3

    def emit_stores(q, i):
        q.wait_ge(ev.h, (i + 1) * G)
        q.dma_start(out=xn_fm32.rearrange("(g p) t -> p g t", p=128)[:, :, 128 * i:128 * (i + 1)],
                    in_=tf32[i % 2].rearrange("p (g c) -> p g c", g=G)).then_inc(st.h, 16)
        q.dma_start(out=xn_fmb.rearrange("(g p) t -> p g t", p=128)[:, :, 128 * i:128 * (i + 1)],
                    in_=tfb[i % 2].rearrange("p (g c) -> p g c", g=G)).then_inc(st.h, 16)
        if with_xhat:
            q.wait_ge(evh.h, (i + 1) * G)
            q.dma_start(out=xh_fmb.rearrange("(g p) t -> p g t", p=128)[:, :, 128 * i:128 * (i + 1)],
                        in_=thb[i % 2].rearrange("p (g c) -> p g c", g=G)).then_inc(st.h, 16)

    with nc.Block() as blk:
        @blk.sync
        def _(q):
            for i in range(NT):
                if i >= 2:
                    q.wait_ge(nrm.h, i - 1)
                q.dma_start(out=sx[i % 2][:], in_=x[128 * i:128 * (i + 1), :]).then_inc(ld2.h, 16)
                if i >= 2:
                    emit_stores(q, i - 2)
            emit_stores(q, NT - 2)
            emit_stores(q, NT - 1)

        @blk.vector
        def _(v):
            for i in range(NT):
                v.wait_ge(ld2.h, 16 * (i + 1))
                if i >= 2:
                    v.wait_ge(stok.h, 16 * (2 if with_xhat else 1) * (i - 1))
                    v.wait_ge(tp.h, G * (i - 1))
                    if with_xhat:
                        v.wait_ge(tph.h, G * (i - 1))
                if with_xhat:
                    v.tensor_scalar(out=sxn[i % 2][:], in0=sx[i % 2][:], scalar1=mean[:, i:i + 1],
                                    scalar2=rstd[:, i:i + 1], op0=OP.subtract, op1=OP.mult)
                    v.tensor_scalar(out=sxh[i % 2][:], in0=sx[i % 2][:], scalar1=mean[:, i:i + 1],
                                    scalar2=cc[:, i:i + 1], op0=OP.subtract, op1=OP.mult).then_inc(nrm.h, 1)
                else:
                    v.tensor_scalar(out=sxn[i % 2][:], in0=sx[i % 2][:], scalar1=mean[:, i:i + 1],
                                    scalar2=rstd[:, i:i + 1], op0=OP.subtract, op1=OP.mult).then_inc(nrm.h, 1)
                for g in range(G):
                    k = i * G + g
                    v.wait_ge(tp.h, k + 1)
                    v.tensor_copy(out=tf32[i % 2][:, 128 * g:128 * (g + 1)], in_=pt[k % 2][:])
                    v.tensor_copy(out=tfb[i % 2][:, 128 * g:128 * (g + 1)], in_=pt[k % 2][:]).then_inc(ev.h, 1)
        nrm.v = NT
        ev.v = NT * G

        @blk.tensor
        def _(t):
            for i in range(NT):
                t.wait_ge(nrm.h, i + 1)
                if i >= 2:
                    t.wait_ge(st.h, 16 * (3 if with_xhat else 2) * (i - 1))
                for g in range(G):
                    k = i * G + g
                    if k >= 2:
                        t.wait_ge(ev.h, k - 1)
                    t.transpose(out=pt[k % 2][:], in_=sxn[i % 2][:, 128 * g:128 * (g + 1)],
                                identity=s_iden[:]).then_inc(tp.h, 1)
                if with_xhat:
                    for g in range(G):
                        k = i * G + g
                        if k >= 2:
                            t.wait_ge(evh.h, k - 1)
                        t.transpose(out=ptb[k % 2][:], in_=sxh[i % 2][:, 128 * g:128 * (g + 1)],
                                    identity=s_idenb[:]).then_inc(tph.h, 1)
        tp.v = NT * G
        tph.v = NT * G if with_xhat else 0

        @blk.scalar
        def _(s):
            if with_xhat:
                for i in range(NT):
                    if i >= 2:
                        s.wait_ge(st.h, 48 * (i - 1))
                    for g in range(G):
                        k = i * G + g
                        s.wait_ge(tph.h, k + 1)
                        s.copy(out=thb[i % 2][:, 128 * g:128 * (g + 1)], in_=ptb[k % 2][:]).then_inc(evh.h, 1)
        evh.v = NT * G if with_xhat else 0

        @blk.gpsimd
        def _(gp):
            for i in range(NT):
                gp.wait_ge(nrm.h, i + 1)
                gp.dma_start(out=xn_tok[128 * i:128 * (i + 1), :], in_=sxn[i % 2][:]).then_inc(stok.h, 16)
                if with_xhat:
                    gp.dma_start(out=xh_tok[128 * i:128 * (i + 1), :], in_=sxh[i % 2][:]).then_inc(stok.h, 16)
        stok.v = 16 * NT * (2 if with_xhat else 1)

        @blk.sync
        def _(q):
            q.wait_ge(st.h, 16 * NT * (3 if with_xhat else 2))
            q.wait_ge(stok.h, stok.v)
        st.v = 16 * NT * (3 if with_xhat else 2)

    return nc


ENGS = ["sync", "gpsimd", "vector", "scalar", "tensor"]


class Step:
    __slots__ = ("eng", "seq", "dma")
    def __init__(self, eng, dma):
        self.eng, self.seq, self.dma = eng, None, dma


class Prog:
    def __init__(self, nc):
        self.nc = nc
        self.sem = {}
        for e in ENGS:
            self.sem[(e, 0)] = nc.alloc_semaphore(f"P_{e}")
            self.sem[(e, 1)] = nc.alloc_semaphore(f"P_{e}_d")
        self.items = []

    def step(self, eng, deps, emit):
        s = Step(eng, False)
        self.items.append((s, list(deps), emit))
        return s

    def dma(self, eng, deps, emit):
        s = Step(eng, True)
        self.items.append((s, list(deps), emit))
        return s

    def build(self):
        nc = self.nc
        seqs = {k: 0 for k in self.sem}
        for s, deps, emit in self.items:
            k = (s.eng, 1 if s.dma else 0)
            seqs[k] += 1
            s.seq = seqs[k]
        per_eng = {e: [] for e in ENGS}
        for it in self.items:
            per_eng[it[0].eng].append(it)

        with nc.Block() as blk0:
            @blk0.sync
            def _(q):
                for h in self.sem.values():
                    q.sem_clear(h)

        with nc.Block() as blk:
            def run_engine(e, h):
                for s, deps, emit in per_eng[e]:
                    need = {}
                    for d in deps:
                        key = (d.eng, 1 if d.dma else 0)
                        v = d.seq * (16 if d.dma else 1)
                        if need.get(key, 0) < v:
                            need[key] = v
                    for key, v in need.items():
                        h.wait_ge(self.sem[key], v)
                    last = emit(h)
                    assert last is not None, f"step on {e} returned None"
                    last.then_inc(self.sem[(s.eng, 1 if s.dma else 0)], 16 if s.dma else 1)

            for e in ENGS:
                if per_eng[e]:
                    getattr(blk, e)(lambda h, e=e: run_engine(e, h))
        return nc


def build_attn0():
    nc = bass.Bass("TRN2", target_bir_lowering=False, detect_race_conditions=False)
    xn_fmb = nc.dram_tensor("xn_fmb", [D, T], BF16, kind="ExternalInput").ap()
    wts = {nm: nc.dram_tensor(nm, [D, D], F32, kind="ExternalInput").ap()
           for nm in ["WpT", "WqT", "WkT", "WvT", "WoT"]}
    bcols = {nm: nc.dram_tensor(nm, [128, G], F32, kind="ExternalInput").ap()
             for nm in ["bpc", "bqc", "bkc", "bvc"]}
    bo_row = nc.dram_tensor("bo_row", [1, D], F32, kind="ExternalInput").ap()
    wsel = nc.dram_tensor("wsel", [G, 128, 16], F32, kind="ExternalInput").ap()
    wbc = nc.dram_tensor("wbc", [G, 16, 128], F32, kind="ExternalInput").ap()
    ones1 = nc.dram_tensor("ones1", [1, 128], F32, kind="ExternalInput").ap()
    s_tok = nc.dram_tensor("s_tok", [T, D], F32, kind="ExternalOutput").ap()
    xp_d = nc.dram_tensor("xp_d", [D, T], BF16)

    W = {nm: nc.alloc_sbuf_tensor("s_" + nm, [128, G * D], BF16) for nm in wts}
    B = {nm: nc.alloc_sbuf_tensor("s_" + nm, [128, G], F32) for nm in bcols}
    s_bo = nc.alloc_sbuf_tensor("s_bo", [1, D], BF16)
    s_ones = nc.alloc_sbuf_tensor("s_ones", [1, 128], BF16)
    s_wsel = nc.alloc_sbuf_tensor("s_wsel", [128, G * 16], BF16)
    s_wbc = nc.alloc_sbuf_tensor("s_wbc", [16, G * 128], BF16)
    xn_sb = [nc.alloc_sbuf_tensor(f"xn_sb{i}", [128, G * 512], BF16) for i in range(2)]
    xp_sb = [nc.alloc_sbuf_tensor(f"xp_sb{i}", [128, G * 512], BF16) for i in range(2)]
    w_sb = [nc.alloc_sbuf_tensor(f"w_sb{i}", [128, G * 514], BF16) for i in range(2)]
    q_sb = nc.alloc_sbuf_tensor("q_sb", [128, G * 512], BF16)
    k_sb = nc.alloc_sbuf_tensor("k_sb", [128, G * 514], BF16)
    v_sb = nc.alloc_sbuf_tensor("v_sb", [128, G * 514], BF16)
    qk_sb = nc.alloc_sbuf_tensor("qk_sb", [128, 3 * 512], BF16)
    sm_sb = nc.alloc_sbuf_tensor("sm_sb", [16, 512], F32)
    e_sb = nc.alloc_sbuf_tensor("e_sb", [16, 3 * 512], F32)
    ex_sb = nc.alloc_sbuf_tensor("ex_sb", [16, 3 * 512], F32)
    den_sb = nc.alloc_sbuf_tensor("den_sb", [16, 512], F32)
    lden_sb = nc.alloc_sbuf_tensor("lden_sb", [16, 512], F32)
    a_sb = nc.alloc_sbuf_tensor("a_sb", [16, 3 * 512], BF16)
    ab_sb = nc.alloc_sbuf_tensor("ab_sb", [128, 3 * 512], BF16)
    av_sb = nc.alloc_sbuf_tensor("av_sb", [128, G * 512], BF16)
    so_sb = [nc.alloc_sbuf_tensor(f"so_sb{i}", [128, D], F32) for i in range(2)]
    P = [nc.alloc_psum_tensor(f"P{i}", [128, 512], F32) for i in range(8)]

    p = Prog(nc)
    NONE = []

    wl = []
    for nm in ["WpT", "WqT", "WkT", "WvT", "WoT"]:
        for ig in range(G):
            wl.append(p.dma("gpsimd", NONE, lambda g, nm=nm, ig=ig: g.dma_start(
                out=W[nm][:, ig * D:(ig + 1) * D], in_=wts[nm][128 * ig:128 * (ig + 1), :])))
    for nm in ["bpc", "bqc", "bkc", "bvc"]:
        wl.append(p.dma("gpsimd", NONE, lambda g, nm=nm: g.dma_start(out=B[nm][:], in_=bcols[nm])))
    wl.append(p.dma("gpsimd", NONE, lambda g: g.dma_start(out=s_bo[:], in_=bo_row)))
    wl.append(p.dma("gpsimd", NONE, lambda g: g.dma_start(out=s_ones[:], in_=ones1)))
    wl.append(p.dma("gpsimd", NONE, lambda g: g.dma_start(
        out=s_wsel.rearrange("p (g c) -> p g c", g=G), in_=wsel.rearrange("g p c -> p g c"))))
    wl.append(p.dma("gpsimd", NONE, lambda g: g.dma_start(
        out=s_wbc.rearrange("p (g c) -> p g c", g=G), in_=wbc.rearrange("g p c -> p g c"))))
    wp_done = wl[7]
    w_all = wl[-1]

    xp_store = []
    ev_prev = {}
    xn_ld = []
    for tc in range(NCH):
        deps = [xp_store[tc - 2]] if tc >= 2 else []
        if tc >= 2:
            deps.append(ev_prev[("xn", tc - 2)])
        xn_ld.append(p.dma("sync", deps, lambda q, tc=tc: q.dma_start(
            out=xn_sb[tc % 2].rearrange("p (g c) -> p g c", g=G),
            in_=xn_fmb.rearrange("(g p) t -> p g t", p=128)[:, :, 512 * tc:512 * (tc + 1)])))
        evs = []
        for og in range(G):
            bank = og % 2
            deps = [xn_ld[tc], wp_done]
            if (tc, og) > (0, 1):
                deps.append(ev_prev[("p1", bank)])
            if og == 0 and tc >= 2:
                deps.append(xp_store[tc - 2])
            def mm(t, tc=tc, og=og, bank=bank):
                for ig in range(G):
                    last = t.matmul(out=P[bank][:],
                                    lhsT=W["WpT"][:, ig * D + 128 * og:ig * D + 128 * (og + 1)],
                                    rhs=xn_sb[tc % 2][:, ig * 512:(ig + 1) * 512],
                                    start=(ig == 0), stop=(ig == G - 1))
                return last
            ms = p.step("tensor", deps, mm)
            ev = p.step("vector", [ms, w_all], lambda v, tc=tc, og=og, bank=bank: v.tensor_scalar_add(
                out=xp_sb[tc % 2][:, og * 512:(og + 1) * 512], in0=P[bank][:],
                scalar1=B["bpc"][:, og:og + 1]))
            ev_prev[("p1", bank)] = ev
            evs.append(ev)
        ev_prev[("xn", tc)] = ms
        xp_store.append(p.dma("sync", [evs[-1]], lambda q, tc=tc: q.dma_start(
            out=xp_d.rearrange("(g p) t -> p g t", p=128)[:, :, 512 * tc:512 * (tc + 1)],
            in_=xp_sb[tc % 2].rearrange("p (g c) -> p g c", g=G))))
    xp_all = xp_store[-1]

    last = {}
    st_steps = []
    for tc in range(NCH):
        wnd = []
        deps = [xp_all]
        if tc >= 2:
            deps.append(last[("qkv_mm", tc - 2)])
        lo = 512 * tc - 1
        for g in range(G):
            if tc == 0:
                p.dma("sync", deps, lambda q, g=g: q.dma_start(
                    out=w_sb[0][:, g * 514:g * 514 + 1], in_=xp_d[128 * g:128 * (g + 1), T - 1:T], allow_slow_non_contiguous=True))
                wnd.append(p.dma("sync", deps, lambda q, g=g: q.dma_start(
                    out=w_sb[0][:, g * 514 + 1:(g + 1) * 514], in_=xp_d[128 * g:128 * (g + 1), 0:513])))
            elif tc == NCH - 1:
                p.dma("sync", deps, lambda q, g=g, lo=lo: q.dma_start(
                    out=w_sb[tc % 2][:, g * 514:(g + 1) * 514 - 1], in_=xp_d[128 * g:128 * (g + 1), lo:T]))
                wnd.append(p.dma("sync", deps, lambda q, g=g: q.dma_start(
                    out=w_sb[1][:, (g + 1) * 514 - 1:(g + 1) * 514], in_=xp_d[128 * g:128 * (g + 1), 0:1], allow_slow_non_contiguous=True)))
            else:
                wnd.append(p.dma("sync", deps, lambda q, g=g, lo=lo, tc=tc: q.dma_start(
                    out=w_sb[tc % 2][:, g * 514:(g + 1) * 514], in_=xp_d[128 * g:128 * (g + 1), lo:lo + 514])))
        wnd_done = wnd[-1]

        qkv_ev = []
        for og in range(G):
            deps = [wnd_done, w_all]
            if (tc, og) != (0, 0):
                deps.append(last[("qkv_ev", None)])
            else:
                deps.append(ev_prev[("p1", 0)])
                deps.append(ev_prev[("p1", 1)])
            if tc >= 1:
                deps.append(last[("abev_tc", tc - 1)])
                deps.append(last[("oev", tc - 1)])
            def qkvmm(t, tc=tc, og=og):
                for ig in range(G):
                    t.matmul(out=P[0][:], lhsT=W["WqT"][:, ig * D + 128 * og:ig * D + 128 * (og + 1)],
                             rhs=w_sb[tc % 2][:, ig * 514 + 1:ig * 514 + 513],
                             start=(ig == 0), stop=(ig == G - 1))
                for ig in range(G):
                    t.matmul(out=P[1][:], lhsT=W["WkT"][:, ig * D + 128 * og:ig * D + 128 * (og + 1)],
                             rhs=w_sb[tc % 2][:, ig * 514:ig * 514 + 512],
                             start=(ig == 0), stop=(ig == G - 1))
                for ig in range(G):
                    t.matmul(out=P[2][:, 0:2], lhsT=W["WkT"][:, ig * D + 128 * og:ig * D + 128 * (og + 1)],
                             rhs=w_sb[tc % 2][:, ig * 514 + 512:ig * 514 + 514],
                             start=(ig == 0), stop=(ig == G - 1))
                for ig in range(G):
                    t.matmul(out=P[3][:], lhsT=W["WvT"][:, ig * D + 128 * og:ig * D + 128 * (og + 1)],
                             rhs=w_sb[tc % 2][:, ig * 514:ig * 514 + 512],
                             start=(ig == 0), stop=(ig == G - 1))
                for ig in range(G):
                    last_i = t.matmul(out=P[4][:, 0:2], lhsT=W["WvT"][:, ig * D + 128 * og:ig * D + 128 * (og + 1)],
                                      rhs=w_sb[tc % 2][:, ig * 514 + 512:ig * 514 + 514],
                                      start=(ig == 0), stop=(ig == G - 1))
                return last_i
            qm = p.step("tensor", deps, qkvmm)
            last[("qkv_mm", tc)] = qm

            def qkvev(v, tc=tc, og=og):
                v.tensor_scalar(out=q_sb[:, og * 512:(og + 1) * 512], in0=P[0][:],
                                scalar1=B["bqc"][:, og:og + 1], scalar2=0.125,
                                op0=OP.add, op1=OP.mult)
                v.tensor_scalar_add(out=k_sb[:, og * 514:og * 514 + 512], in0=P[1][:],
                                    scalar1=B["bkc"][:, og:og + 1])
                v.tensor_scalar_add(out=k_sb[:, og * 514 + 512:og * 514 + 514], in0=P[2][:, 0:2],
                                    scalar1=B["bkc"][:, og:og + 1])
                v.tensor_scalar_add(out=v_sb[:, og * 514:og * 514 + 512], in0=P[3][:],
                                    scalar1=B["bvc"][:, og:og + 1])
                return v.tensor_scalar_add(out=v_sb[:, og * 514 + 512:og * 514 + 514], in0=P[4][:, 0:2],
                                           scalar1=B["bvc"][:, og:og + 1])
            qe = p.step("vector", [qm], qkvev)
            last[("qkv_ev", None)] = qe
            qkv_ev.append(qe)

        sc_mm = None
        for g in range(G):
            deps = [qkv_ev[g]]
            if g >= 1:
                deps.append(sc_mm)
            elif tc >= 1:
                deps.append(last[("smax", tc - 1)])
            def qkprod(v, tc=tc, g=g):
                for sh in range(3):
                    li = v.tensor_mul(out=qk_sb[:, sh * 512:(sh + 1) * 512],
                                      in0=q_sb[:, g * 512:(g + 1) * 512],
                                      in1=k_sb[:, g * 514 + sh:g * 514 + sh + 512])
                return li
            qkp = p.step("vector", deps, qkprod)

            def scmm(t, tc=tc, g=g):
                for sh in range(3):
                    li = t.matmul(out=P[5 + sh][0:16, :],
                                  lhsT=s_wsel[:, g * 16:(g + 1) * 16],
                                  rhs=qk_sb[:, sh * 512:(sh + 1) * 512],
                                  start=(g == 0), stop=(g == G - 1))
                return li
            sc_mm = p.step("tensor", [qkp], scmm)

        def smax1(v, tc=tc):
            v.tensor_copy(out=sm_sb[:], in_=P[5][0:16, :])
            v.tensor_max(out=sm_sb[:], in0=sm_sb[:], in1=P[6][0:16, :])
            v.tensor_max(out=sm_sb[:], in0=sm_sb[:], in1=P[7][0:16, :])
            for sh in range(3):
                li = v.tensor_sub(out=e_sb[:, sh * 512:(sh + 1) * 512],
                                  in0=P[5 + sh][0:16, :], in1=sm_sb[:])
            return li
        sm1 = p.step("vector", [sc_mm], smax1)
        last[("smax", tc)] = sm1

        def smax2(s, tc=tc):
            for sh in range(3):
                li = s.activation(out=ex_sb[:, sh * 512:(sh + 1) * 512],
                                  in_=e_sb[:, sh * 512:(sh + 1) * 512], func=AF.Exp)
            return li
        sm2 = p.step("scalar", [sm1], smax2)

        def smax3(v, tc=tc):
            v.tensor_add(out=den_sb[:], in0=ex_sb[:, 0:512], in1=ex_sb[:, 512:1024])
            return v.tensor_add(out=den_sb[:], in0=den_sb[:], in1=ex_sb[:, 1024:1536])
        sm3 = p.step("vector", [sm2], smax3)

        def smax4(s, tc=tc):
            return s.activation(out=lden_sb[:], in_=den_sb[:], func=AF.Ln)
        sm4 = p.step("scalar", [sm3], smax4)

        def smax5(v, tc=tc):
            for sh in range(3):
                li = v.tensor_sub(out=e_sb[:, sh * 512:(sh + 1) * 512],
                                  in0=e_sb[:, sh * 512:(sh + 1) * 512], in1=lden_sb[:])
            return li
        sm5 = p.step("vector", [sm4], smax5)

        def smax6(s, tc=tc):
            for sh in range(3):
                li = s.activation(out=a_sb[:, sh * 512:(sh + 1) * 512],
                                  in_=e_sb[:, sh * 512:(sh + 1) * 512], func=AF.Exp)
            return li
        sm6 = p.step("scalar", [sm5], smax6)
        last[("asb", tc)] = sm6

        av_st = None
        for g in range(G):
            deps = [sm6]
            if g >= 1:
                deps.append(last[("abev", None)])
            elif tc >= 1:
                deps.append(last[("av", tc - 1)])
            def bcmm(t, tc=tc, g=g):
                for sh in range(3):
                    li = t.matmul(out=P[sh][:], lhsT=s_wbc[:, g * 128:(g + 1) * 128],
                                  rhs=a_sb[:, sh * 512:(sh + 1) * 512], start=True, stop=True)
                return li
            bq = p.step("tensor", deps, bcmm)
            last[("bcast_ev", tc)] = bq

            def abev(s, tc=tc, g=g):
                for sh in range(3):
                    li = s.copy(out=ab_sb[:, sh * 512:(sh + 1) * 512], in_=P[sh][:])
                return li
            ae = p.step("scalar", [bq], abev)
            last[("abev", None)] = ae
            last[("abev_tc", tc)] = ae

            def avc(v, tc=tc, g=g):
                v.tensor_mul(out=av_sb[:, g * 512:(g + 1) * 512], in0=ab_sb[:, 0:512],
                             in1=v_sb[:, g * 514:g * 514 + 512])
                v.tensor_mul(out=qk_sb[:, 0:512], in0=ab_sb[:, 512:1024],
                             in1=v_sb[:, g * 514 + 1:g * 514 + 513])
                v.tensor_add(out=av_sb[:, g * 512:(g + 1) * 512], in0=av_sb[:, g * 512:(g + 1) * 512],
                             in1=qk_sb[:, 0:512])
                v.tensor_mul(out=qk_sb[:, 0:512], in0=ab_sb[:, 1024:1536],
                             in1=v_sb[:, g * 514 + 2:g * 514 + 514])
                return v.tensor_add(out=av_sb[:, g * 512:(g + 1) * 512],
                                    in0=av_sb[:, g * 512:(g + 1) * 512], in1=qk_sb[:, 0:512])
            av = p.step("vector", [ae], avc)
            last[("av", tc)] = av
        av_all = last[("av", tc)]

        oev_step = None
        for tt in range(4):
            deps = [av_all]
            if tt >= 1:
                deps.append(last[("oev", tc)])
            elif tc >= 1:
                deps.append(last[("oev", tc - 1)])
            def womm(t, tc=tc, tt=tt):
                for od in range(2):
                    ps = P[3 + od]
                    for ig in range(G):
                        t.matmul(out=ps[:], lhsT=av_sb[:, ig * 512 + 128 * tt:ig * 512 + 128 * (tt + 1)],
                                 rhs=W["WoT"][:, ig * D + 512 * od:ig * D + 512 * (od + 1)],
                                 start=(ig == 0), stop=False)
                    li = t.matmul(out=ps[:], lhsT=s_ones[:], rhs=s_bo[:, 512 * od:512 * (od + 1)],
                                  start=False, stop=True)
                return li
            wo = p.step("tensor", deps, womm)

            deps = [wo]
            if tc * 4 + tt >= 2:
                deps.append(st_steps[-2])
            def oev(v, tc=tc, tt=tt):
                v.tensor_copy(out=so_sb[tt % 2][:, 0:512], in_=P[3][:])
                return v.tensor_copy(out=so_sb[tt % 2][:, 512:1024], in_=P[4][:])
            oe = p.step("vector", deps, oev)
            last[("oev", tc)] = oe

            st_steps.append(p.dma("sync", [oe], lambda q, tc=tc, tt=tt: q.dma_start(
                out=s_tok[512 * tc + 128 * tt:512 * tc + 128 * (tt + 1), :], in_=so_sb[tt % 2][:])))

    p.step("sync", [st_steps[-1]], lambda q: q.nop())
    return p.build()


F_HALF = 2048
NFG = 16


def build_ffn0e():
    nc = bass.Bass("TRN2", target_bir_lowering=False, detect_race_conditions=False)
    xh_fmb = nc.dram_tensor("xh_fmb", [D, T], BF16, kind="ExternalInput").ap()
    xn_fm32 = nc.dram_tensor("xn_fm32", [D, T], F32, kind="ExternalInput").ap()
    cenT = nc.dram_tensor("cenT", [D, G], F32, kind="ExternalInput").ap()
    w1Te = nc.dram_tensor("w1Te", [D, 4096], F32, kind="ExternalInput").ap()
    w2Te = nc.dram_tensor("w2Te", [4096, D], F32, kind="ExternalInput").ap()
    lg = nc.dram_tensor("lg", [128, G], F32, kind="ExternalInput").ap()
    lb = nc.dram_tensor("lb", [128, G], F32, kind="ExternalInput").ap()
    b1c = nc.dram_tensor("b1c", [128, 32], F32, kind="ExternalInput").ap()
    b2_row = nc.dram_tensor("b2_row", [1, D], F32, kind="ExternalInput").ap()
    ones1 = nc.dram_tensor("ones1", [1, 128], F32, kind="ExternalInput").ap()
    econst = nc.dram_tensor("econst", [128, 1], F32, kind="ExternalInput").ap()
    acc_in = nc.dram_tensor("acc_in", [T, D], F32, kind="ExternalInput").ap()
    acc_out = nc.dram_tensor("acc_out", [T, D], F32, kind="ExternalOutput").ap()
    acc_mid = nc.dram_tensor("acc_mid", [T, D], F32)

    s_cen = nc.alloc_sbuf_tensor("s_cen", [128, G * G], F32)
    s_lg = nc.alloc_sbuf_tensor("s_lg", [128, G], F32)
    s_lb = nc.alloc_sbuf_tensor("s_lb", [128, G], F32)
    s_b1 = nc.alloc_sbuf_tensor("s_b1", [128, 32], F32)
    s_b2 = nc.alloc_sbuf_tensor("s_b2", [1, D], BF16)
    s_ones = nc.alloc_sbuf_tensor("s_ones", [1, 128], BF16)
    s_e = nc.alloc_sbuf_tensor("s_e", [128, 1], F32)
    xn32 = [nc.alloc_sbuf_tensor(f"xn32_{i}", [128, D], F32) for i in range(2)]
    mx_sb = nc.alloc_sbuf_tensor("mx_sb", [128, 8], F32)
    mi_sb = nc.alloc_sbuf_tensor("mi_sb", [128, 8], U32)
    a_col = nc.alloc_sbuf_tensor("a_col", [128, 1], F32)
    am_all = nc.alloc_sbuf_tensor("am_all", [128, NT], F32)
    w1_sb = nc.alloc_sbuf_tensor("w1_sb", [128, G * F_HALF], BF16)
    w2_sb = nc.alloc_sbuf_tensor("w2_sb", [128, NFG * D], BF16)
    xh_sb = [nc.alloc_sbuf_tensor(f"xh_sb{i}", [128, G * 512], BF16) for i in range(2)]
    xe_sb = nc.alloc_sbuf_tensor("xe_sb", [128, G * 512], BF16)
    h_sb = nc.alloc_sbuf_tensor("h_sb", [128, NFG * 512], BF16)
    ac_sb = [nc.alloc_sbuf_tensor(f"ac_sb{i}", [128, D], F32) for i in range(2)]
    P = [nc.alloc_psum_tensor(f"P{i}", [128, 512], F32) for i in range(6)]
    Pq = nc.alloc_psum_tensor("Pq", [128, 8], F32)

    p = Prog(nc)
    NONE = []

    wl = []
    wl.append(p.dma("gpsimd", NONE, lambda g: g.dma_start(
        out=s_cen.rearrange("p (i c) -> p i c", i=G),
        in_=cenT.rearrange("(i p) c -> p i c", p=128))))
    for nm, sb, dr in [("lg", s_lg, lg), ("lb", s_lb, lb), ("b1", s_b1, b1c),
                       ("b2", s_b2, b2_row), ("ones", s_ones, ones1), ("e", s_e, econst)]:
        wl.append(p.dma("gpsimd", NONE, lambda g, sb=sb, dr=dr: g.dma_start(out=sb[:], in_=dr)))
    const_done = wl[-1]

    mprev = None
    for i in range(NT):
        deps = [wl[0]]
        if i >= 2:
            deps.append(last_aff)
        ld = p.dma("sync", deps if i >= 2 else [wl[0]], lambda q, i=i: q.dma_start(
            out=xn32[i % 2].rearrange("p (g c) -> p g c", g=G),
            in_=xn_fm32.rearrange("(g p) t -> p g t", p=128)[:, :, 128 * i:128 * (i + 1)]))
        def affmm(t, i=i):
            for ig in range(G):
                li = t.matmul(out=Pq[:], lhsT=xn32[i % 2][:, ig * 128:(ig + 1) * 128],
                              rhs=s_cen[:, ig * G:(ig + 1) * G], start=(ig == 0), stop=(ig == G - 1))
            return li
        deps = [ld]
        if mprev is not None:
            deps.append(mprev)
        aff = p.step("tensor", deps, affmm)
        last_aff = aff
        mx = p.step("vector", [aff], lambda v, i=i: v.max(out=mx_sb[:], in_=Pq[:]))
        mi = p.step("vector", [mx], lambda v, i=i: v.max_index(out=mi_sb[:], in_max=mx_sb[:], in_=None)
                    if False else v.max_index(out=mi_sb[:], in_max=mx_sb[:], in_values=Pq[:]))
        mprev = mi
        sg = p.step("scalar", [mx], lambda s, i=i: s.activation(out=a_col[:], in_=mx_sb[:, 0:1],
                                                                func=AF.Sigmoid))
        am = p.step("vector", [mi, sg, const_done], lambda v, i=i: v.tensor_scalar(
            out=am_all[:, i:i + 1], in0=mi_sb[:, 0:1], scalar1=s_e[:], scalar2=a_col[:],
            op0=OP.is_equal, op1=OP.mult))
        mprev = am
    am_done = am

    last = {}
    acc_st = []
    for fh in range(2):
        w1l = []
        for ig in range(G):
            w1l.append(p.dma("gpsimd", [last.get(("hmm", None))] if fh else NONE,
                             lambda g, ig=ig, fh=fh: g.dma_start(
                out=w1_sb[:, ig * F_HALF:(ig + 1) * F_HALF],
                in_=w1Te[128 * ig:128 * (ig + 1), F_HALF * fh:F_HALF * (fh + 1)])))
        w2l = []
        for fk in range(NFG):
            w2l.append(p.dma("gpsimd", [last.get(("zmm", None))] if fh else NONE,
                             lambda g, fk=fk, fh=fh: g.dma_start(
                out=w2_sb[:, fk * D:(fk + 1) * D],
                in_=w2Te[F_HALF * fh + 128 * fk:F_HALF * fh + 128 * (fk + 1), :])))
        w1_done, w2_done = w1l[-1], w2l[-1]

        for tc in range(NCH):
            k = fh * NCH + tc
            deps = [const_done]
            if k >= 2:
                deps.append(last[("xe", k - 2)])
            xhl = p.dma("sync", deps, lambda q, tc=tc, k=k: q.dma_start(
                out=xh_sb[k % 2].rearrange("p (g c) -> p g c", g=G),
                in_=xh_fmb.rearrange("(g p) t -> p g t", p=128)[:, :, 512 * tc:512 * (tc + 1)]))

            deps = [xhl, const_done]
            if k >= 1:
                deps.append(last[("hmm", None)])
            def xec(v, k=k):
                for ig in range(G):
                    li = v.tensor_scalar(out=xe_sb[:, ig * 512:(ig + 1) * 512],
                                         in0=xh_sb[k % 2][:, ig * 512:(ig + 1) * 512],
                                         scalar1=s_lg[:, ig:ig + 1], scalar2=s_lb[:, ig:ig + 1],
                                         op0=OP.mult, op1=OP.add)
                return li
            xe = p.step("vector", deps, xec)
            last[("xe", k)] = xe

            hevs = []
            for fg in range(NFG):
                bank = fg % 2
                deps = [xe, w1_done]
                if fg >= 2 or k > 0:
                    deps.append(last.get(("hev", bank), xe))
                if k > 0 and fg < 2:
                    deps.append(last[("zmm", None)])
                def hmm(t, fg=fg, k=k, bank=bank, fh=fh):
                    for ig in range(G):
                        li = t.matmul(out=P[bank][:],
                                      lhsT=w1_sb[:, ig * F_HALF + 128 * fg:ig * F_HALF + 128 * (fg + 1)],
                                      rhs=xe_sb[:, ig * 512:(ig + 1) * 512],
                                      start=(ig == 0), stop=(ig == G - 1))
                    return li
                hm = p.step("tensor", deps, hmm)
                last[("hmm", None)] = hm
                he = p.step("scalar", [hm], lambda s, fg=fg, bank=bank, fh=fh: s.activation(
                    out=h_sb[:, fg * 512:(fg + 1) * 512], in_=P[bank][:], func=AF.Relu,
                    bias=s_b1[:, NFG * fh + fg:NFG * fh + fg + 1]))
                last[("hev", bank)] = he
                hevs.append(he)
            h_done = hevs[-1]

            for tt in range(4):
                i = tc * 4 + tt
                deps = [am_done]
                if i >= 2 or fh > 0:
                    deps.append(acc_st[-2] if len(acc_st) >= 2 else acc_st[0])
                acl = p.dma("sync", deps, lambda q, i=i, fh=fh: q.dma_start(
                    out=ac_sb[i % 2][:],
                    in_=(acc_in if fh == 0 else acc_mid)[128 * i:128 * (i + 1), :]))
                deps = [h_done, w2_done]
                if (fh, tc, tt) != (0, 0, 0):
                    deps.append(last[("zev", None)])
                def zmm(t, tt=tt, fh=fh):
                    for od in range(2):
                        for fk in range(NFG):
                            li = t.matmul(out=P[2 + od][:],
                                          lhsT=h_sb[:, fk * 512 + 128 * tt:fk * 512 + 128 * (tt + 1)],
                                          rhs=w2_sb[:, fk * D + 512 * od:fk * D + 512 * (od + 1)],
                                          start=(fk == 0), stop=(fk == NFG - 1 and fh == 1))
                        if fh == 0:
                            li = t.matmul(out=P[2 + od][:], lhsT=s_ones[:],
                                          rhs=s_b2[:, 512 * od:512 * (od + 1)],
                                          start=False, stop=True)
                    return li
                zm = p.step("tensor", deps, zmm)
                last[("zmm", None)] = zm

                def zev(v, i=i, tt=tt):
                    for od in range(2):
                        li = v.scalar_tensor_tensor(out=ac_sb[i % 2][:, 512 * od:512 * (od + 1)],
                                                    in0=P[2 + od][:], scalar=am_all[:, i:i + 1],
                                                    op0=OP.mult, op1=OP.add,
                                                    in1=ac_sb[i % 2][:, 512 * od:512 * (od + 1)])
                    return li
                ze = p.step("vector", [zm, acl], zev)
                last[("zev", None)] = ze
                acc_st.append(p.dma("sync", [ze], lambda q, i=i, fh=fh: q.dma_start(
                    out=(acc_mid if fh == 0 else acc_out)[128 * i:128 * (i + 1), :],
                    in_=ac_sb[i % 2][:])))

    p.step("sync", [acc_st[-1]], lambda q: q.nop())
    return p.build()


def build_combine(kind):
    nc = bass.Bass("TRN2", target_bir_lowering=False, detect_race_conditions=False)
    if kind == "ffn0fin":
        ins = {nm: nc.dram_tensor(nm, [T, D], F32, kind="ExternalInput").ap()
               for nm in ["acc", "xn_tok", "s_tok", "x"]}
    else:
        ins = {nm: nc.dram_tensor(nm, [T, D], F32, kind="ExternalInput").ap()
               for nm in ["g1_tok", "s_tok", "x"]}
    out = nc.dram_tensor("out", [T, D], F32, kind="ExternalOutput").ap()

    bufs = {nm: [nc.alloc_sbuf_tensor(f"b_{nm}{i}", [128, D], F32) for i in range(2)]
            for nm in ins}
    g_sb = [nc.alloc_sbuf_tensor(f"g_sb{i}", [128, D], F32) for i in range(2)]
    o_sb = [nc.alloc_sbuf_tensor(f"o_sb{i}", [128, D], F32) for i in range(2)]

    p = Prog(nc)
    st = []
    for i in range(NT):
        lds = []
        for nm in ins:
            deps = [st[i - 2]] if i >= 2 else []
            lds.append(p.dma("sync", deps, lambda q, nm=nm, i=i: q.dma_start(
                out=bufs[nm][i % 2][:], in_=ins[nm][128 * i:128 * (i + 1), :])))
        if kind == "ffn0fin":
            c1 = p.step("vector", lds, lambda v, i=i: v.tensor_add(
                out=g_sb[i % 2][:], in0=bufs["acc"][i % 2][:], in1=bufs["xn_tok"][i % 2][:]))
            c2 = p.step("scalar", [c1], lambda s, i=i: s.activation(
                out=g_sb[i % 2][:], in_=g_sb[i % 2][:], func=AF.Silu))
        else:
            c2 = lds[-1]
        def comb(v, i=i, kind=kind):
            gg = g_sb[i % 2] if kind == "ffn0fin" else bufs["g1_tok"][i % 2]
            v.tensor_mul(out=o_sb[i % 2][:], in0=bufs["s_tok"][i % 2][:], in1=gg[:])
            return v.tensor_add(out=o_sb[i % 2][:], in0=o_sb[i % 2][:], in1=bufs["x"][i % 2][:])
        c3 = p.step("vector", [c2] + lds, comb)
        st.append(p.dma("sync", [c3], lambda q, i=i: q.dma_start(
            out=out[128 * i:128 * (i + 1), :], in_=o_sb[i % 2][:])))
    p.step("sync", [st[-1]], lambda q: q.nop())
    return p.build()


def build_attn1():
    nc = bass.Bass("TRN2", target_bir_lowering=False, detect_race_conditions=False)
    xn_fmb = nc.dram_tensor("xn_fmb", [D, T], BF16, kind="ExternalInput").ap()
    WrvkT = nc.dram_tensor("WrvkT", [D, 3 * D], F32, kind="ExternalInput").ap()
    WoutT = nc.dram_tensor("WoutT", [D, D], F32, kind="ExternalInput").ap()
    tdc = nc.dram_tensor("tdc", [128, G], F32, kind="ExternalInput").ap()
    tfc = nc.dram_tensor("tfc", [128, G], F32, kind="ExternalInput").ap()
    s1_tok = nc.dram_tensor("s1_tok", [T, D], F32, kind="ExternalOutput").ap()
    k_d = nc.dram_tensor("k_d", [D, T], F32)
    v_d = nc.dram_tensor("v_d", [D, T], F32)
    sr_d = nc.dram_tensor("sr_d", [D, T], BF16)
    rw_d = nc.dram_tensor("rw_d", [D, T], BF16)

    w_sb = nc.alloc_sbuf_tensor("w_sb", [128, G * D], BF16)
    wo_sb = nc.alloc_sbuf_tensor("wo_sb", [128, G * D], BF16)
    s_td = nc.alloc_sbuf_tensor("s_td", [128, G], F32)
    s_tf = nc.alloc_sbuf_tensor("s_tf", [128, G], F32)
    cols = nc.alloc_sbuf_tensor("cols", [128, 8], F32)
    xn_sb = [nc.alloc_sbuf_tensor(f"xn_sb{i}", [128, G * 512], BF16) for i in range(2)]
    o_sb = [nc.alloc_sbuf_tensor(f"o_sb{i}", [128, G * 512], F32) for i in range(2)]
    ewr = nc.alloc_sbuf_tensor("ewr", [128, T], F32)
    bufA = nc.alloc_sbuf_tensor("bufA", [128, T], F32)
    bufB = nc.alloc_sbuf_tensor("bufB", [128, T], F32)
    bufD = nc.alloc_sbuf_tensor("bufD", [128, T], F32)
    bufE = nc.alloc_sbuf_tensor("bufE", [128, T], F32)
    bufF = nc.alloc_sbuf_tensor("bufF", [128, T], F32)
    sr_sb = nc.alloc_sbuf_tensor("sr_sb", [128, T], BF16)
    y_sb = [nc.alloc_sbuf_tensor(f"y_sb{i}", [128, D], F32) for i in range(2)]
    rwc_sb = xn_sb
    P = [nc.alloc_psum_tensor(f"P{i}", [128, 512], F32) for i in range(4)]

    p = Prog(nc)
    NONE = []

    cl = p.dma("gpsimd", NONE, lambda g: g.dma_start(out=s_td[:], in_=tdc))
    cl2 = p.dma("gpsimd", NONE, lambda g: g.dma_start(out=s_tf[:], in_=tfc))
    wol = [p.dma("gpsimd", NONE, lambda g, ig=ig: g.dma_start(
        out=wo_sb[:, ig * D:(ig + 1) * D], in_=WoutT[128 * ig:128 * (ig + 1), :])) for ig in range(G)]

    last = {}
    stores = []
    for part in range(3):
        wld = [p.dma("gpsimd", [last[("mm", None)]] if part else NONE,
                     lambda g, ig=ig, part=part: g.dma_start(
            out=w_sb[:, ig * D:(ig + 1) * D],
            in_=WrvkT[128 * ig:128 * (ig + 1), D * part:D * (part + 1)])) for ig in range(G)]
        w_done = wld[-1]
        for tc in range(NCH):
            kk = part * NCH + tc
            deps = [w_done]
            if kk >= 2:
                deps.append(last[("mm", kk - 2)])
            xl = p.dma("sync", deps, lambda q, tc=tc, kk=kk: q.dma_start(
                out=xn_sb[kk % 2].rearrange("p (g c) -> p g c", g=G),
                in_=xn_fmb.rearrange("(g p) t -> p g t", p=128)[:, :, 512 * tc:512 * (tc + 1)]))
            evs = []
            for og in range(G):
                bank = og % 2
                deps = [xl, w_done]
                if (kk, og) > (0, 1):
                    deps.append(last.get(("ev", bank)))
                def mmf(t, og=og, kk=kk, bank=bank):
                    for ig in range(G):
                        li = t.matmul(out=P[bank][:],
                                      lhsT=w_sb[:, ig * D + 128 * og:ig * D + 128 * (og + 1)],
                                      rhs=xn_sb[kk % 2][:, ig * 512:(ig + 1) * 512],
                                      start=(ig == 0), stop=(ig == G - 1))
                    return li
                ms = p.step("tensor", deps, mmf)
                last[("mm", None)] = ms
                last[("mm", kk)] = ms
                if part == 0:
                    ev = p.step("scalar", [ms] + ([stores[-2]] if len(stores) >= 2 else []),
                                lambda s, og=og, kk=kk, bank=bank: s.activation(
                        out=o_sb[kk % 2].bitcast(BF16)[:, og * 1024:og * 1024 + 512], in_=P[bank][:],
                        func=AF.Sigmoid))
                else:
                    ev = p.step("vector", [ms] + ([stores[-2]] if len(stores) >= 2 else []),
                                lambda v, og=og, kk=kk, bank=bank: v.tensor_copy(
                        out=o_sb[kk % 2][:, og * 512:(og + 1) * 512], in_=P[bank][:]))
                last[("ev", bank)] = ev
                evs.append(ev)
            dst = [sr_d, v_d, k_d][part]
            if part == 0:
                stores.append(p.dma("sync", [evs[-1]], lambda q, tc=tc, kk=kk: q.dma_start(
                    out=sr_d.rearrange("(g p) t -> p g t", p=128)[:, :, 512 * tc:512 * (tc + 1)],
                    in_=o_sb[kk % 2].bitcast(BF16).rearrange("p (g c) -> p g c", g=G)[:, :, 0:512])))
            else:
                stores.append(p.dma("sync", [evs[-1]], lambda q, tc=tc, kk=kk, dst=dst: q.dma_start(
                    out=dst.rearrange("(g p) t -> p g t", p=128)[:, :, 512 * tc:512 * (tc + 1)],
                    in_=o_sb[kk % 2].rearrange("p (g c) -> p g c", g=G))))
    ph1_done = stores[-1]


    rw_stores = []
    for g in range(G):
        deps = [ph1_done]
        if g >= 1:
            deps.append(last[("rw", g - 1)])
        lk = p.dma("sync", deps, lambda q, g=g: q.dma_start(out=bufA[:], in_=k_d[128 * g:128 * (g + 1), :]))
        lv = p.dma("sync", deps, lambda q, g=g: q.dma_start(out=bufB[:], in_=v_d[128 * g:128 * (g + 1), :]))
        lr = p.dma("sync", deps, lambda q, g=g: q.dma_start(out=sr_sb[:], in_=sr_d[128 * g:128 * (g + 1), :]))

        c1 = p.step("scalar", [cl, cl2] + ([last[("rw", g - 1)]] if g else []), lambda s, g=g: (
            s.activation(out=cols[:, 0:1], in_=s_td[:, g:g + 1], func=AF.Exp),
            s.activation(out=cols[:, 2:3], in_=s_tf[:, g:g + 1], func=AF.Exp))[-1])
        c2 = p.step("scalar", [c1], lambda s, g=g: s.activation(
            out=cols[:, 1:2], in_=cols[:, 0:1], func=AF.Exp, scale=-1.0))
        msetg = p.step("vector", [ph1_done] + ([last[("rw", g - 1)]] if g else []),
                       lambda v: v.memset(ewr[:], 1.0))
        ewt = p.step("vector", [c2, msetg], lambda v, g=g: v.tensor_scalar_mul(
            out=ewr[:], in0=ewr[:], scalar1=cols[:, 1:2]))
        ek = p.step("scalar", [lk, c2], lambda s, g=g: s.activation(out=bufD[:], in_=bufA[:], func=AF.Exp))
        ekv = p.step("vector", [ek, lv], lambda v, g=g: v.tensor_mul(out=bufE[:], in0=bufD[:], in1=bufB[:]))
        sa = p.step("vector", [ekv, ewt], lambda v, g=g: v.tensor_tensor_scan(
            out=bufF[:], data0=ewr[:], data1=bufE[:], initial=0.0, op0=OP.mult, op1=OP.add))
        sb_ = p.step("vector", [sa], lambda v, g=g: v.tensor_tensor_scan(
            out=bufA[:], data0=ewr[:], data1=bufD[:], initial=0.0, op0=OP.mult, op1=OP.add))
        num = p.step("vector", [sb_], lambda v, g=g: (
            v.tensor_scalar_mul(out=bufB[:, 0:1], in0=bufE[:, 0:1], scalar1=cols[:, 2:3]),
            v.scalar_tensor_tensor(out=bufB[:, 1:T], in0=bufE[:, 1:T], scalar=cols[:, 2:3],
                                   op0=OP.mult, op1=OP.add, in1=bufF[:, 0:T - 1]))[-1])
        den = p.step("vector", [num], lambda v, g=g: (
            v.tensor_scalar_mul(out=bufD[:, 0:1], in0=bufD[:, 0:1], scalar1=cols[:, 2:3]),
            v.scalar_tensor_tensor(out=bufD[:, 1:T], in0=bufD[:, 1:T], scalar=cols[:, 2:3],
                                   op0=OP.mult, op1=OP.add, in1=bufA[:, 0:T - 1]))[-1])
        rd = p.step("vector", [den], lambda v, g=g: v.reciprocal(out=bufE[:], in_=bufD[:]))
        wkv = p.step("vector", [rd], lambda v, g=g: v.tensor_mul(out=bufF[:], in0=bufB[:], in1=bufE[:]))
        rw = p.step("vector", [wkv, lr], lambda v, g=g: v.tensor_mul(out=sr_sb[:], in0=bufF[:], in1=sr_sb[:]))
        last[("rw", g)] = rw
        rw_stores.append(p.dma("sync", [rw], lambda q, g=g: q.dma_start(
            out=rw_d[128 * g:128 * (g + 1), :], in_=sr_sb[:])))
    ph2_done = rw_stores[-1]

    st3 = []
    for tc in range(NCH):
        deps = [ph2_done]
        if tc >= 2:
            deps.append(last[("omm", tc - 2)])
        rl = p.dma("sync", deps, lambda q, tc=tc: q.dma_start(
            out=rwc_sb[tc % 2].rearrange("p (g c) -> p g c", g=G),
            in_=rw_d.rearrange("(g p) t -> p g t", p=128)[:, :, 512 * tc:512 * (tc + 1)]))
        for tt in range(4):
            i = tc * 4 + tt
            deps = [rl, wol[-1]]
            if i >= 1:
                deps.append(last[("oev", None)])
            def omm(t, tc=tc, tt=tt):
                for od in range(2):
                    for ig in range(G):
                        li = t.matmul(out=P[od][:],
                                      lhsT=rwc_sb[tc % 2][:, ig * 512 + 128 * tt:ig * 512 + 128 * (tt + 1)],
                                      rhs=wo_sb[:, ig * D + 512 * od:ig * D + 512 * (od + 1)],
                                      start=(ig == 0), stop=(ig == G - 1))
                return li
            om = p.step("tensor", deps, omm)
            last[("omm", tc)] = om
            deps = [om]
            if i >= 2:
                deps.append(st3[i - 2])
            oe = p.step("vector", deps, lambda v, i=i: (
                v.tensor_copy(out=y_sb[i % 2][:, 0:512], in_=P[0][:]),
                v.tensor_copy(out=y_sb[i % 2][:, 512:1024], in_=P[1][:]))[-1])
            last[("oev", None)] = oe
            st3.append(p.dma("sync", [oe], lambda q, i=i: q.dma_start(
                out=s1_tok[128 * i:128 * (i + 1), :], in_=y_sb[i % 2][:])))
    p.step("sync", [st3[-1]], lambda q: q.nop())
    return p.build()


NE1 = 16
NH = 8


def build_ffn1():
    nc = bass.Bass("TRN2", target_bir_lowering=False, detect_race_conditions=False)
    xn_fmb = nc.dram_tensor("xn_fmb", [D, T], BF16, kind="ExternalInput").ap()
    xn_fm32 = nc.dram_tensor("xn_fm32", [D, T], F32, kind="ExternalInput").ap()
    wrT = nc.dram_tensor("wrT", [D, NE1], F32, kind="ExternalInput").ap()
    wmhT = nc.dram_tensor("wmhT", [D, D], F32, kind="ExternalInput").ap()
    wmgT = nc.dram_tensor("wmgT", [D, D], F32, kind="ExternalInput").ap()
    kfT = nc.dram_tensor("kfT", [NH, NE1, 128, 256], F32, kind="ExternalInput").ap()
    vf = nc.dram_tensor("vf", [NH, NE1, 256, 128], F32, kind="ExternalInput").ap()
    iota16 = nc.dram_tensor("iota16", [128, NE1], F32, kind="ExternalInput").ap()
    esel = nc.dram_tensor("esel", [NE1, NE1 * 128], F32, kind="ExternalInput").ap()
    iden = nc.dram_tensor("iden", [128, 128], F32, kind="ExternalInput").ap()
    g1_tok = nc.dram_tensor("g1_tok", [T, D], F32, kind="ExternalOutput").ap()
    y_d = nc.dram_tensor("y_d", [D, T], BF16)

    s_wr = nc.alloc_sbuf_tensor("s_wr", [128, G * NE1], F32)
    s_wmh = nc.alloc_sbuf_tensor("s_wmh", [128, G * D], BF16)
    s_wmg = nc.alloc_sbuf_tensor("s_wmg", [128, G * D], BF16)
    s_io16 = nc.alloc_sbuf_tensor("s_io16", [128, NE1], F32)
    s_esel = nc.alloc_sbuf_tensor("s_esel", [NE1, NE1 * 128], BF16)
    s_iden = nc.alloc_sbuf_tensor("s_iden", [128, 128], F32)
    xn32 = [nc.alloc_sbuf_tensor(f"xn32_{i}", [128, D], F32) for i in range(2)]
    mx_sb = nc.alloc_sbuf_tensor("mx_sb", [128, 8], F32)
    mi_sb = nc.alloc_sbuf_tensor("mi_sb", [128, 8], U32)
    gt_sb = nc.alloc_sbuf_tensor("gt_sb", [128, 2], F32)
    lg_sb = nc.alloc_sbuf_tensor("lg_sb", [128, NE1], F32)
    sa_sb = nc.alloc_sbuf_tensor("sa_sb", [128, NE1], F32)
    mk_sb = nc.alloc_sbuf_tensor("mk_sb", [128, NE1], F32)
    mx2_col = nc.alloc_sbuf_tensor("mx2_col", [128, 1], F32)
    wte_sb = nc.alloc_sbuf_tensor("wte_sb", [128, NE1], F32)
    wteT = nc.alloc_sbuf_tensor("wteT", [NE1, T], F32)
    wteTb = nc.alloc_sbuf_tensor("wteTb", [NE1, T], BF16)
    kf_sb = nc.alloc_sbuf_tensor("kf_sb", [128, 4 * NE1 * 256], BF16)
    vf_sb = nc.alloc_sbuf_tensor("vf_sb", [128, 4 * NE1 * 2 * 128], BF16)
    xnb_sb = [nc.alloc_sbuf_tensor(f"xnb_sb{i}", [128, G * 512], BF16) for i in range(2)]
    q_sb = nc.alloc_sbuf_tensor("q_sb", [128, G * 512], BF16)
    wb_sb = nc.alloc_sbuf_tensor("wb_sb", [128, NE1 * 512], BF16)
    act_sb = nc.alloc_sbuf_tensor("act_sb", [128, 2 * 512], BF16)
    y_sb = nc.alloc_sbuf_tensor("y_sb", [128, 4 * 512], BF16)
    go_sb = [nc.alloc_sbuf_tensor(f"go_sb{i}", [128, D], F32) for i in range(2)]
    sc_ps = nc.alloc_psum_tensor("sc_ps", [128, 1024], F32)
    P = [nc.alloc_psum_tensor(f"Pf{i}", [128, 512], F32) for i in range(5)]
    Pq = nc.alloc_psum_tensor("Pqf", [128, 128], F32)

    p = Prog(nc)
    NONE = []

    cl = []
    cl.append(p.dma("gpsimd", NONE, lambda g: g.dma_start(
        out=s_wr.rearrange("p (i c) -> p i c", i=G), in_=wrT.rearrange("(i p) c -> p i c", p=128))))
    for ig in range(G):
        cl.append(p.dma("gpsimd", NONE, lambda g, ig=ig: g.dma_start(
            out=s_wmh[:, ig * D:(ig + 1) * D], in_=wmhT[128 * ig:128 * (ig + 1), :])))
        cl.append(p.dma("gpsimd", NONE, lambda g, ig=ig: g.dma_start(
            out=s_wmg[:, ig * D:(ig + 1) * D], in_=wmgT[128 * ig:128 * (ig + 1), :])))
    cl.append(p.dma("gpsimd", NONE, lambda g: g.dma_start(out=s_io16[:], in_=iota16)))
    cl.append(p.dma("gpsimd", NONE, lambda g: g.dma_start(out=s_esel[:], in_=esel)))
    cl.append(p.dma("gpsimd", NONE, lambda g: g.dma_start(out=s_iden[:], in_=iden)))
    const_done = cl[-1]

    last = {}
    for i in range(NT):
        deps = [cl[0]]
        if i >= 2:
            deps.append(last[("aff", i - 2)])
        ld = p.dma("sync", deps, lambda q, i=i: q.dma_start(
            out=xn32[i % 2].rearrange("p (g c) -> p g c", g=G),
            in_=xn_fm32.rearrange("(g p) t -> p g t", p=128)[:, :, 128 * i:128 * (i + 1)]))
        def affmm(t, i=i):
            for ig in range(G):
                li = t.matmul(out=Pq[:, 0:NE1], lhsT=xn32[i % 2][:, ig * 128:(ig + 1) * 128],
                              rhs=s_wr[:, ig * NE1:(ig + 1) * NE1], start=(ig == 0), stop=(ig == G - 1))
            return li
        deps = [ld]
        if i >= 1:
            deps.append(last[("wte", i - 1)])
            deps.append(last[("sg", i - 1)])
            deps.append(last[("lcp", i - 1)])
        aff = p.step("tensor", deps, affmm)
        last[("aff", i)] = aff
        lcp = p.step("scalar", [aff], lambda s: s.copy(out=lg_sb[:], in_=Pq[:, 0:NE1]))
        last[("lcp", i)] = lcp
        sga = p.step("scalar", [lcp], lambda s: s.activation(out=sa_sb[:], in_=lg_sb[:],
                                                             func=AF.Sigmoid))
        last[("sg", i)] = sga
        mx = p.step("vector", [sga], lambda v: v.max(out=mx_sb[:], in_=sa_sb[:]))
        ms0 = p.step("vector", [mx], lambda v: v.memset(mx_sb[:, 2:8], -1.0))
        mr = p.step("vector", [ms0], lambda v: v.match_replace(
            out=mk_sb[:], in_to_replace=mx_sb[:], in_values=sa_sb[:], imm_value=0.0))
        wt = p.step("vector", [mr], lambda v: v.tensor_sub(
            out=wte_sb[:], in0=sa_sb[:], in1=mk_sb[:]))
        last[("wte", i)] = wt
        tp = p.step("tensor", [wt, const_done], lambda t, i=i: t.transpose(out=sc_ps[0:NE1, 0:128], in_=wte_sb[:], identity=s_iden[:]))
        ev = p.step("vector", [tp], lambda v, i=i: v.tensor_copy(
            out=wteT[:, 128 * i:128 * (i + 1)], in_=sc_ps[0:NE1, 0:128]))
        last[("wte", i)] = ev
    wcast = p.step("vector", [ev], lambda v: v.tensor_copy(out=wteTb[:], in_=wteT[:]))
    phA_done = wcast

    y_stores = []
    for hh in range(2):
        kl = []
        for h4 in range(4):
            h = hh * 4 + h4
            for e in range(NE1):
                deps = [last[("ymm", None)]] if (hh and e == 0 and h4 == 0) else NONE
                kl.append(p.dma("gpsimd", deps, lambda g, h=h, h4=h4, e=e: g.dma_start(
                    out=kf_sb[:, (h4 * NE1 + e) * 256:(h4 * NE1 + e) * 256 + 256],
                    in_=kfT[h, e])))
                kl.append(p.dma("gpsimd", deps, lambda g, h=h, h4=h4, e=e: g.dma_start(
                    out=vf_sb[:, (h4 * NE1 + e) * 256:(h4 * NE1 + e) * 256 + 256
                              ].rearrange("p (sc c) -> p sc c", sc=2),
                    in_=vf[h, e].rearrange("(sc p) c -> p sc c", p=128))))
        kv_done = kl[-1]

        for tc in range(NCH):
            kk = hh * NCH + tc
            deps = [const_done, phA_done]
            if kk >= 2:
                deps.append(last[("qmm", kk - 2)])
            xl = p.dma("sync", deps, lambda q, tc=tc, kk=kk: q.dma_start(
                out=xnb_sb[kk % 2].rearrange("p (g c) -> p g c", g=G),
                in_=xn_fmb.rearrange("(g p) t -> p g t", p=128)[:, :, 512 * tc:512 * (tc + 1)]))
            for og in range(G):
                bank = og % 2
                deps = [xl]
                if (kk, og) > (0, 1):
                    deps.append(last.get(("qev", bank)))
                def qmm(t, og=og, kk=kk, bank=bank):
                    for ig in range(G):
                        li = t.matmul(out=P[bank][:],
                                      lhsT=s_wmh[:, ig * D + 128 * og:ig * D + 128 * (og + 1)],
                                      rhs=xnb_sb[kk % 2][:, ig * 512:(ig + 1) * 512],
                                      start=(ig == 0), stop=(ig == G - 1))
                    return li
                qm = p.step("tensor", deps, qmm)
                last[("qmm", kk)] = qm
                qe = p.step("vector", [qm], lambda v, og=og, bank=bank: v.tensor_copy(
                    out=q_sb[:, og * 512:(og + 1) * 512], in_=P[bank][:]))
                last[("qev", bank)] = qe
            q_done = last[("qev", (G - 1) % 2)]

            wb_done = None
            for ep in range(NE1 // 2):
                deps2 = [phA_done]
                if ep >= 1:
                    deps2.append(last[("wbev", None)])
                elif kk >= 1:
                    deps2.append(last[("actm", None)])
                def wbmm2(t, tc=tc, ep=ep):
                    t.matmul(out=P[2][:], lhsT=s_esel[:, (2 * ep) * 128:(2 * ep + 1) * 128],
                             rhs=wteTb[:, 512 * tc:512 * (tc + 1)], start=True, stop=True)
                    return t.matmul(out=P[3][:], lhsT=s_esel[:, (2 * ep + 1) * 128:(2 * ep + 2) * 128],
                                    rhs=wteTb[:, 512 * tc:512 * (tc + 1)], start=True, stop=True)
                wm = p.step("tensor", deps2, wbmm2)
                def wbev(s, ep=ep):
                    s.copy(out=wb_sb[:, (2 * ep) * 512:(2 * ep + 1) * 512], in_=P[2][:])
                    return s.copy(out=wb_sb[:, (2 * ep + 1) * 512:(2 * ep + 2) * 512], in_=P[3][:])
                wb_done = p.step("scalar", [wm], wbev)
                last[("wbev", None)] = wb_done

            for h4 in range(4):
                h = hh * 4 + h4
                for e in range(NE1):
                    deps = [kv_done, q_done, wb_done]
                    if last.get(("gelu", None)) is not None:
                        deps.append(last[("gelu", None)])
                    def scmm(t, h4=h4, e=e, h=h):
                        base = (h4 * NE1 + e) * 256
                        for sc in range(2):
                            li = t.matmul(out=sc_ps[:, 512 * sc:512 * (sc + 1)],
                                          lhsT=kf_sb[:, base + 128 * sc:base + 128 * (sc + 1)],
                                          rhs=q_sb[:, h * 512:(h + 1) * 512],
                                          start=True, stop=True)
                        return li
                    sm = p.step("tensor", deps, scmm)
                    deps = [sm]
                    if last.get(("ymm", None)) is not None:
                        deps.append(last[("ymm", None)])
                    ge = p.step("scalar", deps, lambda s: s.activation(
                        out=act_sb[:], in_=sc_ps[:], func=AF.Gelu))
                    last[("gelu", None)] = ge
                    def actm(v, e=e):
                        v.tensor_mul(out=act_sb[:, 0:512], in0=act_sb[:, 0:512],
                                     in1=wb_sb[:, e * 512:(e + 1) * 512])
                        return v.tensor_mul(out=act_sb[:, 512:1024], in0=act_sb[:, 512:1024],
                                            in1=wb_sb[:, e * 512:(e + 1) * 512])
                    am = p.step("vector", [ge], actm)
                    last[("actm", None)] = am

                    deps = [am]
                    if e == 0 and last.get(("yev", None)) is not None:
                        deps.append(last[("yev", None)])
                    def ymm(t, h4=h4, e=e):
                        for sc in range(2):
                            li = t.matmul(out=P[4][:],
                                          lhsT=vf_sb[:, (h4 * NE1 + e) * 256 + 128 * sc:
                                                     (h4 * NE1 + e) * 256 + 128 * (sc + 1)],
                                          rhs=act_sb[:, 512 * sc:512 * (sc + 1)],
                                          start=(e == 0 and sc == 0), stop=(e == NE1 - 1 and sc == 1))
                        return li
                    ym = p.step("tensor", deps, ymm)
                    last[("ymm", None)] = ym
                ye = p.step("vector", [ym], lambda v, h4=h4: v.tensor_copy(
                    out=y_sb[:, h4 * 512:(h4 + 1) * 512], in_=P[4][:]))
                last[("yev", None)] = ye
            y_stores.append(p.dma("sync", [ye], lambda q, hh=hh, tc=tc: q.dma_start(
                out=y_d.rearrange("(g p) t -> p g t", p=128)[:, 4 * hh:4 * (hh + 1),
                                                            512 * tc:512 * (tc + 1)],
                in_=y_sb.rearrange("p (g c) -> p g c", g=4))))
    phB_done = y_stores[-1]

    st3 = []
    for tc in range(NCH):
        deps = [phB_done]
        if tc >= 2:
            deps.append(last[("gmm", tc - 2)])
        yl = p.dma("sync", deps, lambda q, tc=tc: q.dma_start(
            out=xnb_sb[tc % 2].rearrange("p (g c) -> p g c", g=G),
            in_=y_d.rearrange("(g p) t -> p g t", p=128)[:, :, 512 * tc:512 * (tc + 1)]))
        for tt in range(4):
            i = tc * 4 + tt
            deps = [yl]
            if i >= 1:
                deps.append(last[("gev", None)])
            def gmm(t, tc=tc, tt=tt):
                for od in range(2):
                    for ig in range(G):
                        li = t.matmul(out=P[od][:],
                                      lhsT=xnb_sb[tc % 2][:, ig * 512 + 128 * tt:ig * 512 + 128 * (tt + 1)],
                                      rhs=s_wmg[:, ig * D + 512 * od:ig * D + 512 * (od + 1)],
                                      start=(ig == 0), stop=(ig == G - 1))
                return li
            gm = p.step("tensor", deps, gmm)
            last[("gmm", tc)] = gm
            deps = [gm]
            if i >= 2:
                deps.append(st3[i - 2])
            gevs = p.step("scalar", deps, lambda s, i=i: (
                s.activation(out=go_sb[i % 2][:, 0:512], in_=P[0][:], func=AF.Silu),
                s.activation(out=go_sb[i % 2][:, 512:1024], in_=P[1][:], func=AF.Silu))[-1])
            last[("gev", None)] = gevs
            st3.append(p.dma("sync", [gevs], lambda q, i=i: q.dma_start(
                out=g1_tok[128 * i:128 * (i + 1), :], in_=go_sb[i % 2][:])))
    p.step("sync", [st3[-1]], lambda q: q.nop())
    return p.build()


_IDEN = np.eye(128, dtype=np.float32)
_PROGS = {}


def _prog(name, builder):
    if name not in _PROGS:
        _PROGS[name] = builder()
    return _PROGS[name]


def _run(name, builder, maps):
    nc = _prog(name, builder)
    return run_bass_kernel_spmd(nc, maps, core_ids=list(range(len(maps)))).results


def _f32c(a):
    return np.ascontiguousarray(np.asarray(a, dtype=np.float32))


def kernel(**inp):
    xs = _f32c(inp["L0_attn_x"])
    NCORE = xs.shape[0]

    a0 = {}
    for nm, key in [("WpT", "L0_attn_Wp"), ("WqT", "L0_attn_Wq"), ("WkT", "L0_attn_Wk"),
                    ("WvT", "L0_attn_Wv"), ("WoT", "L0_attn_Wo")]:
        a0[nm] = _f32c(np.asarray(inp[key]).T)
    for nm, key in [("bpc", "L0_attn_bp"), ("bqc", "L0_attn_bq"), ("bkc", "L0_attn_bk"),
                    ("bvc", "L0_attn_bv")]:
        a0[nm] = _f32c(_f32c(inp[key]).reshape(8, 128).T)
    a0["bo_row"] = _f32c(inp["L0_attn_bo"]).reshape(1, 1024)
    wsel = np.zeros((8, 128, 16), np.float32)
    wbc = np.zeros((8, 16, 128), np.float32)
    for g in range(8):
        for pp in range(128):
            h = 2 * g + pp // 64
            wsel[g, pp, h] = 1.0
            wbc[g, h, pp] = 1.0
    a0["wsel"] = wsel
    a0["wbc"] = wbc
    a0["ones1"] = np.ones((1, 128), np.float32)

    cen = _f32c(inp["L0_ffn_centroids"])
    lng = _f32c(inp["L0_ffn_ln_g"])
    lnb = _f32c(inp["L0_ffn_ln_b"])
    w1 = _f32c(inp["L0_ffn_w1"])
    b1 = _f32c(inp["L0_ffn_b1"])
    w2 = _f32c(inp["L0_ffn_w2"])
    b2 = _f32c(inp["L0_ffn_b2"])
    f0_base = {"cenT": _f32c(cen.T), "ones1": a0["ones1"]}
    f0_e = []
    for e in range(8):
        f0_e.append({
            "lg": _f32c(lng[e].reshape(8, 128).T),
            "lb": _f32c(lnb[e].reshape(8, 128).T),
            "b1c": _f32c(b1[e].reshape(32, 128).T),
            "b2_row": b2[e].reshape(1, 1024).copy(),
            "w1Te": _f32c(w1[e].T),
            "w2Te": _f32c(w2[e].T),
            "econst": np.full((128, 1), float(e), np.float32),
        })

    a1 = {
        "WrvkT": _f32c(np.asarray(inp["L1_attn_W_rvk"]).T),
        "WoutT": _f32c(np.asarray(inp["L1_attn_W_out"]).T),
        "tdc": _f32c(_f32c(inp["L1_attn_time_decay"]).reshape(8, 128).T),
        "tfc": _f32c(_f32c(inp["L1_attn_time_first"]).reshape(8, 128).T),
    }

    esel = np.zeros((16, 16 * 128), np.float32)
    for e in range(16):
        esel[e, e * 128:(e + 1) * 128] = 1.0
    f1 = {
        "wrT": _f32c(np.asarray(inp["L1_ffn_w_router"]).T),
        "wmhT": _f32c(np.asarray(inp["L1_ffn_w_mh"]).T),
        "wmgT": _f32c(np.asarray(inp["L1_ffn_w_mg"]).T),
        "kfT": _f32c(np.asarray(inp["L1_ffn_k_ffwd"]).transpose(0, 1, 3, 2)),
        "vf": _f32c(inp["L1_ffn_v_ffwd"]),
        "iota16": np.broadcast_to(np.arange(16, dtype=np.float32), (128, 16)).copy(),
        "esel": esel,
        "iden": _IDEN,
    }

    r = _run("ln0", lambda: build_ln(with_xhat=True),
             [{"x": np.ascontiguousarray(xs[c]), "iden": _IDEN} for c in range(NCORE)])
    xn_fmb = [r[c]["xn_fmb"] for c in range(NCORE)]
    xn_fm32 = [r[c]["xn_fm32"] for c in range(NCORE)]
    xn_tok = [r[c]["xn_tok"] for c in range(NCORE)]
    xh_fmb = [r[c]["xh_fmb"] for c in range(NCORE)]

    r = _run("attn0", build_attn0,
             [dict(a0, xn_fmb=xn_fmb[c]) for c in range(NCORE)])
    s_tok = [r[c]["s_tok"] for c in range(NCORE)]

    acc = [np.zeros((T, D), np.float32) for c in range(NCORE)]
    for e in range(8):
        r = _run("ffn0e", build_ffn0e,
                 [dict(f0_base, **f0_e[e], xh_fmb=xh_fmb[c], xn_fm32=xn_fm32[c],
                       acc_in=acc[c]) for c in range(NCORE)])
        acc = [r[c]["acc_out"] for c in range(NCORE)]

    r = _run("comb0", lambda: build_combine("ffn0fin"),
             [{"acc": acc[c], "xn_tok": xn_tok[c], "s_tok": s_tok[c],
               "x": np.ascontiguousarray(xs[c])} for c in range(NCORE)])
    x1 = [r[c]["out"] for c in range(NCORE)]

    r = _run("ln1", lambda: build_ln(with_xhat=False),
             [{"x": x1[c], "iden": _IDEN} for c in range(NCORE)])
    xn1_fmb = [r[c]["xn_fmb"] for c in range(NCORE)]
    xn1_fm32 = [r[c]["xn_fm32"] for c in range(NCORE)]

    r = _run("attn1", build_attn1,
             [dict(a1, xn_fmb=xn1_fmb[c]) for c in range(NCORE)])
    s1_tok = [r[c]["s1_tok"] for c in range(NCORE)]

    r = _run("ffn1", build_ffn1,
             [dict(f1, xn_fmb=xn1_fmb[c], xn_fm32=xn1_fm32[c]) for c in range(NCORE)])
    g1_tok = [r[c]["g1_tok"] for c in range(NCORE)]

    r = _run("comb1", lambda: build_combine("final"),
             [{"g1_tok": g1_tok[c], "s_tok": s1_tok[c], "x": x1[c]} for c in range(NCORE)])
    out = np.stack([r[c]["out"] for c in range(NCORE)]).astype(np.float32)
    return out


# version 02
# speedup vs baseline: 1.7527x; 1.7527x over previous
import numpy as np
import ml_dtypes
import concourse.bass as bass
import concourse.mybir as mybir
from concourse.bass_utils import run_bass_kernel_spmd
from bass_rust import IndirectOffsetOnAxis

F32 = mybir.dt.float32
BF16 = mybir.dt.bfloat16
U32 = mybir.dt.uint32
AF = mybir.ActivationFunctionType
OP = mybir.AluOpType
AX = mybir.AxisListType
NBF = ml_dtypes.bfloat16

T = 4096
D = 1024
G = 8
NT = T // 128
NCH = T // 512
EPS = 1e-5


class Cnt:
    def __init__(self, nc, name):
        self.h = nc.alloc_semaphore(name)
        self.v = 0

    def inc(self, instr, n=1):
        instr.then_inc(self.h, n)
        self.v += n


def build_ln(with_xhat):
    nc = bass.Bass("TRN2", target_bir_lowering=False, detect_race_conditions=False)
    x = nc.dram_tensor("x", [T, D], F32, kind="ExternalInput").ap()
    iden = nc.dram_tensor("iden", [128, 128], F32, kind="ExternalInput").ap()
    xn_fm32 = nc.dram_tensor("xn_fm32", [D, T], F32, kind="ExternalOutput").ap()
    xn_fmb = nc.dram_tensor("xn_fmb", [D, T], BF16, kind="ExternalOutput").ap()
    xn_tok = nc.dram_tensor("xn_tok", [T, D], F32, kind="ExternalOutput").ap()
    if with_xhat:
        xh_fmb = nc.dram_tensor("xh_fmb", [D, T], BF16, kind="ExternalOutput").ap()
        xh_tok = nc.dram_tensor("xh_tok", [T, D], BF16, kind="ExternalOutput").ap()

    sx = [nc.alloc_sbuf_tensor(f"sx{i}", [128, D], F32) for i in range(2)]
    s_sq = nc.alloc_sbuf_tensor("s_sq", [128, D], F32)
    smc = nc.alloc_sbuf_tensor("smc", [128, NT], F32)
    ssc = nc.alloc_sbuf_tensor("ssc", [128, NT], F32)
    mean = nc.alloc_sbuf_tensor("mean", [128, NT], F32)
    rstd = nc.alloc_sbuf_tensor("rstd", [128, NT], F32)
    cc = nc.alloc_sbuf_tensor("cc", [128, NT], F32)
    tmp = [nc.alloc_sbuf_tensor(f"tmp{i}", [128, NT], F32) for i in range(4)]
    s_iden = nc.alloc_sbuf_tensor("s_iden", [128, 128], F32)
    s_idenb = nc.alloc_sbuf_tensor("s_idenb", [128, 128], BF16)
    sxn = [nc.alloc_sbuf_tensor(f"sxn{i}", [128, D], F32) for i in range(2)]
    sxh = [nc.alloc_sbuf_tensor(f"sxh{i}", [128, D], BF16) for i in range(2)]
    tf32 = [nc.alloc_sbuf_tensor(f"tf32_{i}", [128, D], F32) for i in range(2)]
    tfb = [nc.alloc_sbuf_tensor(f"tfb_{i}", [128, D], BF16) for i in range(2)]
    thb = [nc.alloc_sbuf_tensor(f"thb_{i}", [128, D], BF16) for i in range(2)]
    pt = [nc.alloc_psum_tensor(f"pt{i}", [128, 128], F32) for i in range(2)]
    ptb = [nc.alloc_psum_tensor(f"ptb{i}", [128, 128], BF16) for i in range(2)]

    eps_col = nc.alloc_sbuf_tensor("eps_col", [128, 1], F32)
    eps2_col = nc.alloc_sbuf_tensor("eps2_col", [128, 1], F32)
    ld = Cnt(nc, "ld"); su = Cnt(nc, "su"); sv = Cnt(nc, "sv")
    ld2 = Cnt(nc, "ld2"); nrm = Cnt(nc, "nrm")
    tp = Cnt(nc, "tp"); tph = Cnt(nc, "tph"); ev = Cnt(nc, "ev"); evh = Cnt(nc, "evh")
    st = Cnt(nc, "st"); stok = Cnt(nc, "stok")
    c1 = Cnt(nc, "c1"); c2 = Cnt(nc, "c2")

    _all_sems = [ld, su, sv, ld2, nrm, tp, tph, ev, evh, st, stok, c1, c2]
    with nc.Block() as blk0:
        @blk0.sync
        def _(q):
            for _s in _all_sems:
                q.sem_clear(_s.h)

    with nc.Block() as blk:
        @blk.sync
        def _(q):
            q.dma_start(out=s_iden[:], in_=iden).then_inc(ld.h, 16)
            for i in range(NT):
                if i >= 2:
                    q.wait_ge(su.h, i - 1)
                    q.wait_ge(sv.h, i - 1)
                q.dma_start(out=sx[i % 2][:], in_=x[128 * i:128 * (i + 1), :]).then_inc(ld.h, 16)
        ld.v = 16 * (NT + 1)

        @blk.scalar
        def _(s):
            for i in range(NT):
                s.wait_ge(ld.h, 16 * (i + 2))
                s.activation(out=s_sq[:], in_=sx[i % 2][:], func=AF.Square,
                             accum_out=ssc[:, i:i + 1]).then_inc(su.h, 1)
        su.v = NT

        @blk.vector
        def _(v):
            v.memset(eps_col[:], EPS)
            for i in range(NT):
                v.wait_ge(ld.h, 16 * (i + 2))
                v.tensor_reduce(out=smc[:, i:i + 1], in_=sx[i % 2][:], op=OP.add, axis=AX.X).then_inc(sv.h, 1)
        sv.v = NT

    with nc.Block() as blk:
        @blk.vector
        def _(v):
            v.tensor_copy(out=s_idenb[:], in_=s_iden[:])
            v.memset(eps2_col[:], EPS * EPS)
            v.tensor_scalar_mul(out=mean[:], in0=smc[:], scalar1=1.0 / D)
            v.tensor_scalar_mul(out=tmp[0][:], in0=ssc[:], scalar1=1.0 / D)
            v.tensor_mul(out=tmp[1][:], in0=mean[:], in1=mean[:])
            v.tensor_sub(out=tmp[2][:], in0=tmp[0][:], in1=tmp[1][:]).then_inc(c1.h, 1)
        c1.v = 1

        @blk.scalar
        def _(s):
            s.wait_ge(c1.h, 1)
            s.activation(out=tmp[0][:], in_=tmp[2][:], func=AF.Ln, bias=eps_col[:]).then_inc(c2.h, 1)
            s.activation(out=tmp[1][:], in_=tmp[2][:], func=AF.Ln, bias=eps2_col[:], scale=1.0 + EPS).then_inc(c2.h, 1)
            s.wait_ge(c2.h, 2)
            s.activation(out=rstd[:], in_=tmp[0][:], func=AF.Exp, scale=-0.5)
            s.activation(out=cc[:], in_=tmp[1][:], func=AF.Exp, scale=-0.5).then_inc(c2.h, 1)
        c2.v = 3

    def emit_stores(q, i):
        q.wait_ge(ev.h, (i + 1) * G)
        q.dma_start(out=xn_fm32.rearrange("(g p) t -> p g t", p=128)[:, :, 128 * i:128 * (i + 1)],
                    in_=tf32[i % 2].rearrange("p (g c) -> p g c", g=G)).then_inc(st.h, 16)
        q.dma_start(out=xn_fmb.rearrange("(g p) t -> p g t", p=128)[:, :, 128 * i:128 * (i + 1)],
                    in_=tfb[i % 2].rearrange("p (g c) -> p g c", g=G)).then_inc(st.h, 16)
        if with_xhat:
            q.wait_ge(evh.h, (i + 1) * G)
            q.dma_start(out=xh_fmb.rearrange("(g p) t -> p g t", p=128)[:, :, 128 * i:128 * (i + 1)],
                        in_=thb[i % 2].rearrange("p (g c) -> p g c", g=G)).then_inc(st.h, 16)

    with nc.Block() as blk:
        @blk.sync
        def _(q):
            for i in range(NT):
                if i >= 2:
                    q.wait_ge(nrm.h, i - 1)
                q.dma_start(out=sx[i % 2][:], in_=x[128 * i:128 * (i + 1), :]).then_inc(ld2.h, 16)
                if i >= 2:
                    emit_stores(q, i - 2)
            emit_stores(q, NT - 2)
            emit_stores(q, NT - 1)

        @blk.vector
        def _(v):
            for i in range(NT):
                v.wait_ge(ld2.h, 16 * (i + 1))
                if i >= 2:
                    v.wait_ge(stok.h, 16 * (2 if with_xhat else 1) * (i - 1))
                    v.wait_ge(tp.h, G * (i - 1))
                    if with_xhat:
                        v.wait_ge(tph.h, G * (i - 1))
                if with_xhat:
                    v.tensor_scalar(out=sxn[i % 2][:], in0=sx[i % 2][:], scalar1=mean[:, i:i + 1],
                                    scalar2=rstd[:, i:i + 1], op0=OP.subtract, op1=OP.mult)
                    v.tensor_scalar(out=sxh[i % 2][:], in0=sx[i % 2][:], scalar1=mean[:, i:i + 1],
                                    scalar2=cc[:, i:i + 1], op0=OP.subtract, op1=OP.mult).then_inc(nrm.h, 1)
                else:
                    v.tensor_scalar(out=sxn[i % 2][:], in0=sx[i % 2][:], scalar1=mean[:, i:i + 1],
                                    scalar2=rstd[:, i:i + 1], op0=OP.subtract, op1=OP.mult).then_inc(nrm.h, 1)
                for g in range(G):
                    k = i * G + g
                    v.wait_ge(tp.h, k + 1)
                    v.tensor_copy(out=tf32[i % 2][:, 128 * g:128 * (g + 1)], in_=pt[k % 2][:])
                    v.tensor_copy(out=tfb[i % 2][:, 128 * g:128 * (g + 1)], in_=pt[k % 2][:]).then_inc(ev.h, 1)
        nrm.v = NT
        ev.v = NT * G

        @blk.tensor
        def _(t):
            for i in range(NT):
                t.wait_ge(nrm.h, i + 1)
                if i >= 2:
                    t.wait_ge(st.h, 16 * (3 if with_xhat else 2) * (i - 1))
                for g in range(G):
                    k = i * G + g
                    if k >= 2:
                        t.wait_ge(ev.h, k - 1)
                    t.transpose(out=pt[k % 2][:], in_=sxn[i % 2][:, 128 * g:128 * (g + 1)],
                                identity=s_iden[:]).then_inc(tp.h, 1)
                if with_xhat:
                    for g in range(G):
                        k = i * G + g
                        if k >= 2:
                            t.wait_ge(evh.h, k - 1)
                        t.transpose(out=ptb[k % 2][:], in_=sxh[i % 2][:, 128 * g:128 * (g + 1)],
                                    identity=s_idenb[:]).then_inc(tph.h, 1)
        tp.v = NT * G
        tph.v = NT * G if with_xhat else 0

        @blk.scalar
        def _(s):
            if with_xhat:
                for i in range(NT):
                    if i >= 2:
                        s.wait_ge(st.h, 48 * (i - 1))
                    for g in range(G):
                        k = i * G + g
                        s.wait_ge(tph.h, k + 1)
                        s.copy(out=thb[i % 2][:, 128 * g:128 * (g + 1)], in_=ptb[k % 2][:]).then_inc(evh.h, 1)
        evh.v = NT * G if with_xhat else 0

        @blk.gpsimd
        def _(gp):
            for i in range(NT):
                gp.wait_ge(nrm.h, i + 1)
                gp.dma_start(out=xn_tok[128 * i:128 * (i + 1), :], in_=sxn[i % 2][:]).then_inc(stok.h, 16)
                if with_xhat:
                    gp.dma_start(out=xh_tok[128 * i:128 * (i + 1), :], in_=sxh[i % 2][:]).then_inc(stok.h, 16)
        stok.v = 16 * NT * (2 if with_xhat else 1)

        @blk.sync
        def _(q):
            q.wait_ge(st.h, 16 * NT * (3 if with_xhat else 2))
            q.wait_ge(stok.h, stok.v)
        st.v = 16 * NT * (3 if with_xhat else 2)

    return nc


ENGS = ["sync", "gpsimd", "vector", "scalar", "tensor"]


class Step:
    __slots__ = ("eng", "seq", "dma")
    def __init__(self, eng, dma):
        self.eng, self.seq, self.dma = eng, None, dma


class Prog:
    def __init__(self, nc):
        self.nc = nc
        self.sem = {}
        for e in ENGS:
            self.sem[(e, 0)] = nc.alloc_semaphore(f"P_{e}")
            self.sem[(e, 1)] = nc.alloc_semaphore(f"P_{e}_d")
        self.items = []

    def step(self, eng, deps, emit):
        s = Step(eng, False)
        self.items.append((s, [d for d in deps if d is not None], emit))
        return s

    def dma(self, eng, deps, emit):
        s = Step(eng, True)
        self.items.append((s, [d for d in deps if d is not None], emit))
        return s

    def build(self):
        nc = self.nc
        seqs = {k: 0 for k in self.sem}
        for s, deps, emit in self.items:
            k = (s.eng, 1 if s.dma else 0)
            seqs[k] += 1
            s.seq = seqs[k]
        per_eng = {e: [] for e in ENGS}
        for it in self.items:
            per_eng[it[0].eng].append(it)

        with nc.Block() as blk0:
            @blk0.sync
            def _(q):
                for h in self.sem.values():
                    q.sem_clear(h)

        with nc.Block() as blk:
            def run_engine(e, h):
                for s, deps, emit in per_eng[e]:
                    need = {}
                    for d in deps:
                        key = (d.eng, 1 if d.dma else 0)
                        v = d.seq * (16 if d.dma else 1)
                        if need.get(key, 0) < v:
                            need[key] = v
                    for key, v in need.items():
                        h.wait_ge(self.sem[key], v)
                    last = emit(h)
                    assert last is not None, f"step on {e} returned None"
                    last.then_inc(self.sem[(s.eng, 1 if s.dma else 0)], 16 if s.dma else 1)

            for e in ENGS:
                if per_eng[e]:
                    getattr(blk, e)(lambda h, e=e: run_engine(e, h))
        return nc


def build_attn0():
    nc = bass.Bass("TRN2", target_bir_lowering=False, detect_race_conditions=False)
    xn_fmb = nc.dram_tensor("xn_fmb", [D, T], BF16, kind="ExternalInput").ap()
    wts = {nm: nc.dram_tensor(nm, [D, D], F32, kind="ExternalInput").ap()
           for nm in ["WpT", "WqT", "WkT", "WvT", "WoT"]}
    bcols = {nm: nc.dram_tensor(nm, [128, G], F32, kind="ExternalInput").ap()
             for nm in ["bpc", "bqc", "bkc", "bvc"]}
    bo_row = nc.dram_tensor("bo_row", [1, D], F32, kind="ExternalInput").ap()
    wsel = nc.dram_tensor("wsel", [G, 128, 16], F32, kind="ExternalInput").ap()
    wbc = nc.dram_tensor("wbc", [G, 16, 128], F32, kind="ExternalInput").ap()
    ones1 = nc.dram_tensor("ones1", [1, 128], F32, kind="ExternalInput").ap()
    s_tok = nc.dram_tensor("s_tok", [T, D], F32, kind="ExternalOutput").ap()
    xp_d = nc.dram_tensor("xp_d", [D, T], BF16)

    W = {nm: nc.alloc_sbuf_tensor("s_" + nm, [128, G * D], BF16) for nm in wts}
    B = {nm: nc.alloc_sbuf_tensor("s_" + nm, [128, G], F32) for nm in bcols}
    s_bo = nc.alloc_sbuf_tensor("s_bo", [1, D], BF16)
    s_ones = nc.alloc_sbuf_tensor("s_ones", [1, 128], BF16)
    s_wsel = nc.alloc_sbuf_tensor("s_wsel", [128, G * 16], BF16)
    s_wbc = nc.alloc_sbuf_tensor("s_wbc", [16, G * 128], BF16)
    xn_sb = [nc.alloc_sbuf_tensor(f"xn_sb{i}", [128, G * 512], BF16) for i in range(2)]
    xp_sb = [nc.alloc_sbuf_tensor(f"xp_sb{i}", [128, G * 512], BF16) for i in range(2)]
    w_sb = [nc.alloc_sbuf_tensor(f"w_sb{i}", [128, G * 514], BF16) for i in range(2)]
    q_sb = nc.alloc_sbuf_tensor("q_sb", [128, G * 512], BF16)
    k_sb = nc.alloc_sbuf_tensor("k_sb", [128, G * 514], BF16)
    v_sb = nc.alloc_sbuf_tensor("v_sb", [128, G * 514], BF16)
    qk_sb = nc.alloc_sbuf_tensor("qk_sb", [128, 3 * 512], BF16)
    sm_sb = nc.alloc_sbuf_tensor("sm_sb", [16, 512], F32)
    e_sb = nc.alloc_sbuf_tensor("e_sb", [16, 3 * 512], F32)
    ex_sb = nc.alloc_sbuf_tensor("ex_sb", [16, 3 * 512], F32)
    den_sb = nc.alloc_sbuf_tensor("den_sb", [16, 512], F32)
    lden_sb = nc.alloc_sbuf_tensor("lden_sb", [16, 512], F32)
    a_sb = nc.alloc_sbuf_tensor("a_sb", [16, 3 * 512], BF16)
    ab_sb = nc.alloc_sbuf_tensor("ab_sb", [128, 3 * 512], BF16)
    av_sb = nc.alloc_sbuf_tensor("av_sb", [128, G * 512], BF16)
    so_sb = [nc.alloc_sbuf_tensor(f"so_sb{i}", [128, D], F32) for i in range(2)]
    P = [nc.alloc_psum_tensor(f"P{i}", [128, 512], F32) for i in range(8)]

    p = Prog(nc)
    NONE = []

    wl = []
    for nm in ["WpT", "WqT", "WkT", "WvT", "WoT"]:
        for ig in range(G):
            wl.append(p.dma("gpsimd", NONE, lambda g, nm=nm, ig=ig: g.dma_start(
                out=W[nm][:, ig * D:(ig + 1) * D], in_=wts[nm][128 * ig:128 * (ig + 1), :])))
    for nm in ["bpc", "bqc", "bkc", "bvc"]:
        wl.append(p.dma("gpsimd", NONE, lambda g, nm=nm: g.dma_start(out=B[nm][:], in_=bcols[nm])))
    wl.append(p.dma("gpsimd", NONE, lambda g: g.dma_start(out=s_bo[:], in_=bo_row)))
    wl.append(p.dma("gpsimd", NONE, lambda g: g.dma_start(out=s_ones[:], in_=ones1)))
    wl.append(p.dma("gpsimd", NONE, lambda g: g.dma_start(
        out=s_wsel.rearrange("p (g c) -> p g c", g=G), in_=wsel.rearrange("g p c -> p g c"))))
    wl.append(p.dma("gpsimd", NONE, lambda g: g.dma_start(
        out=s_wbc.rearrange("p (g c) -> p g c", g=G), in_=wbc.rearrange("g p c -> p g c"))))
    wp_done = wl[7]
    w_all = wl[-1]

    xp_store = []
    ev_prev = {}
    xn_ld = []
    for tc in range(NCH):
        deps = [xp_store[tc - 2]] if tc >= 2 else []
        if tc >= 2:
            deps.append(ev_prev[("xn", tc - 2)])
        xn_ld.append(p.dma("sync", deps, lambda q, tc=tc: q.dma_start(
            out=xn_sb[tc % 2].rearrange("p (g c) -> p g c", g=G),
            in_=xn_fmb.rearrange("(g p) t -> p g t", p=128)[:, :, 512 * tc:512 * (tc + 1)])))
        evs = []
        for og in range(G):
            bank = og % 2
            deps = [xn_ld[tc], wp_done]
            if (tc, og) > (0, 1):
                deps.append(ev_prev[("p1", bank)])
            if og == 0 and tc >= 2:
                deps.append(xp_store[tc - 2])
            def mm(t, tc=tc, og=og, bank=bank):
                for ig in range(G):
                    last = t.matmul(out=P[bank][:],
                                    lhsT=W["WpT"][:, ig * D + 128 * og:ig * D + 128 * (og + 1)],
                                    rhs=xn_sb[tc % 2][:, ig * 512:(ig + 1) * 512],
                                    start=(ig == 0), stop=(ig == G - 1))
                return last
            ms = p.step("tensor", deps, mm)
            ev = p.step("vector", [ms, w_all], lambda v, tc=tc, og=og, bank=bank: v.tensor_scalar_add(
                out=xp_sb[tc % 2][:, og * 512:(og + 1) * 512], in0=P[bank][:],
                scalar1=B["bpc"][:, og:og + 1]))
            ev_prev[("p1", bank)] = ev
            evs.append(ev)
        ev_prev[("xn", tc)] = ms
        xp_store.append(p.dma("sync", [evs[-1]], lambda q, tc=tc: q.dma_start(
            out=xp_d.rearrange("(g p) t -> p g t", p=128)[:, :, 512 * tc:512 * (tc + 1)],
            in_=xp_sb[tc % 2].rearrange("p (g c) -> p g c", g=G))))
    xp_all = xp_store[-1]

    last = {}
    st_steps = []
    for tc in range(NCH):
        wnd = []
        deps = [xp_all]
        if tc >= 2:
            deps.append(last[("qkv_mm", tc - 2)])
        lo = 512 * tc - 1
        for g in range(G):
            if tc == 0:
                p.dma("sync", deps, lambda q, g=g: q.dma_start(
                    out=w_sb[0][:, g * 514:g * 514 + 1], in_=xp_d[128 * g:128 * (g + 1), T - 1:T], allow_slow_non_contiguous=True))
                wnd.append(p.dma("sync", deps, lambda q, g=g: q.dma_start(
                    out=w_sb[0][:, g * 514 + 1:(g + 1) * 514], in_=xp_d[128 * g:128 * (g + 1), 0:513])))
            elif tc == NCH - 1:
                p.dma("sync", deps, lambda q, g=g, lo=lo: q.dma_start(
                    out=w_sb[tc % 2][:, g * 514:(g + 1) * 514 - 1], in_=xp_d[128 * g:128 * (g + 1), lo:T]))
                wnd.append(p.dma("sync", deps, lambda q, g=g: q.dma_start(
                    out=w_sb[1][:, (g + 1) * 514 - 1:(g + 1) * 514], in_=xp_d[128 * g:128 * (g + 1), 0:1], allow_slow_non_contiguous=True)))
            else:
                wnd.append(p.dma("sync", deps, lambda q, g=g, lo=lo, tc=tc: q.dma_start(
                    out=w_sb[tc % 2][:, g * 514:(g + 1) * 514], in_=xp_d[128 * g:128 * (g + 1), lo:lo + 514])))
        wnd_done = wnd[-1]

        qkv_ev = []
        for og in range(G):
            deps = [wnd_done, w_all]
            if (tc, og) != (0, 0):
                deps.append(last[("qkv_ev", None)])
            else:
                deps.append(ev_prev[("p1", 0)])
                deps.append(ev_prev[("p1", 1)])
            if tc >= 1:
                deps.append(last[("abev_tc", tc - 1)])
                deps.append(last[("oev", tc - 1)])
            def qkvmm(t, tc=tc, og=og):
                for ig in range(G):
                    t.matmul(out=P[0][:], lhsT=W["WqT"][:, ig * D + 128 * og:ig * D + 128 * (og + 1)],
                             rhs=w_sb[tc % 2][:, ig * 514 + 1:ig * 514 + 513],
                             start=(ig == 0), stop=(ig == G - 1))
                for ig in range(G):
                    t.matmul(out=P[1][:], lhsT=W["WkT"][:, ig * D + 128 * og:ig * D + 128 * (og + 1)],
                             rhs=w_sb[tc % 2][:, ig * 514:ig * 514 + 512],
                             start=(ig == 0), stop=(ig == G - 1))
                for ig in range(G):
                    t.matmul(out=P[2][:, 0:2], lhsT=W["WkT"][:, ig * D + 128 * og:ig * D + 128 * (og + 1)],
                             rhs=w_sb[tc % 2][:, ig * 514 + 512:ig * 514 + 514],
                             start=(ig == 0), stop=(ig == G - 1))
                for ig in range(G):
                    t.matmul(out=P[3][:], lhsT=W["WvT"][:, ig * D + 128 * og:ig * D + 128 * (og + 1)],
                             rhs=w_sb[tc % 2][:, ig * 514:ig * 514 + 512],
                             start=(ig == 0), stop=(ig == G - 1))
                for ig in range(G):
                    last_i = t.matmul(out=P[4][:, 0:2], lhsT=W["WvT"][:, ig * D + 128 * og:ig * D + 128 * (og + 1)],
                                      rhs=w_sb[tc % 2][:, ig * 514 + 512:ig * 514 + 514],
                                      start=(ig == 0), stop=(ig == G - 1))
                return last_i
            qm = p.step("tensor", deps, qkvmm)
            last[("qkv_mm", tc)] = qm

            def qkvev(v, tc=tc, og=og):
                v.tensor_scalar(out=q_sb[:, og * 512:(og + 1) * 512], in0=P[0][:],
                                scalar1=B["bqc"][:, og:og + 1], scalar2=0.125,
                                op0=OP.add, op1=OP.mult)
                v.tensor_scalar_add(out=k_sb[:, og * 514:og * 514 + 512], in0=P[1][:],
                                    scalar1=B["bkc"][:, og:og + 1])
                v.tensor_scalar_add(out=k_sb[:, og * 514 + 512:og * 514 + 514], in0=P[2][:, 0:2],
                                    scalar1=B["bkc"][:, og:og + 1])
                v.tensor_scalar_add(out=v_sb[:, og * 514:og * 514 + 512], in0=P[3][:],
                                    scalar1=B["bvc"][:, og:og + 1])
                return v.tensor_scalar_add(out=v_sb[:, og * 514 + 512:og * 514 + 514], in0=P[4][:, 0:2],
                                           scalar1=B["bvc"][:, og:og + 1])
            qe = p.step("vector", [qm], qkvev)
            last[("qkv_ev", None)] = qe
            qkv_ev.append(qe)

        sc_mm = None
        for g in range(G):
            deps = [qkv_ev[g]]
            if g >= 1:
                deps.append(sc_mm)
            elif tc >= 1:
                deps.append(last[("smax", tc - 1)])
            def qkprod(v, tc=tc, g=g):
                for sh in range(3):
                    li = v.tensor_mul(out=qk_sb[:, sh * 512:(sh + 1) * 512],
                                      in0=q_sb[:, g * 512:(g + 1) * 512],
                                      in1=k_sb[:, g * 514 + sh:g * 514 + sh + 512])
                return li
            qkp = p.step("vector", deps, qkprod)

            def scmm(t, tc=tc, g=g):
                for sh in range(3):
                    li = t.matmul(out=P[5 + sh][0:16, :],
                                  lhsT=s_wsel[:, g * 16:(g + 1) * 16],
                                  rhs=qk_sb[:, sh * 512:(sh + 1) * 512],
                                  start=(g == 0), stop=(g == G - 1))
                return li
            sc_mm = p.step("tensor", [qkp], scmm)

        def smax1(v, tc=tc):
            v.tensor_copy(out=sm_sb[:], in_=P[5][0:16, :])
            v.tensor_max(out=sm_sb[:], in0=sm_sb[:], in1=P[6][0:16, :])
            v.tensor_max(out=sm_sb[:], in0=sm_sb[:], in1=P[7][0:16, :])
            for sh in range(3):
                li = v.tensor_sub(out=e_sb[:, sh * 512:(sh + 1) * 512],
                                  in0=P[5 + sh][0:16, :], in1=sm_sb[:])
            return li
        sm1 = p.step("vector", [sc_mm], smax1)
        last[("smax", tc)] = sm1

        def smax2(s, tc=tc):
            for sh in range(3):
                li = s.activation(out=ex_sb[:, sh * 512:(sh + 1) * 512],
                                  in_=e_sb[:, sh * 512:(sh + 1) * 512], func=AF.Exp)
            return li
        sm2 = p.step("scalar", [sm1], smax2)

        def smax3(v, tc=tc):
            v.tensor_add(out=den_sb[:], in0=ex_sb[:, 0:512], in1=ex_sb[:, 512:1024])
            return v.tensor_add(out=den_sb[:], in0=den_sb[:], in1=ex_sb[:, 1024:1536])
        sm3 = p.step("vector", [sm2], smax3)

        def smax4(s, tc=tc):
            return s.activation(out=lden_sb[:], in_=den_sb[:], func=AF.Ln)
        sm4 = p.step("scalar", [sm3], smax4)

        def smax5(v, tc=tc):
            for sh in range(3):
                li = v.tensor_sub(out=e_sb[:, sh * 512:(sh + 1) * 512],
                                  in0=e_sb[:, sh * 512:(sh + 1) * 512], in1=lden_sb[:])
            return li
        sm5 = p.step("vector", [sm4], smax5)

        def smax6(s, tc=tc):
            for sh in range(3):
                li = s.activation(out=a_sb[:, sh * 512:(sh + 1) * 512],
                                  in_=e_sb[:, sh * 512:(sh + 1) * 512], func=AF.Exp)
            return li
        sm6 = p.step("scalar", [sm5], smax6)
        last[("asb", tc)] = sm6

        av_st = None
        for g in range(G):
            deps = [sm6]
            if g >= 1:
                deps.append(last[("abev", None)])
            elif tc >= 1:
                deps.append(last[("av", tc - 1)])
            def bcmm(t, tc=tc, g=g):
                for sh in range(3):
                    li = t.matmul(out=P[sh][:], lhsT=s_wbc[:, g * 128:(g + 1) * 128],
                                  rhs=a_sb[:, sh * 512:(sh + 1) * 512], start=True, stop=True)
                return li
            bq = p.step("tensor", deps, bcmm)
            last[("bcast_ev", tc)] = bq

            def abev(s, tc=tc, g=g):
                for sh in range(3):
                    li = s.copy(out=ab_sb[:, sh * 512:(sh + 1) * 512], in_=P[sh][:])
                return li
            ae = p.step("scalar", [bq], abev)
            last[("abev", None)] = ae
            last[("abev_tc", tc)] = ae

            def avc(v, tc=tc, g=g):
                v.tensor_mul(out=av_sb[:, g * 512:(g + 1) * 512], in0=ab_sb[:, 0:512],
                             in1=v_sb[:, g * 514:g * 514 + 512])
                v.tensor_mul(out=qk_sb[:, 0:512], in0=ab_sb[:, 512:1024],
                             in1=v_sb[:, g * 514 + 1:g * 514 + 513])
                v.tensor_add(out=av_sb[:, g * 512:(g + 1) * 512], in0=av_sb[:, g * 512:(g + 1) * 512],
                             in1=qk_sb[:, 0:512])
                v.tensor_mul(out=qk_sb[:, 0:512], in0=ab_sb[:, 1024:1536],
                             in1=v_sb[:, g * 514 + 2:g * 514 + 514])
                return v.tensor_add(out=av_sb[:, g * 512:(g + 1) * 512],
                                    in0=av_sb[:, g * 512:(g + 1) * 512], in1=qk_sb[:, 0:512])
            av = p.step("vector", [ae], avc)
            last[("av", tc)] = av
        av_all = last[("av", tc)]

        oev_step = None
        for tt in range(4):
            deps = [av_all]
            if tt >= 1:
                deps.append(last[("oev", tc)])
            elif tc >= 1:
                deps.append(last[("oev", tc - 1)])
            def womm(t, tc=tc, tt=tt):
                for od in range(2):
                    ps = P[3 + od]
                    for ig in range(G):
                        t.matmul(out=ps[:], lhsT=av_sb[:, ig * 512 + 128 * tt:ig * 512 + 128 * (tt + 1)],
                                 rhs=W["WoT"][:, ig * D + 512 * od:ig * D + 512 * (od + 1)],
                                 start=(ig == 0), stop=False)
                    li = t.matmul(out=ps[:], lhsT=s_ones[:], rhs=s_bo[:, 512 * od:512 * (od + 1)],
                                  start=False, stop=True)
                return li
            wo = p.step("tensor", deps, womm)

            deps = [wo]
            if tc * 4 + tt >= 2:
                deps.append(st_steps[-2])
            def oev(v, tc=tc, tt=tt):
                v.tensor_copy(out=so_sb[tt % 2][:, 0:512], in_=P[3][:])
                return v.tensor_copy(out=so_sb[tt % 2][:, 512:1024], in_=P[4][:])
            oe = p.step("vector", deps, oev)
            last[("oev", tc)] = oe

            st_steps.append(p.dma("sync", [oe], lambda q, tc=tc, tt=tt: q.dma_start(
                out=s_tok[512 * tc + 128 * tt:512 * tc + 128 * (tt + 1), :], in_=so_sb[tt % 2][:])))

    p.step("sync", [st_steps[-1]], lambda q: q.nop())
    return p.build()


F_HALF = 2048
NFG = 16


def build_ffn0e():
    nc = bass.Bass("TRN2", target_bir_lowering=False, detect_race_conditions=False)
    xh_fmb = nc.dram_tensor("xh_fmb", [D, T], BF16, kind="ExternalInput").ap()
    xn_fm32 = nc.dram_tensor("xn_fm32", [D, T], F32, kind="ExternalInput").ap()
    cenT = nc.dram_tensor("cenT", [D, G], F32, kind="ExternalInput").ap()
    w1Te = nc.dram_tensor("w1Te", [D, 4096], F32, kind="ExternalInput").ap()
    w2Te = nc.dram_tensor("w2Te", [4096, D], F32, kind="ExternalInput").ap()
    lg = nc.dram_tensor("lg", [128, G], F32, kind="ExternalInput").ap()
    lb = nc.dram_tensor("lb", [128, G], F32, kind="ExternalInput").ap()
    b1c = nc.dram_tensor("b1c", [128, 32], F32, kind="ExternalInput").ap()
    b2_row = nc.dram_tensor("b2_row", [1, D], F32, kind="ExternalInput").ap()
    ones1 = nc.dram_tensor("ones1", [1, 128], F32, kind="ExternalInput").ap()
    econst = nc.dram_tensor("econst", [128, 1], F32, kind="ExternalInput").ap()
    acc_in = nc.dram_tensor("acc_in", [T, D], F32, kind="ExternalInput").ap()
    acc_out = nc.dram_tensor("acc_out", [T, D], F32, kind="ExternalOutput").ap()
    acc_mid = nc.dram_tensor("acc_mid", [T, D], F32)

    s_cen = nc.alloc_sbuf_tensor("s_cen", [128, G * G], F32)
    s_lg = nc.alloc_sbuf_tensor("s_lg", [128, G], F32)
    s_lb = nc.alloc_sbuf_tensor("s_lb", [128, G], F32)
    s_b1 = nc.alloc_sbuf_tensor("s_b1", [128, 32], F32)
    s_b2 = nc.alloc_sbuf_tensor("s_b2", [1, D], BF16)
    s_ones = nc.alloc_sbuf_tensor("s_ones", [1, 128], BF16)
    s_e = nc.alloc_sbuf_tensor("s_e", [128, 1], F32)
    xn32 = [nc.alloc_sbuf_tensor(f"xn32_{i}", [128, D], F32) for i in range(2)]
    mx_sb = nc.alloc_sbuf_tensor("mx_sb", [128, 8], F32)
    mi_sb = nc.alloc_sbuf_tensor("mi_sb", [128, 8], U32)
    a_col = nc.alloc_sbuf_tensor("a_col", [128, 1], F32)
    am_all = nc.alloc_sbuf_tensor("am_all", [128, NT], F32)
    w1_sb = nc.alloc_sbuf_tensor("w1_sb", [128, G * F_HALF], BF16)
    w2_sb = nc.alloc_sbuf_tensor("w2_sb", [128, NFG * D], BF16)
    xh_sb = [nc.alloc_sbuf_tensor(f"xh_sb{i}", [128, G * 512], BF16) for i in range(2)]
    xe_sb = nc.alloc_sbuf_tensor("xe_sb", [128, G * 512], BF16)
    h_sb = nc.alloc_sbuf_tensor("h_sb", [128, NFG * 512], BF16)
    ac_sb = [nc.alloc_sbuf_tensor(f"ac_sb{i}", [128, D], F32) for i in range(2)]
    P = [nc.alloc_psum_tensor(f"P{i}", [128, 512], F32) for i in range(6)]
    Pq = nc.alloc_psum_tensor("Pq", [128, 8], F32)

    p = Prog(nc)
    NONE = []

    wl = []
    wl.append(p.dma("gpsimd", NONE, lambda g: g.dma_start(
        out=s_cen.rearrange("p (i c) -> p i c", i=G),
        in_=cenT.rearrange("(i p) c -> p i c", p=128))))
    for nm, sb, dr in [("lg", s_lg, lg), ("lb", s_lb, lb), ("b1", s_b1, b1c),
                       ("b2", s_b2, b2_row), ("ones", s_ones, ones1), ("e", s_e, econst)]:
        wl.append(p.dma("gpsimd", NONE, lambda g, sb=sb, dr=dr: g.dma_start(out=sb[:], in_=dr)))
    const_done = wl[-1]

    mprev = None
    for i in range(NT):
        deps = [wl[0]]
        if i >= 2:
            deps.append(last_aff)
        ld = p.dma("sync", deps if i >= 2 else [wl[0]], lambda q, i=i: q.dma_start(
            out=xn32[i % 2].rearrange("p (g c) -> p g c", g=G),
            in_=xn_fm32.rearrange("(g p) t -> p g t", p=128)[:, :, 128 * i:128 * (i + 1)]))
        def affmm(t, i=i):
            for ig in range(G):
                li = t.matmul(out=Pq[:], lhsT=xn32[i % 2][:, ig * 128:(ig + 1) * 128],
                              rhs=s_cen[:, ig * G:(ig + 1) * G], start=(ig == 0), stop=(ig == G - 1))
            return li
        deps = [ld]
        if mprev is not None:
            deps.append(mprev)
        aff = p.step("tensor", deps, affmm)
        last_aff = aff
        mx = p.step("vector", [aff], lambda v, i=i: v.max(out=mx_sb[:], in_=Pq[:]))
        mi = p.step("vector", [mx], lambda v, i=i: v.max_index(out=mi_sb[:], in_max=mx_sb[:], in_=None)
                    if False else v.max_index(out=mi_sb[:], in_max=mx_sb[:], in_values=Pq[:]))
        mprev = mi
        sg = p.step("scalar", [mx], lambda s, i=i: s.activation(out=a_col[:], in_=mx_sb[:, 0:1],
                                                                func=AF.Sigmoid))
        am = p.step("vector", [mi, sg, const_done], lambda v, i=i: v.tensor_scalar(
            out=am_all[:, i:i + 1], in0=mi_sb[:, 0:1], scalar1=s_e[:], scalar2=a_col[:],
            op0=OP.is_equal, op1=OP.mult))
        mprev = am
    am_done = am

    last = {}
    acc_st = []
    for fh in range(2):
        w1l = []
        for ig in range(G):
            w1l.append(p.dma("gpsimd", [last.get(("hmm", None))] if fh else NONE,
                             lambda g, ig=ig, fh=fh: g.dma_start(
                out=w1_sb[:, ig * F_HALF:(ig + 1) * F_HALF],
                in_=w1Te[128 * ig:128 * (ig + 1), F_HALF * fh:F_HALF * (fh + 1)])))
        w2l = []
        for fk in range(NFG):
            w2l.append(p.dma("gpsimd", [last.get(("zmm", None))] if fh else NONE,
                             lambda g, fk=fk, fh=fh: g.dma_start(
                out=w2_sb[:, fk * D:(fk + 1) * D],
                in_=w2Te[F_HALF * fh + 128 * fk:F_HALF * fh + 128 * (fk + 1), :])))
        w1_done, w2_done = w1l[-1], w2l[-1]

        for tc in range(NCH):
            k = fh * NCH + tc
            deps = [const_done]
            if k >= 2:
                deps.append(last[("xe", k - 2)])
            xhl = p.dma("sync", deps, lambda q, tc=tc, k=k: q.dma_start(
                out=xh_sb[k % 2].rearrange("p (g c) -> p g c", g=G),
                in_=xh_fmb.rearrange("(g p) t -> p g t", p=128)[:, :, 512 * tc:512 * (tc + 1)]))

            deps = [xhl, const_done]
            if k >= 1:
                deps.append(last[("hmm", None)])
            def xec(v, k=k):
                for ig in range(G):
                    li = v.tensor_scalar(out=xe_sb[:, ig * 512:(ig + 1) * 512],
                                         in0=xh_sb[k % 2][:, ig * 512:(ig + 1) * 512],
                                         scalar1=s_lg[:, ig:ig + 1], scalar2=s_lb[:, ig:ig + 1],
                                         op0=OP.mult, op1=OP.add)
                return li
            xe = p.step("vector", deps, xec)
            last[("xe", k)] = xe

            hevs = []
            for fg in range(NFG):
                bank = fg % 2
                deps = [xe, w1_done]
                if fg >= 2 or k > 0:
                    deps.append(last.get(("hev", bank), xe))
                if k > 0 and fg < 2:
                    deps.append(last[("zmm", None)])
                def hmm(t, fg=fg, k=k, bank=bank, fh=fh):
                    for ig in range(G):
                        li = t.matmul(out=P[bank][:],
                                      lhsT=w1_sb[:, ig * F_HALF + 128 * fg:ig * F_HALF + 128 * (fg + 1)],
                                      rhs=xe_sb[:, ig * 512:(ig + 1) * 512],
                                      start=(ig == 0), stop=(ig == G - 1))
                    return li
                hm = p.step("tensor", deps, hmm)
                last[("hmm", None)] = hm
                he = p.step("scalar", [hm], lambda s, fg=fg, bank=bank, fh=fh: s.activation(
                    out=h_sb[:, fg * 512:(fg + 1) * 512], in_=P[bank][:], func=AF.Relu,
                    bias=s_b1[:, NFG * fh + fg:NFG * fh + fg + 1]))
                last[("hev", bank)] = he
                hevs.append(he)
            h_done = hevs[-1]

            for tt in range(4):
                i = tc * 4 + tt
                deps = [am_done]
                if i >= 2 or fh > 0:
                    deps.append(acc_st[-2] if len(acc_st) >= 2 else acc_st[0])
                acl = p.dma("sync", deps, lambda q, i=i, fh=fh: q.dma_start(
                    out=ac_sb[i % 2][:],
                    in_=(acc_in if fh == 0 else acc_mid)[128 * i:128 * (i + 1), :]))
                deps = [h_done, w2_done]
                if (fh, tc, tt) != (0, 0, 0):
                    deps.append(last[("zev", None)])
                def zmm(t, tt=tt, fh=fh):
                    for od in range(2):
                        for fk in range(NFG):
                            li = t.matmul(out=P[2 + od][:],
                                          lhsT=h_sb[:, fk * 512 + 128 * tt:fk * 512 + 128 * (tt + 1)],
                                          rhs=w2_sb[:, fk * D + 512 * od:fk * D + 512 * (od + 1)],
                                          start=(fk == 0), stop=(fk == NFG - 1 and fh == 1))
                        if fh == 0:
                            li = t.matmul(out=P[2 + od][:], lhsT=s_ones[:],
                                          rhs=s_b2[:, 512 * od:512 * (od + 1)],
                                          start=False, stop=True)
                    return li
                zm = p.step("tensor", deps, zmm)
                last[("zmm", None)] = zm

                def zev(v, i=i, tt=tt):
                    for od in range(2):
                        li = v.scalar_tensor_tensor(out=ac_sb[i % 2][:, 512 * od:512 * (od + 1)],
                                                    in0=P[2 + od][:], scalar=am_all[:, i:i + 1],
                                                    op0=OP.mult, op1=OP.add,
                                                    in1=ac_sb[i % 2][:, 512 * od:512 * (od + 1)])
                    return li
                ze = p.step("vector", [zm, acl], zev)
                last[("zev", None)] = ze
                acc_st.append(p.dma("sync", [ze], lambda q, i=i, fh=fh: q.dma_start(
                    out=(acc_mid if fh == 0 else acc_out)[128 * i:128 * (i + 1), :],
                    in_=ac_sb[i % 2][:])))

    p.step("sync", [acc_st[-1]], lambda q: q.nop())
    return p.build()


def build_combine(kind):
    nc = bass.Bass("TRN2", target_bir_lowering=False, detect_race_conditions=False)
    if kind == "ffn0fin":
        ins = {nm: nc.dram_tensor(nm, [T, D], F32, kind="ExternalInput").ap()
               for nm in ["acc", "xn_tok", "s_tok", "x"]}
    else:
        ins = {nm: nc.dram_tensor(nm, [T, D], F32, kind="ExternalInput").ap()
               for nm in ["g1_tok", "s_tok", "x"]}
    out = nc.dram_tensor("out", [T, D], F32, kind="ExternalOutput").ap()

    bufs = {nm: [nc.alloc_sbuf_tensor(f"b_{nm}{i}", [128, D], F32) for i in range(2)]
            for nm in ins}
    g_sb = [nc.alloc_sbuf_tensor(f"g_sb{i}", [128, D], F32) for i in range(2)]
    o_sb = [nc.alloc_sbuf_tensor(f"o_sb{i}", [128, D], F32) for i in range(2)]

    p = Prog(nc)
    st = []
    for i in range(NT):
        lds = []
        for nm in ins:
            deps = [st[i - 2]] if i >= 2 else []
            lds.append(p.dma("sync", deps, lambda q, nm=nm, i=i: q.dma_start(
                out=bufs[nm][i % 2][:], in_=ins[nm][128 * i:128 * (i + 1), :])))
        if kind == "ffn0fin":
            c1 = p.step("vector", lds, lambda v, i=i: v.tensor_add(
                out=g_sb[i % 2][:], in0=bufs["acc"][i % 2][:], in1=bufs["xn_tok"][i % 2][:]))
            c2 = p.step("scalar", [c1], lambda s, i=i: s.activation(
                out=g_sb[i % 2][:], in_=g_sb[i % 2][:], func=AF.Silu))
        else:
            c2 = lds[-1]
        def comb(v, i=i, kind=kind):
            gg = g_sb[i % 2] if kind == "ffn0fin" else bufs["g1_tok"][i % 2]
            v.tensor_mul(out=o_sb[i % 2][:], in0=bufs["s_tok"][i % 2][:], in1=gg[:])
            return v.tensor_add(out=o_sb[i % 2][:], in0=o_sb[i % 2][:], in1=bufs["x"][i % 2][:])
        c3 = p.step("vector", [c2] + lds, comb)
        st.append(p.dma("sync", [c3], lambda q, i=i: q.dma_start(
            out=out[128 * i:128 * (i + 1), :], in_=o_sb[i % 2][:])))
    p.step("sync", [st[-1]], lambda q: q.nop())
    return p.build()


def build_attn1():
    nc = bass.Bass("TRN2", target_bir_lowering=False, detect_race_conditions=False)
    xn_fmb = nc.dram_tensor("xn_fmb", [D, T], BF16, kind="ExternalInput").ap()
    WrvkT = nc.dram_tensor("WrvkT", [D, 3 * D], F32, kind="ExternalInput").ap()
    WoutT = nc.dram_tensor("WoutT", [D, D], F32, kind="ExternalInput").ap()
    tdc = nc.dram_tensor("tdc", [128, G], F32, kind="ExternalInput").ap()
    tfc = nc.dram_tensor("tfc", [128, G], F32, kind="ExternalInput").ap()
    s1_tok = nc.dram_tensor("s1_tok", [T, D], F32, kind="ExternalOutput").ap()
    k_d = nc.dram_tensor("k_d", [D, T], F32)
    v_d = nc.dram_tensor("v_d", [D, T], F32)
    sr_d = nc.dram_tensor("sr_d", [D, T], BF16)
    rw_d = nc.dram_tensor("rw_d", [D, T], BF16)

    w_sb = nc.alloc_sbuf_tensor("w_sb", [128, G * D], BF16)
    wo_sb = nc.alloc_sbuf_tensor("wo_sb", [128, G * D], BF16)
    s_td = nc.alloc_sbuf_tensor("s_td", [128, G], F32)
    s_tf = nc.alloc_sbuf_tensor("s_tf", [128, G], F32)
    cols = nc.alloc_sbuf_tensor("cols", [128, 8], F32)
    xn_sb = [nc.alloc_sbuf_tensor(f"xn_sb{i}", [128, G * 512], BF16) for i in range(2)]
    o_sb = [nc.alloc_sbuf_tensor(f"o_sb{i}", [128, G * 512], F32) for i in range(2)]
    ewr = nc.alloc_sbuf_tensor("ewr", [128, T], F32)
    bufA = nc.alloc_sbuf_tensor("bufA", [128, T], F32)
    bufB = nc.alloc_sbuf_tensor("bufB", [128, T], F32)
    bufD = nc.alloc_sbuf_tensor("bufD", [128, T], F32)
    bufE = nc.alloc_sbuf_tensor("bufE", [128, T], F32)
    bufF = nc.alloc_sbuf_tensor("bufF", [128, T], F32)
    sr_sb = nc.alloc_sbuf_tensor("sr_sb", [128, T], BF16)
    y_sb = [nc.alloc_sbuf_tensor(f"y_sb{i}", [128, D], F32) for i in range(2)]
    rwc_sb = xn_sb
    P = [nc.alloc_psum_tensor(f"P{i}", [128, 512], F32) for i in range(4)]

    p = Prog(nc)
    NONE = []

    cl = p.dma("gpsimd", NONE, lambda g: g.dma_start(out=s_td[:], in_=tdc))
    cl2 = p.dma("gpsimd", NONE, lambda g: g.dma_start(out=s_tf[:], in_=tfc))
    wol = [p.dma("gpsimd", NONE, lambda g, ig=ig: g.dma_start(
        out=wo_sb[:, ig * D:(ig + 1) * D], in_=WoutT[128 * ig:128 * (ig + 1), :])) for ig in range(G)]

    last = {}
    stores = []
    for part in range(3):
        wld = [p.dma("gpsimd", [last[("mm", None)]] if part else NONE,
                     lambda g, ig=ig, part=part: g.dma_start(
            out=w_sb[:, ig * D:(ig + 1) * D],
            in_=WrvkT[128 * ig:128 * (ig + 1), D * part:D * (part + 1)])) for ig in range(G)]
        w_done = wld[-1]
        for tc in range(NCH):
            kk = part * NCH + tc
            deps = [w_done]
            if kk >= 2:
                deps.append(last[("mm", kk - 2)])
            xl = p.dma("sync", deps, lambda q, tc=tc, kk=kk: q.dma_start(
                out=xn_sb[kk % 2].rearrange("p (g c) -> p g c", g=G),
                in_=xn_fmb.rearrange("(g p) t -> p g t", p=128)[:, :, 512 * tc:512 * (tc + 1)]))
            evs = []
            for og in range(G):
                bank = og % 2
                deps = [xl, w_done]
                if (kk, og) > (0, 1):
                    deps.append(last.get(("ev", bank)))
                def mmf(t, og=og, kk=kk, bank=bank):
                    for ig in range(G):
                        li = t.matmul(out=P[bank][:],
                                      lhsT=w_sb[:, ig * D + 128 * og:ig * D + 128 * (og + 1)],
                                      rhs=xn_sb[kk % 2][:, ig * 512:(ig + 1) * 512],
                                      start=(ig == 0), stop=(ig == G - 1))
                    return li
                ms = p.step("tensor", deps, mmf)
                last[("mm", None)] = ms
                last[("mm", kk)] = ms
                if part == 0:
                    ev = p.step("scalar", [ms] + ([stores[-2]] if len(stores) >= 2 else []),
                                lambda s, og=og, kk=kk, bank=bank: s.activation(
                        out=o_sb[kk % 2].bitcast(BF16)[:, og * 1024:og * 1024 + 512], in_=P[bank][:],
                        func=AF.Sigmoid))
                else:
                    ev = p.step("vector", [ms] + ([stores[-2]] if len(stores) >= 2 else []),
                                lambda v, og=og, kk=kk, bank=bank: v.tensor_copy(
                        out=o_sb[kk % 2][:, og * 512:(og + 1) * 512], in_=P[bank][:]))
                last[("ev", bank)] = ev
                evs.append(ev)
            dst = [sr_d, v_d, k_d][part]
            if part == 0:
                stores.append(p.dma("sync", [evs[-1]], lambda q, tc=tc, kk=kk: q.dma_start(
                    out=sr_d.rearrange("(g p) t -> p g t", p=128)[:, :, 512 * tc:512 * (tc + 1)],
                    in_=o_sb[kk % 2].bitcast(BF16).rearrange("p (g c) -> p g c", g=G)[:, :, 0:512])))
            else:
                stores.append(p.dma("sync", [evs[-1]], lambda q, tc=tc, kk=kk, dst=dst: q.dma_start(
                    out=dst.rearrange("(g p) t -> p g t", p=128)[:, :, 512 * tc:512 * (tc + 1)],
                    in_=o_sb[kk % 2].rearrange("p (g c) -> p g c", g=G))))
    ph1_done = stores[-1]


    rw_stores = []
    for g in range(G):
        deps = [ph1_done]
        if g >= 1:
            deps.append(last[("rw", g - 1)])
        lk = p.dma("sync", deps, lambda q, g=g: q.dma_start(out=bufA[:], in_=k_d[128 * g:128 * (g + 1), :]))
        lv = p.dma("sync", deps, lambda q, g=g: q.dma_start(out=bufB[:], in_=v_d[128 * g:128 * (g + 1), :]))
        lr = p.dma("sync", deps, lambda q, g=g: q.dma_start(out=sr_sb[:], in_=sr_d[128 * g:128 * (g + 1), :]))

        c1 = p.step("scalar", [cl, cl2] + ([last[("rw", g - 1)]] if g else []), lambda s, g=g: (
            s.activation(out=cols[:, 0:1], in_=s_td[:, g:g + 1], func=AF.Exp),
            s.activation(out=cols[:, 2:3], in_=s_tf[:, g:g + 1], func=AF.Exp))[-1])
        c2 = p.step("scalar", [c1], lambda s, g=g: s.activation(
            out=cols[:, 1:2], in_=cols[:, 0:1], func=AF.Exp, scale=-1.0))
        msetg = p.step("vector", [ph1_done] + ([last[("rw", g - 1)]] if g else []),
                       lambda v: v.memset(ewr[:], 1.0))
        ewt = p.step("vector", [c2, msetg], lambda v, g=g: v.tensor_scalar_mul(
            out=ewr[:], in0=ewr[:], scalar1=cols[:, 1:2]))
        ek = p.step("scalar", [lk, c2], lambda s, g=g: s.activation(out=bufD[:], in_=bufA[:], func=AF.Exp))
        ekv = p.step("vector", [ek, lv], lambda v, g=g: v.tensor_mul(out=bufE[:], in0=bufD[:], in1=bufB[:]))
        sa = p.step("vector", [ekv, ewt], lambda v, g=g: v.tensor_tensor_scan(
            out=bufF[:], data0=ewr[:], data1=bufE[:], initial=0.0, op0=OP.mult, op1=OP.add))
        sb_ = p.step("vector", [sa], lambda v, g=g: v.tensor_tensor_scan(
            out=bufA[:], data0=ewr[:], data1=bufD[:], initial=0.0, op0=OP.mult, op1=OP.add))
        num = p.step("vector", [sb_], lambda v, g=g: (
            v.tensor_scalar_mul(out=bufB[:, 0:1], in0=bufE[:, 0:1], scalar1=cols[:, 2:3]),
            v.scalar_tensor_tensor(out=bufB[:, 1:T], in0=bufE[:, 1:T], scalar=cols[:, 2:3],
                                   op0=OP.mult, op1=OP.add, in1=bufF[:, 0:T - 1]))[-1])
        den = p.step("vector", [num], lambda v, g=g: (
            v.tensor_scalar_mul(out=bufD[:, 0:1], in0=bufD[:, 0:1], scalar1=cols[:, 2:3]),
            v.scalar_tensor_tensor(out=bufD[:, 1:T], in0=bufD[:, 1:T], scalar=cols[:, 2:3],
                                   op0=OP.mult, op1=OP.add, in1=bufA[:, 0:T - 1]))[-1])
        rd = p.step("vector", [den], lambda v, g=g: v.reciprocal(out=bufE[:], in_=bufD[:]))
        wkv = p.step("vector", [rd], lambda v, g=g: v.tensor_mul(out=bufF[:], in0=bufB[:], in1=bufE[:]))
        rw = p.step("vector", [wkv, lr], lambda v, g=g: v.tensor_mul(out=sr_sb[:], in0=bufF[:], in1=sr_sb[:]))
        last[("rw", g)] = rw
        rw_stores.append(p.dma("sync", [rw], lambda q, g=g: q.dma_start(
            out=rw_d[128 * g:128 * (g + 1), :], in_=sr_sb[:])))
    ph2_done = rw_stores[-1]

    st3 = []
    for tc in range(NCH):
        deps = [ph2_done]
        if tc >= 2:
            deps.append(last[("omm", tc - 2)])
        rl = p.dma("sync", deps, lambda q, tc=tc: q.dma_start(
            out=rwc_sb[tc % 2].rearrange("p (g c) -> p g c", g=G),
            in_=rw_d.rearrange("(g p) t -> p g t", p=128)[:, :, 512 * tc:512 * (tc + 1)]))
        for tt in range(4):
            i = tc * 4 + tt
            deps = [rl, wol[-1]]
            if i >= 1:
                deps.append(last[("oev", None)])
            def omm(t, tc=tc, tt=tt):
                for od in range(2):
                    for ig in range(G):
                        li = t.matmul(out=P[od][:],
                                      lhsT=rwc_sb[tc % 2][:, ig * 512 + 128 * tt:ig * 512 + 128 * (tt + 1)],
                                      rhs=wo_sb[:, ig * D + 512 * od:ig * D + 512 * (od + 1)],
                                      start=(ig == 0), stop=(ig == G - 1))
                return li
            om = p.step("tensor", deps, omm)
            last[("omm", tc)] = om
            deps = [om]
            if i >= 2:
                deps.append(st3[i - 2])
            oe = p.step("vector", deps, lambda v, i=i: (
                v.tensor_copy(out=y_sb[i % 2][:, 0:512], in_=P[0][:]),
                v.tensor_copy(out=y_sb[i % 2][:, 512:1024], in_=P[1][:]))[-1])
            last[("oev", None)] = oe
            st3.append(p.dma("sync", [oe], lambda q, i=i: q.dma_start(
                out=s1_tok[128 * i:128 * (i + 1), :], in_=y_sb[i % 2][:])))
    p.step("sync", [st3[-1]], lambda q: q.nop())
    return p.build()


NE1 = 16
NH = 8


def build_ffn1():
    nc = bass.Bass("TRN2", target_bir_lowering=False, detect_race_conditions=False)
    xn_fmb = nc.dram_tensor("xn_fmb", [D, T], BF16, kind="ExternalInput").ap()
    xn_fm32 = nc.dram_tensor("xn_fm32", [D, T], F32, kind="ExternalInput").ap()
    wrT = nc.dram_tensor("wrT", [D, NE1], F32, kind="ExternalInput").ap()
    wmhT = nc.dram_tensor("wmhT", [D, D], F32, kind="ExternalInput").ap()
    wmgT = nc.dram_tensor("wmgT", [D, D], F32, kind="ExternalInput").ap()
    kfT = nc.dram_tensor("kfT", [NH, NE1, 128, 256], F32, kind="ExternalInput").ap()
    vf = nc.dram_tensor("vf", [NH, NE1, 256, 128], F32, kind="ExternalInput").ap()
    iota16 = nc.dram_tensor("iota16", [128, NE1], F32, kind="ExternalInput").ap()
    esel = nc.dram_tensor("esel", [NE1, NE1 * 128], F32, kind="ExternalInput").ap()
    iden = nc.dram_tensor("iden", [128, 128], F32, kind="ExternalInput").ap()
    g1_tok = nc.dram_tensor("g1_tok", [T, D], F32, kind="ExternalOutput").ap()
    y_d = nc.dram_tensor("y_d", [D, T], BF16)

    s_wr = nc.alloc_sbuf_tensor("s_wr", [128, G * NE1], F32)
    s_wmh = nc.alloc_sbuf_tensor("s_wmh", [128, G * D], BF16)
    s_wmg = nc.alloc_sbuf_tensor("s_wmg", [128, G * D], BF16)
    s_io16 = nc.alloc_sbuf_tensor("s_io16", [128, NE1], F32)
    s_esel = nc.alloc_sbuf_tensor("s_esel", [NE1, NE1 * 128], BF16)
    s_iden = nc.alloc_sbuf_tensor("s_iden", [128, 128], F32)
    xn32 = [nc.alloc_sbuf_tensor(f"xn32_{i}", [128, D], F32) for i in range(2)]
    mx_sb = nc.alloc_sbuf_tensor("mx_sb", [128, 8], F32)
    mi_sb = nc.alloc_sbuf_tensor("mi_sb", [128, 8], U32)
    gt_sb = nc.alloc_sbuf_tensor("gt_sb", [128, 2], F32)
    lg_sb = nc.alloc_sbuf_tensor("lg_sb", [128, NE1], F32)
    sa_sb = nc.alloc_sbuf_tensor("sa_sb", [128, NE1], F32)
    mk_sb = nc.alloc_sbuf_tensor("mk_sb", [128, NE1], F32)
    mx2_col = nc.alloc_sbuf_tensor("mx2_col", [128, 1], F32)
    wte_sb = nc.alloc_sbuf_tensor("wte_sb", [128, NE1], F32)
    wteT = nc.alloc_sbuf_tensor("wteT", [NE1, T], F32)
    wteTb = nc.alloc_sbuf_tensor("wteTb", [NE1, T], BF16)
    kf_sb = nc.alloc_sbuf_tensor("kf_sb", [128, 4 * NE1 * 256], BF16)
    vf_sb = nc.alloc_sbuf_tensor("vf_sb", [128, 4 * NE1 * 2 * 128], BF16)
    xnb_sb = [nc.alloc_sbuf_tensor(f"xnb_sb{i}", [128, G * 512], BF16) for i in range(2)]
    q_sb = nc.alloc_sbuf_tensor("q_sb", [128, G * 512], BF16)
    wb_sb = nc.alloc_sbuf_tensor("wb_sb", [128, NE1 * 512], BF16)
    act_sb = nc.alloc_sbuf_tensor("act_sb", [128, 2 * 512], BF16)
    y_sb = nc.alloc_sbuf_tensor("y_sb", [128, 4 * 512], BF16)
    go_sb = [nc.alloc_sbuf_tensor(f"go_sb{i}", [128, D], F32) for i in range(2)]
    sc_ps = nc.alloc_psum_tensor("sc_ps", [128, 1024], F32)
    P = [nc.alloc_psum_tensor(f"Pf{i}", [128, 512], F32) for i in range(5)]
    Pq = nc.alloc_psum_tensor("Pqf", [128, 128], F32)

    p = Prog(nc)
    NONE = []

    cl = []
    cl.append(p.dma("gpsimd", NONE, lambda g: g.dma_start(
        out=s_wr.rearrange("p (i c) -> p i c", i=G), in_=wrT.rearrange("(i p) c -> p i c", p=128))))
    for ig in range(G):
        cl.append(p.dma("gpsimd", NONE, lambda g, ig=ig: g.dma_start(
            out=s_wmh[:, ig * D:(ig + 1) * D], in_=wmhT[128 * ig:128 * (ig + 1), :])))
        cl.append(p.dma("gpsimd", NONE, lambda g, ig=ig: g.dma_start(
            out=s_wmg[:, ig * D:(ig + 1) * D], in_=wmgT[128 * ig:128 * (ig + 1), :])))
    cl.append(p.dma("gpsimd", NONE, lambda g: g.dma_start(out=s_io16[:], in_=iota16)))
    cl.append(p.dma("gpsimd", NONE, lambda g: g.dma_start(out=s_esel[:], in_=esel)))
    cl.append(p.dma("gpsimd", NONE, lambda g: g.dma_start(out=s_iden[:], in_=iden)))
    const_done = cl[-1]

    last = {}
    for i in range(NT):
        deps = [cl[0]]
        if i >= 2:
            deps.append(last[("aff", i - 2)])
        ld = p.dma("sync", deps, lambda q, i=i: q.dma_start(
            out=xn32[i % 2].rearrange("p (g c) -> p g c", g=G),
            in_=xn_fm32.rearrange("(g p) t -> p g t", p=128)[:, :, 128 * i:128 * (i + 1)]))
        def affmm(t, i=i):
            for ig in range(G):
                li = t.matmul(out=Pq[:, 0:NE1], lhsT=xn32[i % 2][:, ig * 128:(ig + 1) * 128],
                              rhs=s_wr[:, ig * NE1:(ig + 1) * NE1], start=(ig == 0), stop=(ig == G - 1))
            return li
        deps = [ld]
        if i >= 1:
            deps.append(last[("wte", i - 1)])
            deps.append(last[("sg", i - 1)])
            deps.append(last[("lcp", i - 1)])
        aff = p.step("tensor", deps, affmm)
        last[("aff", i)] = aff
        lcp = p.step("scalar", [aff], lambda s: s.copy(out=lg_sb[:], in_=Pq[:, 0:NE1]))
        last[("lcp", i)] = lcp
        sga = p.step("scalar", [lcp], lambda s: s.activation(out=sa_sb[:], in_=lg_sb[:],
                                                             func=AF.Sigmoid))
        last[("sg", i)] = sga
        mx = p.step("vector", [sga], lambda v: v.max(out=mx_sb[:], in_=sa_sb[:]))
        ms0 = p.step("vector", [mx], lambda v: v.memset(mx_sb[:, 2:8], -1.0))
        mr = p.step("vector", [ms0], lambda v: v.match_replace(
            out=mk_sb[:], in_to_replace=mx_sb[:], in_values=sa_sb[:], imm_value=0.0))
        wt = p.step("vector", [mr], lambda v: v.tensor_sub(
            out=wte_sb[:], in0=sa_sb[:], in1=mk_sb[:]))
        last[("wte", i)] = wt
        tp = p.step("tensor", [wt, const_done], lambda t, i=i: t.transpose(out=sc_ps[0:NE1, 0:128], in_=wte_sb[:], identity=s_iden[:]))
        ev = p.step("vector", [tp], lambda v, i=i: v.tensor_copy(
            out=wteT[:, 128 * i:128 * (i + 1)], in_=sc_ps[0:NE1, 0:128]))
        last[("wte", i)] = ev
    wcast = p.step("vector", [ev], lambda v: v.tensor_copy(out=wteTb[:], in_=wteT[:]))
    phA_done = wcast

    y_stores = []
    for hh in range(2):
        kl = []
        for h4 in range(4):
            h = hh * 4 + h4
            for e in range(NE1):
                deps = [last[("ymm", None)]] if (hh and e == 0 and h4 == 0) else NONE
                kl.append(p.dma("gpsimd", deps, lambda g, h=h, h4=h4, e=e: g.dma_start(
                    out=kf_sb[:, (h4 * NE1 + e) * 256:(h4 * NE1 + e) * 256 + 256],
                    in_=kfT[h, e])))
                kl.append(p.dma("gpsimd", deps, lambda g, h=h, h4=h4, e=e: g.dma_start(
                    out=vf_sb[:, (h4 * NE1 + e) * 256:(h4 * NE1 + e) * 256 + 256
                              ].rearrange("p (sc c) -> p sc c", sc=2),
                    in_=vf[h, e].rearrange("(sc p) c -> p sc c", p=128))))
        kv_done = kl[-1]

        for tc in range(NCH):
            kk = hh * NCH + tc
            deps = [const_done, phA_done]
            if kk >= 2:
                deps.append(last[("qmm", kk - 2)])
            xl = p.dma("sync", deps, lambda q, tc=tc, kk=kk: q.dma_start(
                out=xnb_sb[kk % 2].rearrange("p (g c) -> p g c", g=G),
                in_=xn_fmb.rearrange("(g p) t -> p g t", p=128)[:, :, 512 * tc:512 * (tc + 1)]))
            for og in range(G):
                bank = og % 2
                deps = [xl]
                if (kk, og) > (0, 1):
                    deps.append(last.get(("qev", bank)))
                def qmm(t, og=og, kk=kk, bank=bank):
                    for ig in range(G):
                        li = t.matmul(out=P[bank][:],
                                      lhsT=s_wmh[:, ig * D + 128 * og:ig * D + 128 * (og + 1)],
                                      rhs=xnb_sb[kk % 2][:, ig * 512:(ig + 1) * 512],
                                      start=(ig == 0), stop=(ig == G - 1))
                    return li
                qm = p.step("tensor", deps, qmm)
                last[("qmm", kk)] = qm
                qe = p.step("vector", [qm], lambda v, og=og, bank=bank: v.tensor_copy(
                    out=q_sb[:, og * 512:(og + 1) * 512], in_=P[bank][:]))
                last[("qev", bank)] = qe
            q_done = last[("qev", (G - 1) % 2)]

            wb_done = None
            for ep in range(NE1 // 2):
                deps2 = [phA_done]
                if ep >= 1:
                    deps2.append(last[("wbev", None)])
                elif kk >= 1:
                    deps2.append(last[("actm", None)])
                def wbmm2(t, tc=tc, ep=ep):
                    t.matmul(out=P[2][:], lhsT=s_esel[:, (2 * ep) * 128:(2 * ep + 1) * 128],
                             rhs=wteTb[:, 512 * tc:512 * (tc + 1)], start=True, stop=True)
                    return t.matmul(out=P[3][:], lhsT=s_esel[:, (2 * ep + 1) * 128:(2 * ep + 2) * 128],
                                    rhs=wteTb[:, 512 * tc:512 * (tc + 1)], start=True, stop=True)
                wm = p.step("tensor", deps2, wbmm2)
                def wbev(s, ep=ep):
                    s.copy(out=wb_sb[:, (2 * ep) * 512:(2 * ep + 1) * 512], in_=P[2][:])
                    return s.copy(out=wb_sb[:, (2 * ep + 1) * 512:(2 * ep + 2) * 512], in_=P[3][:])
                wb_done = p.step("scalar", [wm], wbev)
                last[("wbev", None)] = wb_done

            for h4 in range(4):
                h = hh * 4 + h4
                for e in range(NE1):
                    deps = [kv_done, q_done, wb_done]
                    if last.get(("gelu", None)) is not None:
                        deps.append(last[("gelu", None)])
                    def scmm(t, h4=h4, e=e, h=h):
                        base = (h4 * NE1 + e) * 256
                        for sc in range(2):
                            li = t.matmul(out=sc_ps[:, 512 * sc:512 * (sc + 1)],
                                          lhsT=kf_sb[:, base + 128 * sc:base + 128 * (sc + 1)],
                                          rhs=q_sb[:, h * 512:(h + 1) * 512],
                                          start=True, stop=True)
                        return li
                    sm = p.step("tensor", deps, scmm)
                    deps = [sm]
                    if last.get(("ymm", None)) is not None:
                        deps.append(last[("ymm", None)])
                    ge = p.step("scalar", deps, lambda s: s.activation(
                        out=act_sb[:], in_=sc_ps[:], func=AF.Gelu))
                    last[("gelu", None)] = ge
                    def actm(v, e=e):
                        v.tensor_mul(out=act_sb[:, 0:512], in0=act_sb[:, 0:512],
                                     in1=wb_sb[:, e * 512:(e + 1) * 512])
                        return v.tensor_mul(out=act_sb[:, 512:1024], in0=act_sb[:, 512:1024],
                                            in1=wb_sb[:, e * 512:(e + 1) * 512])
                    am = p.step("vector", [ge], actm)
                    last[("actm", None)] = am

                    deps = [am]
                    if e == 0 and last.get(("yev", None)) is not None:
                        deps.append(last[("yev", None)])
                    def ymm(t, h4=h4, e=e):
                        for sc in range(2):
                            li = t.matmul(out=P[4][:],
                                          lhsT=vf_sb[:, (h4 * NE1 + e) * 256 + 128 * sc:
                                                     (h4 * NE1 + e) * 256 + 128 * (sc + 1)],
                                          rhs=act_sb[:, 512 * sc:512 * (sc + 1)],
                                          start=(e == 0 and sc == 0), stop=(e == NE1 - 1 and sc == 1))
                        return li
                    ym = p.step("tensor", deps, ymm)
                    last[("ymm", None)] = ym
                ye = p.step("vector", [ym], lambda v, h4=h4: v.tensor_copy(
                    out=y_sb[:, h4 * 512:(h4 + 1) * 512], in_=P[4][:]))
                last[("yev", None)] = ye
            y_stores.append(p.dma("sync", [ye], lambda q, hh=hh, tc=tc: q.dma_start(
                out=y_d.rearrange("(g p) t -> p g t", p=128)[:, 4 * hh:4 * (hh + 1),
                                                            512 * tc:512 * (tc + 1)],
                in_=y_sb.rearrange("p (g c) -> p g c", g=4))))
    phB_done = y_stores[-1]

    st3 = []
    for tc in range(NCH):
        deps = [phB_done]
        if tc >= 2:
            deps.append(last[("gmm", tc - 2)])
        yl = p.dma("sync", deps, lambda q, tc=tc: q.dma_start(
            out=xnb_sb[tc % 2].rearrange("p (g c) -> p g c", g=G),
            in_=y_d.rearrange("(g p) t -> p g t", p=128)[:, :, 512 * tc:512 * (tc + 1)]))
        for tt in range(4):
            i = tc * 4 + tt
            deps = [yl]
            if i >= 1:
                deps.append(last[("gev", None)])
            def gmm(t, tc=tc, tt=tt):
                for od in range(2):
                    for ig in range(G):
                        li = t.matmul(out=P[od][:],
                                      lhsT=xnb_sb[tc % 2][:, ig * 512 + 128 * tt:ig * 512 + 128 * (tt + 1)],
                                      rhs=s_wmg[:, ig * D + 512 * od:ig * D + 512 * (od + 1)],
                                      start=(ig == 0), stop=(ig == G - 1))
                return li
            gm = p.step("tensor", deps, gmm)
            last[("gmm", tc)] = gm
            deps = [gm]
            if i >= 2:
                deps.append(st3[i - 2])
            gevs = p.step("scalar", deps, lambda s, i=i: (
                s.activation(out=go_sb[i % 2][:, 0:512], in_=P[0][:], func=AF.Silu),
                s.activation(out=go_sb[i % 2][:, 512:1024], in_=P[1][:], func=AF.Silu))[-1])
            last[("gev", None)] = gevs
            st3.append(p.dma("sync", [gevs], lambda q, i=i: q.dma_start(
                out=g1_tok[128 * i:128 * (i + 1), :], in_=go_sb[i % 2][:])))
    p.step("sync", [st3[-1]], lambda q: q.nop())
    return p.build()


_IDEN = np.eye(128, dtype=np.float32)
_PROGS = {}


def _prog(name, builder):
    if name not in _PROGS:
        _PROGS[name] = builder()
    return _PROGS[name]


def _run(name, builder, maps):
    nc = _prog(name, builder)
    return run_bass_kernel_spmd(nc, maps, core_ids=list(range(len(maps)))).results


def _f32c(a):
    return np.ascontiguousarray(np.asarray(a, dtype=np.float32))


def kernel(**inp):
    xs = _f32c(inp["L0_attn_x"])
    NCORE = xs.shape[0]

    a0 = {}
    for nm, key in [("WpT", "L0_attn_Wp"), ("WqT", "L0_attn_Wq"), ("WkT", "L0_attn_Wk"),
                    ("WvT", "L0_attn_Wv"), ("WoT", "L0_attn_Wo")]:
        a0[nm] = _f32c(np.asarray(inp[key]).T)
    for nm, key in [("bpc", "L0_attn_bp"), ("bqc", "L0_attn_bq"), ("bkc", "L0_attn_bk"),
                    ("bvc", "L0_attn_bv")]:
        a0[nm] = _f32c(_f32c(inp[key]).reshape(8, 128).T)
    a0["bo_row"] = _f32c(inp["L0_attn_bo"]).reshape(1, 1024)
    wsel = np.zeros((8, 128, 16), np.float32)
    wbc = np.zeros((8, 16, 128), np.float32)
    for g in range(8):
        for pp in range(128):
            h = 2 * g + pp // 64
            wsel[g, pp, h] = 1.0
            wbc[g, h, pp] = 1.0
    a0["wsel"] = wsel
    a0["wbc"] = wbc
    a0["ones1"] = np.ones((1, 128), np.float32)

    cen = _f32c(inp["L0_ffn_centroids"])
    lng = _f32c(inp["L0_ffn_ln_g"])
    lnb = _f32c(inp["L0_ffn_ln_b"])
    w1 = _f32c(inp["L0_ffn_w1"])
    b1 = _f32c(inp["L0_ffn_b1"])
    w2 = _f32c(inp["L0_ffn_w2"])
    b2 = _f32c(inp["L0_ffn_b2"])
    f0_base = {"cenT": _f32c(cen.T), "ones1": a0["ones1"]}
    f0_e = []
    for e in range(8):
        f0_e.append({
            "lg": _f32c(lng[e].reshape(8, 128).T),
            "lb": _f32c(lnb[e].reshape(8, 128).T),
            "b1c": _f32c(b1[e].reshape(32, 128).T),
            "b2_row": b2[e].reshape(1, 1024).copy(),
            "w1Te": _f32c(w1[e].T),
            "w2Te": _f32c(w2[e].T),
            "econst": np.full((128, 1), float(e), np.float32),
        })

    a1 = {
        "WrvkT": _f32c(np.asarray(inp["L1_attn_W_rvk"]).T),
        "WoutT": _f32c(np.asarray(inp["L1_attn_W_out"]).T),
        "tdc": _f32c(_f32c(inp["L1_attn_time_decay"]).reshape(8, 128).T),
        "tfc": _f32c(_f32c(inp["L1_attn_time_first"]).reshape(8, 128).T),
    }

    esel = np.zeros((16, 16 * 128), np.float32)
    for e in range(16):
        esel[e, e * 128:(e + 1) * 128] = 1.0
    f1 = {
        "wrT": _f32c(np.asarray(inp["L1_ffn_w_router"]).T),
        "wmhT": _f32c(np.asarray(inp["L1_ffn_w_mh"]).T),
        "wmgT": _f32c(np.asarray(inp["L1_ffn_w_mg"]).T),
        "kfT": _f32c(np.asarray(inp["L1_ffn_k_ffwd"]).transpose(0, 1, 3, 2)),
        "vf": _f32c(inp["L1_ffn_v_ffwd"]),
        "iota16": np.broadcast_to(np.arange(16, dtype=np.float32), (128, 16)).copy(),
        "esel": esel,
        "iden": _IDEN,
    }

    r = _run("ln0", lambda: build_ln(with_xhat=True),
             [{"x": np.ascontiguousarray(xs[c]), "iden": _IDEN} for c in range(NCORE)])
    xn_fmb = [r[c]["xn_fmb"] for c in range(NCORE)]
    xn_fm32 = [r[c]["xn_fm32"] for c in range(NCORE)]
    xn_tok = [r[c]["xn_tok"] for c in range(NCORE)]
    xh_tok = [r[c]["xh_tok"] for c in range(NCORE)]

    r = _run("attn0", build_attn0,
             [dict(a0, xn_fmb=xn_fmb[c]) for c in range(NCORE)])
    s_tok = [r[c]["s_tok"] for c in range(NCORE)]

    stril = np.zeros((128, 128), np.float32)
    for _p in range(128):
        stril[_p, _p + 1:] = 1.0
    rt = {"cenT": f0_base["cenT"], "strilT": stril,
          "iota8": np.broadcast_to(np.arange(8, dtype=np.float32), (128, 8)).copy()}
    r = _run("route0", build_route0,
             [dict(rt, xn_fm32=xn_fm32[c], xh_tok=xh_tok[c]) for c in range(NCORE)])
    xg = [r[c]["xg"] for c in range(NCORE)]
    slot_d = [r[c]["slot_d"] for c in range(NCORE)]
    a_d = [r[c]["a_d"] for c in range(NCORE)]

    lgt = np.zeros((128, 64), np.float32)
    lbt = np.zeros((128, 64), np.float32)
    b1t = np.zeros((128, 256), np.float32)
    for e in range(8):
        lgt[:, e * 8:(e + 1) * 8] = lng[e].reshape(8, 128).T
        lbt[:, e * 8:(e + 1) * 8] = lnb[e].reshape(8, 128).T
        b1t[:, e * 32:(e + 1) * 32] = b1[e].reshape(32, 128).T
    fs = {"w1T": _f32c(w1.transpose(0, 2, 1)), "w2T": _f32c(w2.transpose(0, 2, 1)),
          "lgt": lgt, "lbt": lbt, "b1t": b1t, "b2t": b2,
          "ones1": a0["ones1"], "idenb": _IDEN.astype(NBF)}
    r = _run("ffn0s", build_ffn0s, [dict(fs, xg=xg[c]) for c in range(NCORE)])
    z_slots = [r[c]["z_slots"] for c in range(NCORE)]

    r = _run("comb0s", build_comb0s,
             [{"z_slots": z_slots[c], "slot_d": slot_d[c], "a_d": a_d[c],
               "xn_tok": xn_tok[c], "s_tok": s_tok[c],
               "x": np.ascontiguousarray(xs[c])} for c in range(NCORE)])
    x1 = [r[c]["out"] for c in range(NCORE)]

    r = _run("ln1", lambda: build_ln(with_xhat=False),
             [{"x": x1[c], "iden": _IDEN} for c in range(NCORE)])
    xn1_fmb = [r[c]["xn_fmb"] for c in range(NCORE)]
    xn1_fm32 = [r[c]["xn_fm32"] for c in range(NCORE)]

    r = _run("attn1", build_attn1,
             [dict(a1, xn_fmb=xn1_fmb[c]) for c in range(NCORE)])
    s1_tok = [r[c]["s1_tok"] for c in range(NCORE)]

    r = _run("ffn1", build_ffn1,
             [dict(f1, xn_fmb=xn1_fmb[c], xn_fm32=xn1_fm32[c]) for c in range(NCORE)])
    g1_tok = [r[c]["g1_tok"] for c in range(NCORE)]

    r = _run("comb1", lambda: build_combine("final"),
             [{"g1_tok": g1_tok[c], "s_tok": s1_tok[c], "x": x1[c]} for c in range(NCORE)])
    out = np.stack([r[c]["out"] for c in range(NCORE)]).astype(np.float32)
    return out


CAP = 768
NST = 8 * CAP // 128
BIGIDX = 1 << 20


def build_route0():
    nc = bass.Bass("TRN2", target_bir_lowering=False, detect_race_conditions=False)
    xn_fm32 = nc.dram_tensor("xn_fm32", [D, T], F32, kind="ExternalInput").ap()
    xh_tok = nc.dram_tensor("xh_tok", [T, D], BF16, kind="ExternalInput").ap()
    cenT = nc.dram_tensor("cenT", [D, G], F32, kind="ExternalInput").ap()
    strilT = nc.dram_tensor("strilT", [128, 128], F32, kind="ExternalInput").ap()
    iota8 = nc.dram_tensor("iota8", [128, G], F32, kind="ExternalInput").ap()
    xg = nc.dram_tensor("xg", [8 * CAP, D], BF16, kind="ExternalOutput").ap()
    slot_d = nc.dram_tensor("slot_d", [128, NT], U32, kind="ExternalOutput").ap()
    a_d = nc.dram_tensor("a_d", [128, NT], F32, kind="ExternalOutput").ap()
    dbg_m = nc.dram_tensor("dbg_m", [128, G * NT], F32, kind="ExternalOutput").ap()
    dbg_c = nc.dram_tensor("dbg_c", [128, 2 * G], F32, kind="ExternalOutput").ap()

    s_cen = nc.alloc_sbuf_tensor("s_cen", [128, G * G], F32)
    s_stril = nc.alloc_sbuf_tensor("s_stril", [128, 128], F32)
    s_io8 = nc.alloc_sbuf_tensor("s_io8", [128, G], F32)
    xn32 = [nc.alloc_sbuf_tensor(f"xn32_{i}", [128, D], F32) for i in range(2)]
    lg_sb = nc.alloc_sbuf_tensor("lg_sb", [128, G], F32)
    sa_sb = nc.alloc_sbuf_tensor("sa_sb", [128, G], F32)
    mx_sb = nc.alloc_sbuf_tensor("mx_sb", [128, 8], F32)
    mr_sb = nc.alloc_sbuf_tensor("mr_sb", [128, G], F32)
    am_sb = nc.alloc_sbuf_tensor("am_sb", [128, G], F32)
    a_all = nc.alloc_sbuf_tensor("a_all", [128, NT], F32)
    mall = nc.alloc_sbuf_tensor("mall", [128, G * NT], F32)
    ones32 = nc.alloc_sbuf_tensor("ones32", [128, NT], F32)
    incl = nc.alloc_sbuf_tensor("incl", [128, G * NT], F32)
    cnt_sb = nc.alloc_sbuf_tensor("cnt_sb", [128, G], F32)
    offs_sb = nc.alloc_sbuf_tensor("offs_sb", [128, G], F32)
    rank = nc.alloc_sbuf_tensor("rank", [128, NT], F32)
    slot_f = nc.alloc_sbuf_tensor("slot_f", [128, NT], F32)
    slot_u = nc.alloc_sbuf_tensor("slot_u", [128, NT], U32)
    tmp_sb = nc.alloc_sbuf_tensor("tmp_sb", [128, NT], F32)
    xh_sb = [nc.alloc_sbuf_tensor(f"xh_sb{i}", [128, D], BF16) for i in range(2)]
    idxc = [nc.alloc_sbuf_tensor(f"idxc{i}", [128, 1], U32) for i in range(2)]
    Pq = nc.alloc_psum_tensor("Pq", [128, G], F32)
    Po = nc.alloc_psum_tensor("Po", [128, G], F32)

    p = Prog(nc)
    NONE = []
    cl = []
    cl.append(p.dma("gpsimd", NONE, lambda g: g.dma_start(
        out=s_cen.rearrange("p (i c) -> p i c", i=G), in_=cenT.rearrange("(i p) c -> p i c", p=128))))
    cl.append(p.dma("gpsimd", NONE, lambda g: g.dma_start(out=s_stril[:], in_=strilT)))
    cl.append(p.dma("gpsimd", NONE, lambda g: g.dma_start(out=s_io8[:], in_=iota8)))
    const_done = cl[-1]

    last = {}
    for i in range(NT):
        deps = [cl[0]]
        if i >= 2:
            deps.append(last[("aff", i - 2)])
        ld = p.dma("sync", deps, lambda q, i=i: q.dma_start(
            out=xn32[i % 2].rearrange("p (g c) -> p g c", g=G),
            in_=xn_fm32.rearrange("(g p) t -> p g t", p=128)[:, :, 128 * i:128 * (i + 1)]))
        def affmm(t, i=i):
            for ig in range(G):
                li = t.matmul(out=Pq[:], lhsT=xn32[i % 2][:, ig * 128:(ig + 1) * 128],
                              rhs=s_cen[:, ig * G:(ig + 1) * G], start=(ig == 0), stop=(ig == G - 1))
            return li
        deps = [ld]
        if i >= 1:
            deps.append(last[("lcp", i - 1)])
        aff = p.step("tensor", deps, affmm)
        last[("aff", i)] = aff
        lcp = p.step("scalar", [aff], lambda s: s.copy(out=lg_sb[:], in_=Pq[:]))
        last[("lcp", i)] = lcp
        sga = p.step("scalar", [lcp], lambda s: s.activation(out=sa_sb[:], in_=lg_sb[:], func=AF.Sigmoid))
        mx = p.step("vector", [sga] + ([last[("eid", i - 1)]] if i else []),
                    lambda v: v.max(out=mx_sb[:], in_=sa_sb[:]))
        ms0 = p.step("vector", [mx], lambda v: v.memset(mx_sb[:, 1:8], -1.0))
        mr = p.step("vector", [ms0], lambda v: v.match_replace(
            out=mr_sb[:], in_to_replace=mx_sb[:], in_values=sa_sb[:], imm_value=0.0))
        am = p.step("vector", [mr], lambda v, i=i: v.tensor_sub(out=am_sb[:], in0=sa_sb[:], in1=mr_sb[:]))
        a1 = p.step("vector", [am], lambda v, i=i: v.tensor_reduce(
            out=a_all[:, i:i + 1], in_=am_sb[:], op=OP.add, axis=AX.X))
        mk = p.step("vector", [a1], lambda v, i=i: v.tensor_scalar(
            out=mall[:, i:G * NT:NT], in0=am_sb[:], scalar1=0.0, scalar2=None, op0=OP.is_gt))
        eid = p.step("vector", [mk, const_done], lambda v, i=i: v.tensor_mul(
            out=mr_sb[:], in0=mall[:, i:G * NT:NT], in1=s_io8[:]))
        last[("eid", i)] = eid
    routing_done = last[("eid", NT - 1)]

    mo = p.step("vector", [routing_done], lambda v: v.memset(ones32[:], 1.0))
    scans = []
    for e in range(G):
        scans.append(p.step("vector", [mo], lambda v, e=e: v.tensor_tensor_scan(
            out=incl[:, e * NT:(e + 1) * NT], data0=ones32[:], data1=mall[:, e * NT:(e + 1) * NT],
            initial=0.0, op0=OP.mult, op1=OP.add)))
    cntc = p.step("vector", [scans[-1]], lambda v: v.tensor_copy(
        out=cnt_sb[:], in_=incl[:, NT - 1:G * NT:NT]))
    offm = p.step("tensor", [cntc, const_done], lambda t: t.matmul(
        out=Po[:], lhsT=s_stril[:], rhs=cnt_sb[:], start=True, stop=True))
    offe = p.step("scalar", [offm], lambda s: s.copy(out=offs_sb[:], in_=Po[:]))
    sl0 = p.step("vector", [offe], lambda v: v.memset(slot_f[:], 0.0))
    prev = sl0
    for e in range(G):
        s_a1 = p.step("vector", [prev], lambda v, e=e: v.scalar_tensor_tensor(
            out=rank[:], in0=incl[:, e * NT:(e + 1) * NT],
            scalar=offs_sb[:, e:e + 1], op0=OP.add, op1=OP.subtract,
            in1=mall[:, e * NT:(e + 1) * NT]))
        s_a2 = p.step("vector", [s_a1], lambda v, e=e: v.tensor_scalar_add(
            out=tmp_sb[:], in0=rank[:], scalar1=float(e * CAP)))
        s_a3 = p.step("vector", [s_a2], lambda v, e=e: v.tensor_scalar(
            out=ones32[:], in0=rank[:], scalar1=float(CAP),
            scalar2=float(BIGIDX), op0=OP.is_ge, op1=OP.mult))
        s_a4 = p.step("vector", [s_a3], lambda v, e=e: v.tensor_add(
            out=tmp_sb[:], in0=tmp_sb[:], in1=ones32[:]))
        s_a5 = p.step("vector", [s_a4], lambda v, e=e: v.tensor_mul(
            out=tmp_sb[:], in0=tmp_sb[:], in1=mall[:, e * NT:(e + 1) * NT]))
        prev = p.step("vector", [s_a5], lambda v, e=e: v.tensor_add(
            out=slot_f[:], in0=slot_f[:], in1=tmp_sb[:]))
    slu = p.step("vector", [prev], lambda v: v.tensor_copy(out=slot_u[:], in_=slot_f[:]))

    regcell = {}
    def mkreg(g):
        regcell["bc"] = g.to_reg(8 * CAP - 1)
        return g.dma_start(out=xh_sb[0][:], in_=xh_tok[0:128, :])
    first_ld = p.dma("gpsimd", [slu], mkreg)

    sc_prev = None
    for i in range(NT):
        if i == 0:
            xl = first_ld
        else:
            deps = [slu]
            if i >= 2:
                deps.append(last[("sct", i - 2)])
            xl = p.dma("gpsimd", deps, lambda g, i=i: g.dma_start(
                out=xh_sb[i % 2][:], in_=xh_tok[128 * i:128 * (i + 1), :]))
        cpi = p.step("vector", [xl, slu], lambda v, i=i: v.tensor_copy(
            out=idxc[i % 2][:], in_=slot_u[:, i:i + 1]))
        s1 = p.dma("gpsimd", [cpi], lambda g, i=i: g.indirect_dma_start(
            out=xg, out_offset=IndirectOffsetOnAxis(ap=idxc[i % 2][:], axis=0),
            in_=xh_sb[i % 2][:], in_offset=None,
            bounds_check=regcell["bc"], oob_is_err=False))
        last[("sct", i)] = s1
        sc_prev = s1

    ast = p.dma("sync", [routing_done], lambda q: q.dma_start(out=a_d, in_=a_all[:]))
    ast2 = p.dma("sync", [slu], lambda q: q.dma_start(out=slot_d, in_=slot_u[:]))
    d1 = p.dma("sync", [slu], lambda q: q.dma_start(out=dbg_m, in_=incl[:]))
    dcc = p.step("vector", [slu], lambda v: (v.tensor_copy(out=rank[:, 0:G], in_=cnt_sb[:]),
                                             v.tensor_copy(out=rank[:, G:2 * G], in_=offs_sb[:]))[-1])
    d2 = p.dma("sync", [dcc], lambda q: q.dma_start(out=dbg_c, in_=rank[:, 0:2 * G]))
    p.step("sync", [sc_prev, ast, ast2, d1, d2], lambda q: q.nop())
    return p.build()


def build_ffn0s():
    nc = bass.Bass("TRN2", target_bir_lowering=False, detect_race_conditions=False)
    xg = nc.dram_tensor("xg", [8 * CAP, D], BF16, kind="ExternalInput").ap()
    w1T = nc.dram_tensor("w1T", [G, D, 4096], F32, kind="ExternalInput").ap()
    w2T = nc.dram_tensor("w2T", [G, 4096, D], F32, kind="ExternalInput").ap()
    lgt = nc.dram_tensor("lgt", [128, G * G], F32, kind="ExternalInput").ap()
    lbt = nc.dram_tensor("lbt", [128, G * G], F32, kind="ExternalInput").ap()
    b1t = nc.dram_tensor("b1t", [128, G * 32], F32, kind="ExternalInput").ap()
    b2t = nc.dram_tensor("b2t", [G, D], F32, kind="ExternalInput").ap()
    ones1 = nc.dram_tensor("ones1", [1, 128], F32, kind="ExternalInput").ap()
    idenb = nc.dram_tensor("idenb", [128, 128], BF16, kind="ExternalInput").ap()
    z_slots = nc.dram_tensor("z_slots", [8 * CAP, D], BF16, kind="ExternalOutput").ap()

    s_lg = nc.alloc_sbuf_tensor("s_lg", [128, G * G], F32)
    s_lb = nc.alloc_sbuf_tensor("s_lb", [128, G * G], F32)
    s_b1 = nc.alloc_sbuf_tensor("s_b1", [128, G * 32], F32)
    s_b2 = nc.alloc_sbuf_tensor("s_b2", [1, G * D], BF16)
    s_ones = nc.alloc_sbuf_tensor("s_ones", [1, 128], BF16)
    s_idb = nc.alloc_sbuf_tensor("s_idb", [128, 128], BF16)
    w1_sb = nc.alloc_sbuf_tensor("w1_sb", [128, G * F_HALF], BF16)
    w2_sb = nc.alloc_sbuf_tensor("w2_sb", [128, 32 * D], BF16)
    xgt_sb = [nc.alloc_sbuf_tensor(f"xgt{i}", [128, D], BF16) for i in range(2)]
    xe_sb = nc.alloc_sbuf_tensor("xe_sb", [128, G * CAP], BF16)
    h_sb = nc.alloc_sbuf_tensor("h_sb", [128, 32 * CAP], BF16)
    z_sb = [nc.alloc_sbuf_tensor(f"z_sb{i}", [128, D], BF16) for i in range(2)]
    Ptp = [nc.alloc_psum_tensor(f"Ptp{i}", [128, 128], BF16) for i in range(2)]
    P = [nc.alloc_psum_tensor(f"Pz{i}", [128, 512], F32) for i in range(4)]

    p = Prog(nc)
    NONE = []
    cl = []
    for sb, dr in [(s_lg, lgt), (s_lb, lbt), (s_b1, b1t), (s_ones, ones1), (s_idb, idenb)]:
        cl.append(p.dma("gpsimd", NONE, lambda g, sb=sb, dr=dr: g.dma_start(out=sb[:], in_=dr)))
    cl.append(p.dma("gpsimd", NONE, lambda g: g.dma_start(
        out=s_b2.rearrange("p (e c) -> p e c", e=G), in_=b2t.rearrange("e c -> () e c"))))
    const_done = cl[-1]

    last = {}
    CHU = [(0, 512), (512, 256)]
    st_all = []
    for e in range(G):
        w2l = []
        for fk in range(32):
            w2l.append(p.dma("gpsimd", [last[("zmm", e - 1)]] if e else NONE,
                             lambda g, fk=fk, e=e: g.dma_start(
                out=w2_sb[:, fk * D:(fk + 1) * D], in_=w2T[e, 128 * fk:128 * (fk + 1), :])))
        w2_done = w2l[-1]

        tps = []
        for st in range(6):
            k = e * 6 + st
            deps = [const_done]
            if k >= 2:
                deps.append(last[("xemm", k - 2)])
            xl = p.dma("sync", deps, lambda q, e=e, st=st, k=k: q.dma_start(
                out=xgt_sb[k % 2][:], in_=xg[e * CAP + 128 * st:e * CAP + 128 * (st + 1), :]))
            for g_ in range(G):
                kk = k * G + g_
                deps = [xl, const_done]
                if kk >= 2:
                    deps.append(last[("xev", kk - 2)])
                tp = p.step("tensor", deps, lambda t, k=k, g_=g_, kk=kk: t.transpose(
                    out=Ptp[kk % 2][:], in_=xgt_sb[k % 2][:, 128 * g_:128 * (g_ + 1)],
                    identity=s_idb[:]))
                last[("xemm", k)] = tp
                ev = p.step("vector", [tp], lambda v, e=e, st=st, g_=g_, kk=kk: v.tensor_scalar(
                    out=xe_sb[:, g_ * CAP + 128 * st:g_ * CAP + 128 * (st + 1)], in0=Ptp[kk % 2][:],
                    scalar1=s_lg[:, e * G + g_:e * G + g_ + 1],
                    scalar2=s_lb[:, e * G + g_:e * G + g_ + 1], op0=OP.mult, op1=OP.add))
                last[("xev", kk)] = ev
                tps.append(ev)
        xe_done = tps[-1]

        hlast = None
        for fh in range(2):
            w1l = []
            for ig in range(G):
                deps = [last[("hmm", e, fh - 1)]] if fh else ([last[("hmm", e - 1, 1)]] if e else NONE)
                w1l.append(p.dma("gpsimd", deps, lambda g, ig=ig, fh=fh, e=e: g.dma_start(
                    out=w1_sb[:, ig * F_HALF:(ig + 1) * F_HALF],
                    in_=w1T[e, 128 * ig:128 * (ig + 1), F_HALF * fh:F_HALF * (fh + 1)])))
            w1_done = w1l[-1]
            for fg in range(NFG):
                fga = fh * NFG + fg
                for ci, (c0, cw) in enumerate(CHU):
                    bank = (fg * 2 + ci) % 2
                    deps = [xe_done, w1_done]
                    if (fh, fg, ci) != (0, 0, 0):
                        deps.append(last.get(("hev", bank)))
                    elif e > 0:
                        deps.append(last[("zev", None)])
                    def hmm(t, fg=fg, c0=c0, cw=cw, bank=bank):
                        for ig in range(G):
                            li = t.matmul(out=P[bank][:, 0:cw],
                                          lhsT=w1_sb[:, ig * F_HALF + 128 * fg:ig * F_HALF + 128 * (fg + 1)],
                                          rhs=xe_sb[:, ig * CAP + c0:ig * CAP + c0 + cw],
                                          start=(ig == 0), stop=(ig == G - 1))
                        return li
                    hm = p.step("tensor", deps, hmm)
                    last[("hmm", e, fh)] = hm
                    he = p.step("scalar", [hm], lambda s, fga=fga, c0=c0, cw=cw, bank=bank, e=e: s.activation(
                        out=h_sb[:, fga * CAP + c0:fga * CAP + c0 + cw], in_=P[bank][:, 0:cw],
                        func=AF.Relu, bias=s_b1[:, e * 32 + fga:e * 32 + fga + 1]))
                    last[("hev", bank)] = he
                    hlast = he

        for st in range(6):
            i = e * 6 + st
            deps = [hlast, w2_done]
            if st >= 1 or e > 0:
                deps.append(last[("zev", None)])
            def zmm(t, st=st, e=e):
                for od in range(2):
                    for fk in range(32):
                        t.matmul(out=P[2 + od][:],
                                 lhsT=h_sb[:, fk * CAP + 128 * st:fk * CAP + 128 * (st + 1)],
                                 rhs=w2_sb[:, fk * D + 512 * od:fk * D + 512 * (od + 1)],
                                 start=(fk == 0), stop=False)
                    li = t.matmul(out=P[2 + od][:], lhsT=s_ones[:],
                                  rhs=s_b2[:, e * D + 512 * od:e * D + 512 * (od + 1)],
                                  start=False, stop=True)
                return li
            zm = p.step("tensor", deps, zmm)
            last[("zmm", e)] = zm
            deps = [zm]
            if len(st_all) >= 2:
                deps.append(st_all[-2])
            ze = p.step("vector", deps, lambda v, i=i: (
                v.tensor_copy(out=z_sb[i % 2][:, 0:512], in_=P[2][:]),
                v.tensor_copy(out=z_sb[i % 2][:, 512:1024], in_=P[3][:]))[-1])
            last[("zev", None)] = ze
            st_all.append(p.dma("sync", [ze], lambda q, e=e, st=st, i=i: q.dma_start(
                out=z_slots[e * CAP + 128 * st:e * CAP + 128 * (st + 1), :], in_=z_sb[i % 2][:])))
    p.step("sync", [st_all[-1]], lambda q: q.nop())
    return p.build()


def build_comb0s():
    nc = bass.Bass("TRN2", target_bir_lowering=False, detect_race_conditions=False)
    z_slots = nc.dram_tensor("z_slots", [8 * CAP, D], BF16, kind="ExternalInput").ap()
    slot_d = nc.dram_tensor("slot_d", [128, NT], U32, kind="ExternalInput").ap()
    a_d = nc.dram_tensor("a_d", [128, NT], F32, kind="ExternalInput").ap()
    xn_tok = nc.dram_tensor("xn_tok", [T, D], F32, kind="ExternalInput").ap()
    s_tok = nc.dram_tensor("s_tok", [T, D], F32, kind="ExternalInput").ap()
    x = nc.dram_tensor("x", [T, D], F32, kind="ExternalInput").ap()
    out = nc.dram_tensor("out", [T, D], F32, kind="ExternalOutput").ap()

    s_slot = nc.alloc_sbuf_tensor("s_slot", [128, NT], U32)
    s_a = nc.alloc_sbuf_tensor("s_a", [128, NT], F32)
    idxc = [nc.alloc_sbuf_tensor(f"idxc{i}", [128, 1], U32) for i in range(2)]
    zb = [nc.alloc_sbuf_tensor(f"zb{i}", [128, D], BF16) for i in range(2)]
    xnb = [nc.alloc_sbuf_tensor(f"xnb{i}", [128, D], F32) for i in range(2)]
    sb_ = [nc.alloc_sbuf_tensor(f"sb_{i}", [128, D], F32) for i in range(2)]
    xb = [nc.alloc_sbuf_tensor(f"xb{i}", [128, D], F32) for i in range(2)]
    gb = [nc.alloc_sbuf_tensor(f"gb{i}", [128, D], F32) for i in range(2)]

    p = Prog(nc)
    NONE = []
    c1 = p.dma("sync", NONE, lambda q: q.dma_start(out=s_slot[:], in_=slot_d))
    c2 = p.dma("sync", NONE, lambda q: q.dma_start(out=s_a[:], in_=a_d))
    regcell = {}
    rg = p.dma("gpsimd", NONE, lambda g: (regcell.__setitem__("bc", g.to_reg(8 * CAP - 1)),
                                          g.dma_start(out=idxc[0][:], in_=slot_d[:, 0:1],
                                                      allow_slow_non_contiguous=True))[-1])
    last = {}
    st = []
    for i in range(NT):
        deps = [c1]
        if i >= 2:
            deps.append(last[("g", i - 2)])
        cp = p.step("vector", deps + [c2], lambda v, i=i: v.tensor_copy(
            out=idxc[i % 2][:], in_=s_slot[:, i:i + 1]))
        mz = p.step("vector", [cp], lambda v, i=i: v.memset(zb[i % 2][:], 0.0))
        gz = p.dma("gpsimd", [mz, rg], lambda g, i=i: g.indirect_dma_start(
            out=zb[i % 2][:], out_offset=None, in_=z_slots,
            in_offset=IndirectOffsetOnAxis(ap=idxc[i % 2][:], axis=0),
            bounds_check=regcell["bc"], oob_is_err=False))
        lds = []
        for nm, buf in [("xn", xnb), ("s", sb_), ("x", xb)]:
            dd = [st[i - 2]] if i >= 2 else []
            src_ = {"xn": xn_tok, "s": s_tok, "x": x}[nm]
            lds.append(p.dma("sync", dd, lambda q, i=i, src_=src_, buf=buf: q.dma_start(
                out=buf[i % 2][:], in_=src_[128 * i:128 * (i + 1), :])))
        def comb(v, i=i):
            v.scalar_tensor_tensor(out=gb[i % 2][:], in0=zb[i % 2][:], scalar=s_a[:, i:i + 1],
                                   op0=OP.mult, op1=OP.add, in1=xnb[i % 2][:])
            return v.engine_nop()
        cmb = p.step("vector", [gz] + lds, lambda v, i=i: v.scalar_tensor_tensor(
            out=gb[i % 2][:], in0=zb[i % 2][:], scalar=s_a[:, i:i + 1],
            op0=OP.mult, op1=OP.add, in1=xnb[i % 2][:]))
        last[("g", i)] = cmb
        sil = p.step("scalar", [cmb], lambda s, i=i: s.activation(
            out=gb[i % 2][:], in_=gb[i % 2][:], func=AF.Silu))
        fin = p.step("vector", [sil], lambda v, i=i: (
            v.tensor_mul(out=gb[i % 2][:], in0=gb[i % 2][:], in1=sb_[i % 2][:]),
            v.tensor_add(out=gb[i % 2][:], in0=gb[i % 2][:], in1=xb[i % 2][:]))[-1])
        st.append(p.dma("sync", [fin], lambda q, i=i: q.dma_start(
            out=out[128 * i:128 * (i + 1), :], in_=gb[i % 2][:])))
    p.step("sync", [st[-1]], lambda q: q.nop())
    return p.build()




# version 04
# speedup vs baseline: 1.7697x; 1.0097x over previous
import os
import numpy as np
import ml_dtypes
import concourse.bass as bass
import concourse.mybir as mybir
from concourse.bass_utils import run_bass_kernel_spmd
from bass_rust import IndirectOffsetOnAxis

F32 = mybir.dt.float32
BF16 = mybir.dt.bfloat16
U32 = mybir.dt.uint32
AF = mybir.ActivationFunctionType
OP = mybir.AluOpType
AX = mybir.AxisListType
NBF = ml_dtypes.bfloat16

T = 4096
D = 1024
G = 8
NT = T // 128
NCH = T // 512
EPS = 1e-5


class Cnt:
    def __init__(self, nc, name):
        self.h = nc.alloc_semaphore(name)
        self.v = 0

    def inc(self, instr, n=1):
        instr.then_inc(self.h, n)
        self.v += n


def build_ln(with_xhat):
    nc = bass.Bass("TRN2", target_bir_lowering=False, detect_race_conditions=False)
    x = nc.dram_tensor("x", [T, D], F32, kind="ExternalInput").ap()
    iden = nc.dram_tensor("iden", [128, 128], F32, kind="ExternalInput").ap()
    xn_fm32 = nc.dram_tensor("xn_fm32", [D, T], F32, kind="ExternalOutput").ap()
    xn_fmb = nc.dram_tensor("xn_fmb", [D, T], BF16, kind="ExternalOutput").ap()
    xn_tok = nc.dram_tensor("xn_tok", [T, D], F32, kind="ExternalOutput").ap()
    if with_xhat:
        xh_fmb = nc.dram_tensor("xh_fmb", [D, T], BF16, kind="ExternalOutput").ap()
        xh_tok = nc.dram_tensor("xh_tok", [T, D], BF16, kind="ExternalOutput").ap()

    sx = [nc.alloc_sbuf_tensor(f"sx{i}", [128, D], F32) for i in range(2)]
    s_sq = nc.alloc_sbuf_tensor("s_sq", [128, D], F32)
    smc = nc.alloc_sbuf_tensor("smc", [128, NT], F32)
    ssc = nc.alloc_sbuf_tensor("ssc", [128, NT], F32)
    mean = nc.alloc_sbuf_tensor("mean", [128, NT], F32)
    rstd = nc.alloc_sbuf_tensor("rstd", [128, NT], F32)
    cc = nc.alloc_sbuf_tensor("cc", [128, NT], F32)
    tmp = [nc.alloc_sbuf_tensor(f"tmp{i}", [128, NT], F32) for i in range(4)]
    s_iden = nc.alloc_sbuf_tensor("s_iden", [128, 128], F32)
    s_idenb = nc.alloc_sbuf_tensor("s_idenb", [128, 128], BF16)
    sxn = [nc.alloc_sbuf_tensor(f"sxn{i}", [128, D], F32) for i in range(2)]
    sxh = [nc.alloc_sbuf_tensor(f"sxh{i}", [128, D], BF16) for i in range(2)]
    tf32 = [nc.alloc_sbuf_tensor(f"tf32_{i}", [128, D], F32) for i in range(2)]
    tfb = [nc.alloc_sbuf_tensor(f"tfb_{i}", [128, D], BF16) for i in range(2)]
    thb = [nc.alloc_sbuf_tensor(f"thb_{i}", [128, D], BF16) for i in range(2)]
    pt = [nc.alloc_psum_tensor(f"pt{i}", [128, 128], F32) for i in range(2)]
    ptb = [nc.alloc_psum_tensor(f"ptb{i}", [128, 128], BF16) for i in range(2)]

    eps_col = nc.alloc_sbuf_tensor("eps_col", [128, 1], F32)
    eps2_col = nc.alloc_sbuf_tensor("eps2_col", [128, 1], F32)
    ld = Cnt(nc, "ld"); su = Cnt(nc, "su"); sv = Cnt(nc, "sv")
    ld2 = Cnt(nc, "ld2"); nrm = Cnt(nc, "nrm")
    tp = Cnt(nc, "tp"); tph = Cnt(nc, "tph"); ev = Cnt(nc, "ev"); evh = Cnt(nc, "evh")
    st = Cnt(nc, "st"); stok = Cnt(nc, "stok")
    c1 = Cnt(nc, "c1"); c2 = Cnt(nc, "c2")

    _all_sems = [ld, su, sv, ld2, nrm, tp, tph, ev, evh, st, stok, c1, c2]
    with nc.Block() as blk0:
        @blk0.sync
        def _(q):
            for _s in _all_sems:
                q.sem_clear(_s.h)

    with nc.Block() as blk:
        @blk.sync
        def _(q):
            q.dma_start(out=s_iden[:], in_=iden).then_inc(ld.h, 16)
            for i in range(NT):
                if i >= 2:
                    q.wait_ge(su.h, i - 1)
                    q.wait_ge(sv.h, i - 1)
                q.dma_start(out=sx[i % 2][:], in_=x[128 * i:128 * (i + 1), :]).then_inc(ld.h, 16)
        ld.v = 16 * (NT + 1)

        @blk.scalar
        def _(s):
            for i in range(NT):
                s.wait_ge(ld.h, 16 * (i + 2))
                s.activation(out=s_sq[:], in_=sx[i % 2][:], func=AF.Square,
                             accum_out=ssc[:, i:i + 1]).then_inc(su.h, 1)
        su.v = NT

        @blk.vector
        def _(v):
            v.memset(eps_col[:], EPS)
            for i in range(NT):
                v.wait_ge(ld.h, 16 * (i + 2))
                v.tensor_reduce(out=smc[:, i:i + 1], in_=sx[i % 2][:], op=OP.add, axis=AX.X).then_inc(sv.h, 1)
        sv.v = NT

    with nc.Block() as blk:
        @blk.vector
        def _(v):
            v.tensor_copy(out=s_idenb[:], in_=s_iden[:])
            v.memset(eps2_col[:], EPS * EPS)
            v.tensor_scalar_mul(out=mean[:], in0=smc[:], scalar1=1.0 / D)
            v.tensor_scalar_mul(out=tmp[0][:], in0=ssc[:], scalar1=1.0 / D)
            v.tensor_mul(out=tmp[1][:], in0=mean[:], in1=mean[:])
            v.tensor_sub(out=tmp[2][:], in0=tmp[0][:], in1=tmp[1][:]).then_inc(c1.h, 1)
        c1.v = 1

        @blk.scalar
        def _(s):
            s.wait_ge(c1.h, 1)
            s.activation(out=tmp[0][:], in_=tmp[2][:], func=AF.Ln, bias=eps_col[:]).then_inc(c2.h, 1)
            s.activation(out=tmp[1][:], in_=tmp[2][:], func=AF.Ln, bias=eps2_col[:], scale=1.0 + EPS).then_inc(c2.h, 1)
            s.wait_ge(c2.h, 2)
            s.activation(out=rstd[:], in_=tmp[0][:], func=AF.Exp, scale=-0.5)
            s.activation(out=cc[:], in_=tmp[1][:], func=AF.Exp, scale=-0.5).then_inc(c2.h, 1)
        c2.v = 3

    def emit_stores(q, i):
        q.wait_ge(ev.h, (i + 1) * G)
        q.dma_start(out=xn_fm32.rearrange("(g p) t -> p g t", p=128)[:, :, 128 * i:128 * (i + 1)],
                    in_=tf32[i % 2].rearrange("p (g c) -> p g c", g=G)).then_inc(st.h, 16)
        q.dma_start(out=xn_fmb.rearrange("(g p) t -> p g t", p=128)[:, :, 128 * i:128 * (i + 1)],
                    in_=tfb[i % 2].rearrange("p (g c) -> p g c", g=G)).then_inc(st.h, 16)
        if with_xhat:
            q.wait_ge(evh.h, (i + 1) * G)
            q.dma_start(out=xh_fmb.rearrange("(g p) t -> p g t", p=128)[:, :, 128 * i:128 * (i + 1)],
                        in_=thb[i % 2].rearrange("p (g c) -> p g c", g=G)).then_inc(st.h, 16)

    with nc.Block() as blk:
        @blk.sync
        def _(q):
            for i in range(NT):
                if i >= 2:
                    q.wait_ge(nrm.h, i - 1)
                q.dma_start(out=sx[i % 2][:], in_=x[128 * i:128 * (i + 1), :]).then_inc(ld2.h, 16)
                if i >= 2:
                    emit_stores(q, i - 2)
            emit_stores(q, NT - 2)
            emit_stores(q, NT - 1)

        @blk.vector
        def _(v):
            for i in range(NT):
                v.wait_ge(ld2.h, 16 * (i + 1))
                if i >= 2:
                    v.wait_ge(stok.h, 16 * (2 if with_xhat else 1) * (i - 1))
                    v.wait_ge(tp.h, G * (i - 1))
                    if with_xhat:
                        v.wait_ge(tph.h, G * (i - 1))
                if with_xhat:
                    v.tensor_scalar(out=sxn[i % 2][:], in0=sx[i % 2][:], scalar1=mean[:, i:i + 1],
                                    scalar2=rstd[:, i:i + 1], op0=OP.subtract, op1=OP.mult)
                    v.tensor_scalar(out=sxh[i % 2][:], in0=sx[i % 2][:], scalar1=mean[:, i:i + 1],
                                    scalar2=cc[:, i:i + 1], op0=OP.subtract, op1=OP.mult).then_inc(nrm.h, 1)
                else:
                    v.tensor_scalar(out=sxn[i % 2][:], in0=sx[i % 2][:], scalar1=mean[:, i:i + 1],
                                    scalar2=rstd[:, i:i + 1], op0=OP.subtract, op1=OP.mult).then_inc(nrm.h, 1)
                for g in range(G):
                    k = i * G + g
                    v.wait_ge(tp.h, k + 1)
                    v.tensor_copy(out=tf32[i % 2][:, 128 * g:128 * (g + 1)], in_=pt[k % 2][:])
                    v.tensor_copy(out=tfb[i % 2][:, 128 * g:128 * (g + 1)], in_=pt[k % 2][:]).then_inc(ev.h, 1)
        nrm.v = NT
        ev.v = NT * G

        @blk.tensor
        def _(t):
            for i in range(NT):
                t.wait_ge(nrm.h, i + 1)
                if i >= 2:
                    t.wait_ge(st.h, 16 * (3 if with_xhat else 2) * (i - 1))
                for g in range(G):
                    k = i * G + g
                    if k >= 2:
                        t.wait_ge(ev.h, k - 1)
                    t.transpose(out=pt[k % 2][:], in_=sxn[i % 2][:, 128 * g:128 * (g + 1)],
                                identity=s_iden[:]).then_inc(tp.h, 1)
                if with_xhat:
                    for g in range(G):
                        k = i * G + g
                        if k >= 2:
                            t.wait_ge(evh.h, k - 1)
                        t.transpose(out=ptb[k % 2][:], in_=sxh[i % 2][:, 128 * g:128 * (g + 1)],
                                    identity=s_idenb[:]).then_inc(tph.h, 1)
        tp.v = NT * G
        tph.v = NT * G if with_xhat else 0

        @blk.scalar
        def _(s):
            if with_xhat:
                for i in range(NT):
                    if i >= 2:
                        s.wait_ge(st.h, 48 * (i - 1))
                    for g in range(G):
                        k = i * G + g
                        s.wait_ge(tph.h, k + 1)
                        s.copy(out=thb[i % 2][:, 128 * g:128 * (g + 1)], in_=ptb[k % 2][:]).then_inc(evh.h, 1)
        evh.v = NT * G if with_xhat else 0

        @blk.gpsimd
        def _(gp):
            for i in range(NT):
                gp.wait_ge(nrm.h, i + 1)
                gp.dma_start(out=xn_tok[128 * i:128 * (i + 1), :], in_=sxn[i % 2][:]).then_inc(stok.h, 16)
                if with_xhat:
                    gp.dma_start(out=xh_tok[128 * i:128 * (i + 1), :], in_=sxh[i % 2][:]).then_inc(stok.h, 16)
        stok.v = 16 * NT * (2 if with_xhat else 1)

        @blk.sync
        def _(q):
            q.wait_ge(st.h, 16 * NT * (3 if with_xhat else 2))
            q.wait_ge(stok.h, stok.v)
        st.v = 16 * NT * (3 if with_xhat else 2)

    return nc


ENGS = ["sync", "gpsimd", "vector", "scalar", "tensor"]


class Step:
    __slots__ = ("eng", "seq", "dma")
    def __init__(self, eng, dma):
        self.eng, self.seq, self.dma = eng, None, dma


class Prog:
    def __init__(self, nc):
        self.nc = nc
        self.sem = {}
        for e in ENGS:
            self.sem[(e, 0)] = nc.alloc_semaphore(f"P_{e}")
            self.sem[(e, 1)] = nc.alloc_semaphore(f"P_{e}_d")
        self.items = []

    def step(self, eng, deps, emit):
        s = Step(eng, False)
        self.items.append((s, [d for d in deps if d is not None], emit))
        return s

    def dma(self, eng, deps, emit):
        s = Step(eng, True)
        self.items.append((s, [d for d in deps if d is not None], emit))
        return s

    def build(self):
        nc = self.nc
        seqs = {k: 0 for k in self.sem}
        for s, deps, emit in self.items:
            k = (s.eng, 1 if s.dma else 0)
            seqs[k] += 1
            s.seq = seqs[k]
        per_eng = {e: [] for e in ENGS}
        for it in self.items:
            per_eng[it[0].eng].append(it)

        with nc.Block() as blk0:
            @blk0.sync
            def _(q):
                for h in self.sem.values():
                    q.sem_clear(h)

        with nc.Block() as blk:
            def run_engine(e, h):
                for s, deps, emit in per_eng[e]:
                    need = {}
                    for d in deps:
                        key = (d.eng, 1 if d.dma else 0)
                        v = d.seq * (16 if d.dma else 1)
                        if need.get(key, 0) < v:
                            need[key] = v
                    for key, v in need.items():
                        h.wait_ge(self.sem[key], v)
                    last = emit(h)
                    assert last is not None, f"step on {e} returned None"
                    last.then_inc(self.sem[(s.eng, 1 if s.dma else 0)], 16 if s.dma else 1)

            for e in ENGS:
                if per_eng[e]:
                    getattr(blk, e)(lambda h, e=e: run_engine(e, h))
        return nc


def build_attn0():
    nc = bass.Bass("TRN2", target_bir_lowering=False, detect_race_conditions=False)
    xn_fmb = nc.dram_tensor("xn_fmb", [D, T], BF16, kind="ExternalInput").ap()
    wts = {nm: nc.dram_tensor(nm, [D, D], F32, kind="ExternalInput").ap()
           for nm in ["WpT", "WqT", "WkT", "WvT", "WoT"]}
    bcols = {nm: nc.dram_tensor(nm, [128, G], F32, kind="ExternalInput").ap()
             for nm in ["bpc", "bqc", "bkc", "bvc"]}
    bo_row = nc.dram_tensor("bo_row", [1, D], F32, kind="ExternalInput").ap()
    wsel = nc.dram_tensor("wsel", [G, 128, 16], F32, kind="ExternalInput").ap()
    wbc = nc.dram_tensor("wbc", [G, 16, 128], F32, kind="ExternalInput").ap()
    ones1 = nc.dram_tensor("ones1", [1, 128], F32, kind="ExternalInput").ap()
    s_tok = nc.dram_tensor("s_tok", [T, D], F32, kind="ExternalOutput").ap()
    xp_d = nc.dram_tensor("xp_d", [D, T], BF16)

    W = {nm: nc.alloc_sbuf_tensor("s_" + nm, [128, G * D], BF16) for nm in wts}
    B = {nm: nc.alloc_sbuf_tensor("s_" + nm, [128, G], F32) for nm in bcols}
    s_bo = nc.alloc_sbuf_tensor("s_bo", [1, D], BF16)
    s_ones = nc.alloc_sbuf_tensor("s_ones", [1, 128], BF16)
    s_wsel = nc.alloc_sbuf_tensor("s_wsel", [128, G * 16], BF16)
    s_wbc = nc.alloc_sbuf_tensor("s_wbc", [16, G * 128], BF16)
    xn_sb = [nc.alloc_sbuf_tensor(f"xn_sb{i}", [128, G * 512], BF16) for i in range(2)]
    xp_sb = [nc.alloc_sbuf_tensor(f"xp_sb{i}", [128, G * 512], BF16) for i in range(2)]
    w_sb = [nc.alloc_sbuf_tensor(f"w_sb{i}", [128, G * 514], BF16) for i in range(2)]
    q_sb = nc.alloc_sbuf_tensor("q_sb", [128, G * 512], BF16)
    k_sb = nc.alloc_sbuf_tensor("k_sb", [128, G * 514], BF16)
    v_sb = nc.alloc_sbuf_tensor("v_sb", [128, G * 514], BF16)
    qk_sb = nc.alloc_sbuf_tensor("qk_sb", [128, 3 * 512], BF16)
    sm_sb = nc.alloc_sbuf_tensor("sm_sb", [16, 512], F32)
    e_sb = nc.alloc_sbuf_tensor("e_sb", [16, 3 * 512], F32)
    ex_sb = nc.alloc_sbuf_tensor("ex_sb", [16, 3 * 512], F32)
    den_sb = nc.alloc_sbuf_tensor("den_sb", [16, 512], F32)
    lden_sb = nc.alloc_sbuf_tensor("lden_sb", [16, 512], F32)
    a_sb = nc.alloc_sbuf_tensor("a_sb", [16, 3 * 512], BF16)
    ab_sb = nc.alloc_sbuf_tensor("ab_sb", [128, 3 * 512], BF16)
    av_sb = nc.alloc_sbuf_tensor("av_sb", [128, G * 512], BF16)
    so_sb = [nc.alloc_sbuf_tensor(f"so_sb{i}", [128, D], F32) for i in range(2)]
    P = [nc.alloc_psum_tensor(f"P{i}", [128, 512], F32) for i in range(8)]

    p = Prog(nc)
    NONE = []

    wl = []
    for nm in ["WpT", "WqT", "WkT", "WvT", "WoT"]:
        for ig in range(G):
            wl.append(p.dma("gpsimd", NONE, lambda g, nm=nm, ig=ig: g.dma_start(
                out=W[nm][:, ig * D:(ig + 1) * D], in_=wts[nm][128 * ig:128 * (ig + 1), :])))
    for nm in ["bpc", "bqc", "bkc", "bvc"]:
        wl.append(p.dma("gpsimd", NONE, lambda g, nm=nm: g.dma_start(out=B[nm][:], in_=bcols[nm])))
    wl.append(p.dma("gpsimd", NONE, lambda g: g.dma_start(out=s_bo[:], in_=bo_row)))
    wl.append(p.dma("gpsimd", NONE, lambda g: g.dma_start(out=s_ones[:], in_=ones1)))
    wl.append(p.dma("gpsimd", NONE, lambda g: g.dma_start(
        out=s_wsel.rearrange("p (g c) -> p g c", g=G), in_=wsel.rearrange("g p c -> p g c"))))
    wl.append(p.dma("gpsimd", NONE, lambda g: g.dma_start(
        out=s_wbc.rearrange("p (g c) -> p g c", g=G), in_=wbc.rearrange("g p c -> p g c"))))
    wp_done = wl[7]
    w_all = wl[-1]

    xp_store = []
    ev_prev = {}
    xn_ld = []
    for tc in range(NCH):
        deps = [xp_store[tc - 2]] if tc >= 2 else []
        if tc >= 2:
            deps.append(ev_prev[("xn", tc - 2)])
        xn_ld.append(p.dma("sync", deps, lambda q, tc=tc: q.dma_start(
            out=xn_sb[tc % 2].rearrange("p (g c) -> p g c", g=G),
            in_=xn_fmb.rearrange("(g p) t -> p g t", p=128)[:, :, 512 * tc:512 * (tc + 1)])))
        evs = []
        for og in range(G):
            bank = og % 2
            deps = [xn_ld[tc], wp_done]
            if (tc, og) > (0, 1):
                deps.append(ev_prev[("p1", bank)])
            if og == 0 and tc >= 2:
                deps.append(xp_store[tc - 2])
            def mm(t, tc=tc, og=og, bank=bank):
                for ig in range(G):
                    last = t.matmul(out=P[bank][:],
                                    lhsT=W["WpT"][:, ig * D + 128 * og:ig * D + 128 * (og + 1)],
                                    rhs=xn_sb[tc % 2][:, ig * 512:(ig + 1) * 512],
                                    start=(ig == 0), stop=(ig == G - 1))
                return last
            ms = p.step("tensor", deps, mm)
            ev = p.step("vector", [ms, w_all], lambda v, tc=tc, og=og, bank=bank: v.tensor_scalar_add(
                out=xp_sb[tc % 2][:, og * 512:(og + 1) * 512], in0=P[bank][:],
                scalar1=B["bpc"][:, og:og + 1]))
            ev_prev[("p1", bank)] = ev
            evs.append(ev)
        ev_prev[("xn", tc)] = ms
        xp_store.append(p.dma("sync", [evs[-1]], lambda q, tc=tc: q.dma_start(
            out=xp_d.rearrange("(g p) t -> p g t", p=128)[:, :, 512 * tc:512 * (tc + 1)],
            in_=xp_sb[tc % 2].rearrange("p (g c) -> p g c", g=G))))
    xp_all = xp_store[-1]

    last = {}
    st_steps = []
    for tc in range(NCH):
        wnd = []
        deps = [xp_all]
        if tc >= 2:
            deps.append(last[("qkv_mm", tc - 2)])
        lo = 512 * tc - 1
        for g in range(G):
            if tc == 0:
                p.dma("sync", deps, lambda q, g=g: q.dma_start(
                    out=w_sb[0][:, g * 514:g * 514 + 1], in_=xp_d[128 * g:128 * (g + 1), T - 1:T], allow_slow_non_contiguous=True))
                wnd.append(p.dma("sync", deps, lambda q, g=g: q.dma_start(
                    out=w_sb[0][:, g * 514 + 1:(g + 1) * 514], in_=xp_d[128 * g:128 * (g + 1), 0:513])))
            elif tc == NCH - 1:
                p.dma("sync", deps, lambda q, g=g, lo=lo: q.dma_start(
                    out=w_sb[tc % 2][:, g * 514:(g + 1) * 514 - 1], in_=xp_d[128 * g:128 * (g + 1), lo:T]))
                wnd.append(p.dma("sync", deps, lambda q, g=g: q.dma_start(
                    out=w_sb[1][:, (g + 1) * 514 - 1:(g + 1) * 514], in_=xp_d[128 * g:128 * (g + 1), 0:1], allow_slow_non_contiguous=True)))
            else:
                wnd.append(p.dma("sync", deps, lambda q, g=g, lo=lo, tc=tc: q.dma_start(
                    out=w_sb[tc % 2][:, g * 514:(g + 1) * 514], in_=xp_d[128 * g:128 * (g + 1), lo:lo + 514])))
        wnd_done = wnd[-1]

        qkv_ev = []
        for og in range(G):
            deps = [wnd_done, w_all]
            if (tc, og) != (0, 0):
                deps.append(last[("qkv_ev", None)])
            else:
                deps.append(ev_prev[("p1", 0)])
                deps.append(ev_prev[("p1", 1)])
            if tc >= 1:
                deps.append(last[("abev_tc", tc - 1)])
                deps.append(last[("oev", tc - 1)])
            def qkvmm(t, tc=tc, og=og):
                for ig in range(G):
                    t.matmul(out=P[0][:], lhsT=W["WqT"][:, ig * D + 128 * og:ig * D + 128 * (og + 1)],
                             rhs=w_sb[tc % 2][:, ig * 514 + 1:ig * 514 + 513],
                             start=(ig == 0), stop=(ig == G - 1))
                for ig in range(G):
                    t.matmul(out=P[1][:], lhsT=W["WkT"][:, ig * D + 128 * og:ig * D + 128 * (og + 1)],
                             rhs=w_sb[tc % 2][:, ig * 514:ig * 514 + 512],
                             start=(ig == 0), stop=(ig == G - 1))
                for ig in range(G):
                    t.matmul(out=P[2][:, 0:2], lhsT=W["WkT"][:, ig * D + 128 * og:ig * D + 128 * (og + 1)],
                             rhs=w_sb[tc % 2][:, ig * 514 + 512:ig * 514 + 514],
                             start=(ig == 0), stop=(ig == G - 1))
                for ig in range(G):
                    t.matmul(out=P[3][:], lhsT=W["WvT"][:, ig * D + 128 * og:ig * D + 128 * (og + 1)],
                             rhs=w_sb[tc % 2][:, ig * 514:ig * 514 + 512],
                             start=(ig == 0), stop=(ig == G - 1))
                for ig in range(G):
                    last_i = t.matmul(out=P[4][:, 0:2], lhsT=W["WvT"][:, ig * D + 128 * og:ig * D + 128 * (og + 1)],
                                      rhs=w_sb[tc % 2][:, ig * 514 + 512:ig * 514 + 514],
                                      start=(ig == 0), stop=(ig == G - 1))
                return last_i
            qm = p.step("tensor", deps, qkvmm)
            last[("qkv_mm", tc)] = qm

            def qkvev(v, tc=tc, og=og):
                v.tensor_scalar(out=q_sb[:, og * 512:(og + 1) * 512], in0=P[0][:],
                                scalar1=B["bqc"][:, og:og + 1], scalar2=0.125,
                                op0=OP.add, op1=OP.mult)
                v.tensor_scalar_add(out=k_sb[:, og * 514:og * 514 + 512], in0=P[1][:],
                                    scalar1=B["bkc"][:, og:og + 1])
                v.tensor_scalar_add(out=k_sb[:, og * 514 + 512:og * 514 + 514], in0=P[2][:, 0:2],
                                    scalar1=B["bkc"][:, og:og + 1])
                v.tensor_scalar_add(out=v_sb[:, og * 514:og * 514 + 512], in0=P[3][:],
                                    scalar1=B["bvc"][:, og:og + 1])
                return v.tensor_scalar_add(out=v_sb[:, og * 514 + 512:og * 514 + 514], in0=P[4][:, 0:2],
                                           scalar1=B["bvc"][:, og:og + 1])
            qe = p.step("vector", [qm], qkvev)
            last[("qkv_ev", None)] = qe
            qkv_ev.append(qe)

        sc_mm = None
        for g in range(G):
            deps = [qkv_ev[g]]
            if g >= 1:
                deps.append(sc_mm)
            elif tc >= 1:
                deps.append(last[("smax", tc - 1)])
            def qkprod(v, tc=tc, g=g):
                for sh in range(3):
                    li = v.tensor_mul(out=qk_sb[:, sh * 512:(sh + 1) * 512],
                                      in0=q_sb[:, g * 512:(g + 1) * 512],
                                      in1=k_sb[:, g * 514 + sh:g * 514 + sh + 512])
                return li
            qkp = p.step("vector", deps, qkprod)

            def scmm(t, tc=tc, g=g):
                for sh in range(3):
                    li = t.matmul(out=P[5 + sh][0:16, :],
                                  lhsT=s_wsel[:, g * 16:(g + 1) * 16],
                                  rhs=qk_sb[:, sh * 512:(sh + 1) * 512],
                                  start=(g == 0), stop=(g == G - 1))
                return li
            sc_mm = p.step("tensor", [qkp], scmm)

        def smax1(v, tc=tc):
            v.tensor_copy(out=sm_sb[:], in_=P[5][0:16, :])
            v.tensor_max(out=sm_sb[:], in0=sm_sb[:], in1=P[6][0:16, :])
            v.tensor_max(out=sm_sb[:], in0=sm_sb[:], in1=P[7][0:16, :])
            for sh in range(3):
                li = v.tensor_sub(out=e_sb[:, sh * 512:(sh + 1) * 512],
                                  in0=P[5 + sh][0:16, :], in1=sm_sb[:])
            return li
        sm1 = p.step("vector", [sc_mm], smax1)
        last[("smax", tc)] = sm1

        def smax2(s, tc=tc):
            for sh in range(3):
                li = s.activation(out=ex_sb[:, sh * 512:(sh + 1) * 512],
                                  in_=e_sb[:, sh * 512:(sh + 1) * 512], func=AF.Exp)
            return li
        sm2 = p.step("scalar", [sm1], smax2)

        def smax3(v, tc=tc):
            v.tensor_add(out=den_sb[:], in0=ex_sb[:, 0:512], in1=ex_sb[:, 512:1024])
            return v.tensor_add(out=den_sb[:], in0=den_sb[:], in1=ex_sb[:, 1024:1536])
        sm3 = p.step("vector", [sm2], smax3)

        def smax4(s, tc=tc):
            return s.activation(out=lden_sb[:], in_=den_sb[:], func=AF.Ln)
        sm4 = p.step("scalar", [sm3], smax4)

        def smax5(v, tc=tc):
            for sh in range(3):
                li = v.tensor_sub(out=e_sb[:, sh * 512:(sh + 1) * 512],
                                  in0=e_sb[:, sh * 512:(sh + 1) * 512], in1=lden_sb[:])
            return li
        sm5 = p.step("vector", [sm4], smax5)

        def smax6(s, tc=tc):
            for sh in range(3):
                li = s.activation(out=a_sb[:, sh * 512:(sh + 1) * 512],
                                  in_=e_sb[:, sh * 512:(sh + 1) * 512], func=AF.Exp)
            return li
        sm6 = p.step("scalar", [sm5], smax6)
        last[("asb", tc)] = sm6

        av_st = None
        for g in range(G):
            deps = [sm6]
            if g >= 1:
                deps.append(last[("abev", None)])
            elif tc >= 1:
                deps.append(last[("av", tc - 1)])
            def bcmm(t, tc=tc, g=g):
                for sh in range(3):
                    li = t.matmul(out=P[sh][:], lhsT=s_wbc[:, g * 128:(g + 1) * 128],
                                  rhs=a_sb[:, sh * 512:(sh + 1) * 512], start=True, stop=True)
                return li
            bq = p.step("tensor", deps, bcmm)
            last[("bcast_ev", tc)] = bq

            def abev(s, tc=tc, g=g):
                for sh in range(3):
                    li = s.copy(out=ab_sb[:, sh * 512:(sh + 1) * 512], in_=P[sh][:])
                return li
            ae = p.step("scalar", [bq], abev)
            last[("abev", None)] = ae
            last[("abev_tc", tc)] = ae

            def avc(v, tc=tc, g=g):
                v.tensor_mul(out=av_sb[:, g * 512:(g + 1) * 512], in0=ab_sb[:, 0:512],
                             in1=v_sb[:, g * 514:g * 514 + 512])
                v.tensor_mul(out=qk_sb[:, 0:512], in0=ab_sb[:, 512:1024],
                             in1=v_sb[:, g * 514 + 1:g * 514 + 513])
                v.tensor_add(out=av_sb[:, g * 512:(g + 1) * 512], in0=av_sb[:, g * 512:(g + 1) * 512],
                             in1=qk_sb[:, 0:512])
                v.tensor_mul(out=qk_sb[:, 0:512], in0=ab_sb[:, 1024:1536],
                             in1=v_sb[:, g * 514 + 2:g * 514 + 514])
                return v.tensor_add(out=av_sb[:, g * 512:(g + 1) * 512],
                                    in0=av_sb[:, g * 512:(g + 1) * 512], in1=qk_sb[:, 0:512])
            av = p.step("vector", [ae], avc)
            last[("av", tc)] = av
        av_all = last[("av", tc)]

        oev_step = None
        for tt in range(4):
            deps = [av_all]
            if tt >= 1:
                deps.append(last[("oev", tc)])
            elif tc >= 1:
                deps.append(last[("oev", tc - 1)])
            def womm(t, tc=tc, tt=tt):
                for od in range(2):
                    ps = P[3 + od]
                    for ig in range(G):
                        t.matmul(out=ps[:], lhsT=av_sb[:, ig * 512 + 128 * tt:ig * 512 + 128 * (tt + 1)],
                                 rhs=W["WoT"][:, ig * D + 512 * od:ig * D + 512 * (od + 1)],
                                 start=(ig == 0), stop=False)
                    li = t.matmul(out=ps[:], lhsT=s_ones[:], rhs=s_bo[:, 512 * od:512 * (od + 1)],
                                  start=False, stop=True)
                return li
            wo = p.step("tensor", deps, womm)

            deps = [wo]
            if tc * 4 + tt >= 2:
                deps.append(st_steps[-2])
            def oev(v, tc=tc, tt=tt):
                v.tensor_copy(out=so_sb[tt % 2][:, 0:512], in_=P[3][:])
                return v.tensor_copy(out=so_sb[tt % 2][:, 512:1024], in_=P[4][:])
            oe = p.step("vector", deps, oev)
            last[("oev", tc)] = oe

            st_steps.append(p.dma("sync", [oe], lambda q, tc=tc, tt=tt: q.dma_start(
                out=s_tok[512 * tc + 128 * tt:512 * tc + 128 * (tt + 1), :], in_=so_sb[tt % 2][:])))

    p.step("sync", [st_steps[-1]], lambda q: q.nop())
    return p.build()


F_HALF = 2048
NFG = 16


def build_ffn0e():
    nc = bass.Bass("TRN2", target_bir_lowering=False, detect_race_conditions=False)
    xh_fmb = nc.dram_tensor("xh_fmb", [D, T], BF16, kind="ExternalInput").ap()
    xn_fm32 = nc.dram_tensor("xn_fm32", [D, T], F32, kind="ExternalInput").ap()
    cenT = nc.dram_tensor("cenT", [D, G], F32, kind="ExternalInput").ap()
    w1Te = nc.dram_tensor("w1Te", [D, 4096], F32, kind="ExternalInput").ap()
    w2Te = nc.dram_tensor("w2Te", [4096, D], F32, kind="ExternalInput").ap()
    lg = nc.dram_tensor("lg", [128, G], F32, kind="ExternalInput").ap()
    lb = nc.dram_tensor("lb", [128, G], F32, kind="ExternalInput").ap()
    b1c = nc.dram_tensor("b1c", [128, 32], F32, kind="ExternalInput").ap()
    b2_row = nc.dram_tensor("b2_row", [1, D], F32, kind="ExternalInput").ap()
    ones1 = nc.dram_tensor("ones1", [1, 128], F32, kind="ExternalInput").ap()
    econst = nc.dram_tensor("econst", [128, 1], F32, kind="ExternalInput").ap()
    acc_in = nc.dram_tensor("acc_in", [T, D], F32, kind="ExternalInput").ap()
    acc_out = nc.dram_tensor("acc_out", [T, D], F32, kind="ExternalOutput").ap()
    acc_mid = nc.dram_tensor("acc_mid", [T, D], F32)

    s_cen = nc.alloc_sbuf_tensor("s_cen", [128, G * G], F32)
    s_lg = nc.alloc_sbuf_tensor("s_lg", [128, G], F32)
    s_lb = nc.alloc_sbuf_tensor("s_lb", [128, G], F32)
    s_b1 = nc.alloc_sbuf_tensor("s_b1", [128, 32], F32)
    s_b2 = nc.alloc_sbuf_tensor("s_b2", [1, D], BF16)
    s_ones = nc.alloc_sbuf_tensor("s_ones", [1, 128], BF16)
    s_e = nc.alloc_sbuf_tensor("s_e", [128, 1], F32)
    xn32 = [nc.alloc_sbuf_tensor(f"xn32_{i}", [128, D], F32) for i in range(2)]
    mx_sb = nc.alloc_sbuf_tensor("mx_sb", [128, 8], F32)
    mi_sb = nc.alloc_sbuf_tensor("mi_sb", [128, 8], U32)
    a_col = nc.alloc_sbuf_tensor("a_col", [128, 1], F32)
    am_all = nc.alloc_sbuf_tensor("am_all", [128, NT], F32)
    w1_sb = nc.alloc_sbuf_tensor("w1_sb", [128, G * F_HALF], BF16)
    w2_sb = nc.alloc_sbuf_tensor("w2_sb", [128, NFG * D], BF16)
    xh_sb = [nc.alloc_sbuf_tensor(f"xh_sb{i}", [128, G * 512], BF16) for i in range(2)]
    xe_sb = nc.alloc_sbuf_tensor("xe_sb", [128, G * 512], BF16)
    h_sb = nc.alloc_sbuf_tensor("h_sb", [128, NFG * 512], BF16)
    ac_sb = [nc.alloc_sbuf_tensor(f"ac_sb{i}", [128, D], F32) for i in range(2)]
    P = [nc.alloc_psum_tensor(f"P{i}", [128, 512], F32) for i in range(6)]
    Pq = nc.alloc_psum_tensor("Pq", [128, 8], F32)

    p = Prog(nc)
    NONE = []

    wl = []
    wl.append(p.dma("gpsimd", NONE, lambda g: g.dma_start(
        out=s_cen.rearrange("p (i c) -> p i c", i=G),
        in_=cenT.rearrange("(i p) c -> p i c", p=128))))
    for nm, sb, dr in [("lg", s_lg, lg), ("lb", s_lb, lb), ("b1", s_b1, b1c),
                       ("b2", s_b2, b2_row), ("ones", s_ones, ones1), ("e", s_e, econst)]:
        wl.append(p.dma("gpsimd", NONE, lambda g, sb=sb, dr=dr: g.dma_start(out=sb[:], in_=dr)))
    const_done = wl[-1]

    mprev = None
    for i in range(NT):
        deps = [wl[0]]
        if i >= 2:
            deps.append(last_aff)
        ld = p.dma("sync", deps if i >= 2 else [wl[0]], lambda q, i=i: q.dma_start(
            out=xn32[i % 2].rearrange("p (g c) -> p g c", g=G),
            in_=xn_fm32.rearrange("(g p) t -> p g t", p=128)[:, :, 128 * i:128 * (i + 1)]))
        def affmm(t, i=i):
            for ig in range(G):
                li = t.matmul(out=Pq[:], lhsT=xn32[i % 2][:, ig * 128:(ig + 1) * 128],
                              rhs=s_cen[:, ig * G:(ig + 1) * G], start=(ig == 0), stop=(ig == G - 1))
            return li
        deps = [ld]
        if mprev is not None:
            deps.append(mprev)
        aff = p.step("tensor", deps, affmm)
        last_aff = aff
        mx = p.step("vector", [aff], lambda v, i=i: v.max(out=mx_sb[:], in_=Pq[:]))
        mi = p.step("vector", [mx], lambda v, i=i: v.max_index(out=mi_sb[:], in_max=mx_sb[:], in_=None)
                    if False else v.max_index(out=mi_sb[:], in_max=mx_sb[:], in_values=Pq[:]))
        mprev = mi
        sg = p.step("scalar", [mx], lambda s, i=i: s.activation(out=a_col[:], in_=mx_sb[:, 0:1],
                                                                func=AF.Sigmoid))
        am = p.step("vector", [mi, sg, const_done], lambda v, i=i: v.tensor_scalar(
            out=am_all[:, i:i + 1], in0=mi_sb[:, 0:1], scalar1=s_e[:], scalar2=a_col[:],
            op0=OP.is_equal, op1=OP.mult))
        mprev = am
    am_done = am

    last = {}
    acc_st = []
    for fh in range(2):
        w1l = []
        for ig in range(G):
            w1l.append(p.dma("gpsimd", [last.get(("hmm", None))] if fh else NONE,
                             lambda g, ig=ig, fh=fh: g.dma_start(
                out=w1_sb[:, ig * F_HALF:(ig + 1) * F_HALF],
                in_=w1Te[128 * ig:128 * (ig + 1), F_HALF * fh:F_HALF * (fh + 1)])))
        w2l = []
        for fk in range(NFG):
            w2l.append(p.dma("gpsimd", [last.get(("zmm", None))] if fh else NONE,
                             lambda g, fk=fk, fh=fh: g.dma_start(
                out=w2_sb[:, fk * D:(fk + 1) * D],
                in_=w2Te[F_HALF * fh + 128 * fk:F_HALF * fh + 128 * (fk + 1), :])))
        w1_done, w2_done = w1l[-1], w2l[-1]

        for tc in range(NCH):
            k = fh * NCH + tc
            deps = [const_done]
            if k >= 2:
                deps.append(last[("xe", k - 2)])
            xhl = p.dma("sync", deps, lambda q, tc=tc, k=k: q.dma_start(
                out=xh_sb[k % 2].rearrange("p (g c) -> p g c", g=G),
                in_=xh_fmb.rearrange("(g p) t -> p g t", p=128)[:, :, 512 * tc:512 * (tc + 1)]))

            deps = [xhl, const_done]
            if k >= 1:
                deps.append(last[("hmm", None)])
            def xec(v, k=k):
                for ig in range(G):
                    li = v.tensor_scalar(out=xe_sb[:, ig * 512:(ig + 1) * 512],
                                         in0=xh_sb[k % 2][:, ig * 512:(ig + 1) * 512],
                                         scalar1=s_lg[:, ig:ig + 1], scalar2=s_lb[:, ig:ig + 1],
                                         op0=OP.mult, op1=OP.add)
                return li
            xe = p.step("vector", deps, xec)
            last[("xe", k)] = xe

            hevs = []
            for fg in range(NFG):
                bank = fg % 2
                deps = [xe, w1_done]
                if fg >= 2 or k > 0:
                    deps.append(last.get(("hev", bank), xe))
                if k > 0 and fg < 2:
                    deps.append(last[("zmm", None)])
                def hmm(t, fg=fg, k=k, bank=bank, fh=fh):
                    for ig in range(G):
                        li = t.matmul(out=P[bank][:],
                                      lhsT=w1_sb[:, ig * F_HALF + 128 * fg:ig * F_HALF + 128 * (fg + 1)],
                                      rhs=xe_sb[:, ig * 512:(ig + 1) * 512],
                                      start=(ig == 0), stop=(ig == G - 1))
                    return li
                hm = p.step("tensor", deps, hmm)
                last[("hmm", None)] = hm
                he = p.step("scalar", [hm], lambda s, fg=fg, bank=bank, fh=fh: s.activation(
                    out=h_sb[:, fg * 512:(fg + 1) * 512], in_=P[bank][:], func=AF.Relu,
                    bias=s_b1[:, NFG * fh + fg:NFG * fh + fg + 1]))
                last[("hev", bank)] = he
                hevs.append(he)
            h_done = hevs[-1]

            for tt in range(4):
                i = tc * 4 + tt
                deps = [am_done]
                if i >= 2 or fh > 0:
                    deps.append(acc_st[-2] if len(acc_st) >= 2 else acc_st[0])
                acl = p.dma("sync", deps, lambda q, i=i, fh=fh: q.dma_start(
                    out=ac_sb[i % 2][:],
                    in_=(acc_in if fh == 0 else acc_mid)[128 * i:128 * (i + 1), :]))
                deps = [h_done, w2_done]
                if (fh, tc, tt) != (0, 0, 0):
                    deps.append(last[("zev", None)])
                def zmm(t, tt=tt, fh=fh):
                    for od in range(2):
                        for fk in range(NFG):
                            li = t.matmul(out=P[2 + od][:],
                                          lhsT=h_sb[:, fk * 512 + 128 * tt:fk * 512 + 128 * (tt + 1)],
                                          rhs=w2_sb[:, fk * D + 512 * od:fk * D + 512 * (od + 1)],
                                          start=(fk == 0), stop=(fk == NFG - 1 and fh == 1))
                        if fh == 0:
                            li = t.matmul(out=P[2 + od][:], lhsT=s_ones[:],
                                          rhs=s_b2[:, 512 * od:512 * (od + 1)],
                                          start=False, stop=True)
                    return li
                zm = p.step("tensor", deps, zmm)
                last[("zmm", None)] = zm

                def zev(v, i=i, tt=tt):
                    for od in range(2):
                        li = v.scalar_tensor_tensor(out=ac_sb[i % 2][:, 512 * od:512 * (od + 1)],
                                                    in0=P[2 + od][:], scalar=am_all[:, i:i + 1],
                                                    op0=OP.mult, op1=OP.add,
                                                    in1=ac_sb[i % 2][:, 512 * od:512 * (od + 1)])
                    return li
                ze = p.step("vector", [zm, acl], zev)
                last[("zev", None)] = ze
                acc_st.append(p.dma("sync", [ze], lambda q, i=i, fh=fh: q.dma_start(
                    out=(acc_mid if fh == 0 else acc_out)[128 * i:128 * (i + 1), :],
                    in_=ac_sb[i % 2][:])))

    p.step("sync", [acc_st[-1]], lambda q: q.nop())
    return p.build()


def build_combine(kind):
    nc = bass.Bass("TRN2", target_bir_lowering=False, detect_race_conditions=False)
    if kind == "ffn0fin":
        ins = {nm: nc.dram_tensor(nm, [T, D], F32, kind="ExternalInput").ap()
               for nm in ["acc", "xn_tok", "s_tok", "x"]}
    else:
        ins = {nm: nc.dram_tensor(nm, [T, D], F32, kind="ExternalInput").ap()
               for nm in ["g1_tok", "s_tok", "x"]}
    out = nc.dram_tensor("out", [T, D], F32, kind="ExternalOutput").ap()

    bufs = {nm: [nc.alloc_sbuf_tensor(f"b_{nm}{i}", [128, D], F32) for i in range(2)]
            for nm in ins}
    g_sb = [nc.alloc_sbuf_tensor(f"g_sb{i}", [128, D], F32) for i in range(2)]
    o_sb = [nc.alloc_sbuf_tensor(f"o_sb{i}", [128, D], F32) for i in range(2)]

    p = Prog(nc)
    st = []
    for i in range(NT):
        lds = []
        for nm in ins:
            deps = [st[i - 2]] if i >= 2 else []
            lds.append(p.dma("sync", deps, lambda q, nm=nm, i=i: q.dma_start(
                out=bufs[nm][i % 2][:], in_=ins[nm][128 * i:128 * (i + 1), :])))
        if kind == "ffn0fin":
            c1 = p.step("vector", lds, lambda v, i=i: v.tensor_add(
                out=g_sb[i % 2][:], in0=bufs["acc"][i % 2][:], in1=bufs["xn_tok"][i % 2][:]))
            c2 = p.step("scalar", [c1], lambda s, i=i: s.activation(
                out=g_sb[i % 2][:], in_=g_sb[i % 2][:], func=AF.Silu))
        else:
            c2 = lds[-1]
        def comb(v, i=i, kind=kind):
            gg = g_sb[i % 2] if kind == "ffn0fin" else bufs["g1_tok"][i % 2]
            v.tensor_mul(out=o_sb[i % 2][:], in0=bufs["s_tok"][i % 2][:], in1=gg[:])
            return v.tensor_add(out=o_sb[i % 2][:], in0=o_sb[i % 2][:], in1=bufs["x"][i % 2][:])
        c3 = p.step("vector", [c2] + lds, comb)
        st.append(p.dma("sync", [c3], lambda q, i=i: q.dma_start(
            out=out[128 * i:128 * (i + 1), :], in_=o_sb[i % 2][:])))
    p.step("sync", [st[-1]], lambda q: q.nop())
    return p.build()


def build_attn1():
    nc = bass.Bass("TRN2", target_bir_lowering=False, detect_race_conditions=False)
    xn_fmb = nc.dram_tensor("xn_fmb", [D, T], BF16, kind="ExternalInput").ap()
    WrvkT = nc.dram_tensor("WrvkT", [D, 3 * D], F32, kind="ExternalInput").ap()
    WoutT = nc.dram_tensor("WoutT", [D, D], F32, kind="ExternalInput").ap()
    tdc = nc.dram_tensor("tdc", [128, G], F32, kind="ExternalInput").ap()
    tfc = nc.dram_tensor("tfc", [128, G], F32, kind="ExternalInput").ap()
    s1_tok = nc.dram_tensor("s1_tok", [T, D], F32, kind="ExternalOutput").ap()
    k_d = nc.dram_tensor("k_d", [D, T], F32)
    v_d = nc.dram_tensor("v_d", [D, T], F32)
    sr_d = nc.dram_tensor("sr_d", [D, T], BF16)
    rw_d = nc.dram_tensor("rw_d", [D, T], BF16)

    w_sb = nc.alloc_sbuf_tensor("w_sb", [128, G * D], BF16)
    wo_sb = nc.alloc_sbuf_tensor("wo_sb", [128, G * D], BF16)
    s_td = nc.alloc_sbuf_tensor("s_td", [128, G], F32)
    s_tf = nc.alloc_sbuf_tensor("s_tf", [128, G], F32)
    cols = nc.alloc_sbuf_tensor("cols", [128, 8], F32)
    xn_sb = [nc.alloc_sbuf_tensor(f"xn_sb{i}", [128, G * 512], BF16) for i in range(2)]
    o_sb = [nc.alloc_sbuf_tensor(f"o_sb{i}", [128, G * 512], F32) for i in range(2)]
    ewr = nc.alloc_sbuf_tensor("ewr", [128, T], F32)
    bufA = nc.alloc_sbuf_tensor("bufA", [128, T], F32)
    bufB = nc.alloc_sbuf_tensor("bufB", [128, T], F32)
    bufD = nc.alloc_sbuf_tensor("bufD", [128, T], F32)
    bufE = nc.alloc_sbuf_tensor("bufE", [128, T], F32)
    bufF = nc.alloc_sbuf_tensor("bufF", [128, T], F32)
    sr_sb = nc.alloc_sbuf_tensor("sr_sb", [128, T], BF16)
    y_sb = [nc.alloc_sbuf_tensor(f"y_sb{i}", [128, D], F32) for i in range(2)]
    rwc_sb = xn_sb
    P = [nc.alloc_psum_tensor(f"P{i}", [128, 512], F32) for i in range(4)]

    p = Prog(nc)
    NONE = []

    cl = p.dma("gpsimd", NONE, lambda g: g.dma_start(out=s_td[:], in_=tdc))
    cl2 = p.dma("gpsimd", NONE, lambda g: g.dma_start(out=s_tf[:], in_=tfc))
    wol = [p.dma("gpsimd", NONE, lambda g, ig=ig: g.dma_start(
        out=wo_sb[:, ig * D:(ig + 1) * D], in_=WoutT[128 * ig:128 * (ig + 1), :])) for ig in range(G)]

    last = {}
    stores = []
    for part in range(3):
        wld = [p.dma("gpsimd", [last[("mm", None)]] if part else NONE,
                     lambda g, ig=ig, part=part: g.dma_start(
            out=w_sb[:, ig * D:(ig + 1) * D],
            in_=WrvkT[128 * ig:128 * (ig + 1), D * part:D * (part + 1)])) for ig in range(G)]
        w_done = wld[-1]
        for tc in range(NCH):
            kk = part * NCH + tc
            deps = [w_done]
            if kk >= 2:
                deps.append(last[("mm", kk - 2)])
            xl = p.dma("sync", deps, lambda q, tc=tc, kk=kk: q.dma_start(
                out=xn_sb[kk % 2].rearrange("p (g c) -> p g c", g=G),
                in_=xn_fmb.rearrange("(g p) t -> p g t", p=128)[:, :, 512 * tc:512 * (tc + 1)]))
            evs = []
            for og in range(G):
                bank = og % 2
                deps = [xl, w_done]
                if (kk, og) > (0, 1):
                    deps.append(last.get(("ev", bank)))
                def mmf(t, og=og, kk=kk, bank=bank):
                    for ig in range(G):
                        li = t.matmul(out=P[bank][:],
                                      lhsT=w_sb[:, ig * D + 128 * og:ig * D + 128 * (og + 1)],
                                      rhs=xn_sb[kk % 2][:, ig * 512:(ig + 1) * 512],
                                      start=(ig == 0), stop=(ig == G - 1))
                    return li
                ms = p.step("tensor", deps, mmf)
                last[("mm", None)] = ms
                last[("mm", kk)] = ms
                if part == 0:
                    ev = p.step("scalar", [ms] + ([stores[-2]] if len(stores) >= 2 else []),
                                lambda s, og=og, kk=kk, bank=bank: s.activation(
                        out=o_sb[kk % 2].bitcast(BF16)[:, og * 1024:og * 1024 + 512], in_=P[bank][:],
                        func=AF.Sigmoid))
                else:
                    ev = p.step("vector", [ms] + ([stores[-2]] if len(stores) >= 2 else []),
                                lambda v, og=og, kk=kk, bank=bank: v.tensor_copy(
                        out=o_sb[kk % 2][:, og * 512:(og + 1) * 512], in_=P[bank][:]))
                last[("ev", bank)] = ev
                evs.append(ev)
            dst = [sr_d, v_d, k_d][part]
            if part == 0:
                stores.append(p.dma("sync", [evs[-1]], lambda q, tc=tc, kk=kk: q.dma_start(
                    out=sr_d.rearrange("(g p) t -> p g t", p=128)[:, :, 512 * tc:512 * (tc + 1)],
                    in_=o_sb[kk % 2].bitcast(BF16).rearrange("p (g c) -> p g c", g=G)[:, :, 0:512])))
            else:
                stores.append(p.dma("sync", [evs[-1]], lambda q, tc=tc, kk=kk, dst=dst: q.dma_start(
                    out=dst.rearrange("(g p) t -> p g t", p=128)[:, :, 512 * tc:512 * (tc + 1)],
                    in_=o_sb[kk % 2].rearrange("p (g c) -> p g c", g=G))))
    ph1_done = stores[-1]


    rw_stores = []
    for g in range(G):
        deps = [ph1_done]
        if g >= 1:
            deps.append(last[("rw", g - 1)])
        lk = p.dma("sync", deps, lambda q, g=g: q.dma_start(out=bufA[:], in_=k_d[128 * g:128 * (g + 1), :]))
        lv = p.dma("sync", deps, lambda q, g=g: q.dma_start(out=bufB[:], in_=v_d[128 * g:128 * (g + 1), :]))
        lr = p.dma("sync", deps, lambda q, g=g: q.dma_start(out=sr_sb[:], in_=sr_d[128 * g:128 * (g + 1), :]))

        c1 = p.step("scalar", [cl, cl2] + ([last[("rw", g - 1)]] if g else []), lambda s, g=g: (
            s.activation(out=cols[:, 0:1], in_=s_td[:, g:g + 1], func=AF.Exp),
            s.activation(out=cols[:, 2:3], in_=s_tf[:, g:g + 1], func=AF.Exp))[-1])
        c2 = p.step("scalar", [c1], lambda s, g=g: s.activation(
            out=cols[:, 1:2], in_=cols[:, 0:1], func=AF.Exp, scale=-1.0))
        msetg = p.step("vector", [ph1_done] + ([last[("rw", g - 1)]] if g else []),
                       lambda v: v.memset(ewr[:], 1.0))
        ewt = p.step("vector", [c2, msetg], lambda v, g=g: v.tensor_scalar_mul(
            out=ewr[:], in0=ewr[:], scalar1=cols[:, 1:2]))
        ek = p.step("scalar", [lk, c2], lambda s, g=g: s.activation(out=bufD[:], in_=bufA[:], func=AF.Exp))
        ekv = p.step("vector", [ek, lv], lambda v, g=g: v.tensor_mul(out=bufE[:], in0=bufD[:], in1=bufB[:]))
        sa = p.step("vector", [ekv, ewt], lambda v, g=g: v.tensor_tensor_scan(
            out=bufF[:], data0=ewr[:], data1=bufE[:], initial=0.0, op0=OP.mult, op1=OP.add))
        sb_ = p.step("vector", [sa], lambda v, g=g: v.tensor_tensor_scan(
            out=bufA[:], data0=ewr[:], data1=bufD[:], initial=0.0, op0=OP.mult, op1=OP.add))
        num = p.step("vector", [sb_], lambda v, g=g: (
            v.tensor_scalar_mul(out=bufB[:, 0:1], in0=bufE[:, 0:1], scalar1=cols[:, 2:3]),
            v.scalar_tensor_tensor(out=bufB[:, 1:T], in0=bufE[:, 1:T], scalar=cols[:, 2:3],
                                   op0=OP.mult, op1=OP.add, in1=bufF[:, 0:T - 1]))[-1])
        den = p.step("vector", [num], lambda v, g=g: (
            v.tensor_scalar_mul(out=bufD[:, 0:1], in0=bufD[:, 0:1], scalar1=cols[:, 2:3]),
            v.scalar_tensor_tensor(out=bufD[:, 1:T], in0=bufD[:, 1:T], scalar=cols[:, 2:3],
                                   op0=OP.mult, op1=OP.add, in1=bufA[:, 0:T - 1]))[-1])
        rd = p.step("vector", [den], lambda v, g=g: v.reciprocal(out=bufE[:], in_=bufD[:]))
        wkv = p.step("vector", [rd], lambda v, g=g: v.tensor_mul(out=bufF[:], in0=bufB[:], in1=bufE[:]))
        rw = p.step("vector", [wkv, lr], lambda v, g=g: v.tensor_mul(out=sr_sb[:], in0=bufF[:], in1=sr_sb[:]))
        last[("rw", g)] = rw
        rw_stores.append(p.dma("sync", [rw], lambda q, g=g: q.dma_start(
            out=rw_d[128 * g:128 * (g + 1), :], in_=sr_sb[:])))
    ph2_done = rw_stores[-1]

    st3 = []
    for tc in range(NCH):
        deps = [ph2_done]
        if tc >= 2:
            deps.append(last[("omm", tc - 2)])
        rl = p.dma("sync", deps, lambda q, tc=tc: q.dma_start(
            out=rwc_sb[tc % 2].rearrange("p (g c) -> p g c", g=G),
            in_=rw_d.rearrange("(g p) t -> p g t", p=128)[:, :, 512 * tc:512 * (tc + 1)]))
        for tt in range(4):
            i = tc * 4 + tt
            deps = [rl, wol[-1]]
            if i >= 1:
                deps.append(last[("oev", None)])
            def omm(t, tc=tc, tt=tt):
                for od in range(2):
                    for ig in range(G):
                        li = t.matmul(out=P[od][:],
                                      lhsT=rwc_sb[tc % 2][:, ig * 512 + 128 * tt:ig * 512 + 128 * (tt + 1)],
                                      rhs=wo_sb[:, ig * D + 512 * od:ig * D + 512 * (od + 1)],
                                      start=(ig == 0), stop=(ig == G - 1))
                return li
            om = p.step("tensor", deps, omm)
            last[("omm", tc)] = om
            deps = [om]
            if i >= 2:
                deps.append(st3[i - 2])
            oe = p.step("vector", deps, lambda v, i=i: (
                v.tensor_copy(out=y_sb[i % 2][:, 0:512], in_=P[0][:]),
                v.tensor_copy(out=y_sb[i % 2][:, 512:1024], in_=P[1][:]))[-1])
            last[("oev", None)] = oe
            st3.append(p.dma("sync", [oe], lambda q, i=i: q.dma_start(
                out=s1_tok[128 * i:128 * (i + 1), :], in_=y_sb[i % 2][:])))
    p.step("sync", [st3[-1]], lambda q: q.nop())
    return p.build()


NE1 = 16
NH = 8


def build_ffn1():
    nc = bass.Bass("TRN2", target_bir_lowering=False, detect_race_conditions=False)
    xn_fmb = nc.dram_tensor("xn_fmb", [D, T], BF16, kind="ExternalInput").ap()
    xn_fm32 = nc.dram_tensor("xn_fm32", [D, T], F32, kind="ExternalInput").ap()
    wrT = nc.dram_tensor("wrT", [D, NE1], F32, kind="ExternalInput").ap()
    wmhT = nc.dram_tensor("wmhT", [D, D], F32, kind="ExternalInput").ap()
    wmgT = nc.dram_tensor("wmgT", [D, D], F32, kind="ExternalInput").ap()
    kfT = nc.dram_tensor("kfT", [NH, NE1, 128, 256], F32, kind="ExternalInput").ap()
    vf = nc.dram_tensor("vf", [NH, NE1, 256, 128], F32, kind="ExternalInput").ap()
    iota16 = nc.dram_tensor("iota16", [128, NE1], F32, kind="ExternalInput").ap()
    esel = nc.dram_tensor("esel", [NE1, NE1 * 128], F32, kind="ExternalInput").ap()
    iden = nc.dram_tensor("iden", [128, 128], F32, kind="ExternalInput").ap()
    g1_tok = nc.dram_tensor("g1_tok", [T, D], F32, kind="ExternalOutput").ap()
    y_d = nc.dram_tensor("y_d", [D, T], BF16)

    s_wr = nc.alloc_sbuf_tensor("s_wr", [128, G * NE1], F32)
    s_wmh = nc.alloc_sbuf_tensor("s_wmh", [128, G * D], BF16)
    s_wmg = nc.alloc_sbuf_tensor("s_wmg", [128, G * D], BF16)
    s_io16 = nc.alloc_sbuf_tensor("s_io16", [128, NE1], F32)
    s_esel = nc.alloc_sbuf_tensor("s_esel", [NE1, NE1 * 128], BF16)
    s_iden = nc.alloc_sbuf_tensor("s_iden", [128, 128], F32)
    xn32 = [nc.alloc_sbuf_tensor(f"xn32_{i}", [128, D], F32) for i in range(2)]
    mx_sb = nc.alloc_sbuf_tensor("mx_sb", [128, 8], F32)
    mi_sb = nc.alloc_sbuf_tensor("mi_sb", [128, 8], U32)
    gt_sb = nc.alloc_sbuf_tensor("gt_sb", [128, 2], F32)
    lg_sb = nc.alloc_sbuf_tensor("lg_sb", [128, NE1], F32)
    sa_sb = nc.alloc_sbuf_tensor("sa_sb", [128, NE1], F32)
    mk_sb = nc.alloc_sbuf_tensor("mk_sb", [128, NE1], F32)
    mx2_col = nc.alloc_sbuf_tensor("mx2_col", [128, 1], F32)
    wte_sb = nc.alloc_sbuf_tensor("wte_sb", [128, NE1], F32)
    wteT = nc.alloc_sbuf_tensor("wteT", [NE1, T], F32)
    wteTb = nc.alloc_sbuf_tensor("wteTb", [NE1, T], BF16)
    kf_sb = nc.alloc_sbuf_tensor("kf_sb", [128, 4 * NE1 * 256], BF16)
    vf_sb = nc.alloc_sbuf_tensor("vf_sb", [128, 4 * NE1 * 2 * 128], BF16)
    xnb_sb = [nc.alloc_sbuf_tensor(f"xnb_sb{i}", [128, G * 512], BF16) for i in range(2)]
    q_sb = nc.alloc_sbuf_tensor("q_sb", [128, G * 512], BF16)
    wb_sb = nc.alloc_sbuf_tensor("wb_sb", [128, NE1 * 512], BF16)
    act_sb = nc.alloc_sbuf_tensor("act_sb", [128, 2 * 512], BF16)
    y_sb = nc.alloc_sbuf_tensor("y_sb", [128, 4 * 512], BF16)
    go_sb = [nc.alloc_sbuf_tensor(f"go_sb{i}", [128, D], F32) for i in range(2)]
    sc_ps = nc.alloc_psum_tensor("sc_ps", [128, 1024], F32)
    P = [nc.alloc_psum_tensor(f"Pf{i}", [128, 512], F32) for i in range(5)]
    Pq = nc.alloc_psum_tensor("Pqf", [128, 128], F32)

    p = Prog(nc)
    NONE = []

    cl = []
    cl.append(p.dma("gpsimd", NONE, lambda g: g.dma_start(
        out=s_wr.rearrange("p (i c) -> p i c", i=G), in_=wrT.rearrange("(i p) c -> p i c", p=128))))
    for ig in range(G):
        cl.append(p.dma("gpsimd", NONE, lambda g, ig=ig: g.dma_start(
            out=s_wmh[:, ig * D:(ig + 1) * D], in_=wmhT[128 * ig:128 * (ig + 1), :])))
        cl.append(p.dma("gpsimd", NONE, lambda g, ig=ig: g.dma_start(
            out=s_wmg[:, ig * D:(ig + 1) * D], in_=wmgT[128 * ig:128 * (ig + 1), :])))
    cl.append(p.dma("gpsimd", NONE, lambda g: g.dma_start(out=s_io16[:], in_=iota16)))
    cl.append(p.dma("gpsimd", NONE, lambda g: g.dma_start(out=s_esel[:], in_=esel)))
    cl.append(p.dma("gpsimd", NONE, lambda g: g.dma_start(out=s_iden[:], in_=iden)))
    const_done = cl[-1]

    last = {}
    for i in range(NT):
        deps = [cl[0]]
        if i >= 2:
            deps.append(last[("aff", i - 2)])
        ld = p.dma("sync", deps, lambda q, i=i: q.dma_start(
            out=xn32[i % 2].rearrange("p (g c) -> p g c", g=G),
            in_=xn_fm32.rearrange("(g p) t -> p g t", p=128)[:, :, 128 * i:128 * (i + 1)]))
        def affmm(t, i=i):
            for ig in range(G):
                li = t.matmul(out=Pq[:, 0:NE1], lhsT=xn32[i % 2][:, ig * 128:(ig + 1) * 128],
                              rhs=s_wr[:, ig * NE1:(ig + 1) * NE1], start=(ig == 0), stop=(ig == G - 1))
            return li
        deps = [ld]
        if i >= 1:
            deps.append(last[("wte", i - 1)])
            deps.append(last[("sg", i - 1)])
            deps.append(last[("lcp", i - 1)])
        aff = p.step("tensor", deps, affmm)
        last[("aff", i)] = aff
        lcp = p.step("scalar", [aff], lambda s: s.copy(out=lg_sb[:], in_=Pq[:, 0:NE1]))
        last[("lcp", i)] = lcp
        sga = p.step("scalar", [lcp], lambda s: s.activation(out=sa_sb[:], in_=lg_sb[:],
                                                             func=AF.Sigmoid))
        last[("sg", i)] = sga
        mx = p.step("vector", [sga], lambda v: v.max(out=mx_sb[:], in_=sa_sb[:]))
        ms0 = p.step("vector", [mx], lambda v: v.memset(mx_sb[:, 2:8], -1.0))
        mr = p.step("vector", [ms0], lambda v: v.match_replace(
            out=mk_sb[:], in_to_replace=mx_sb[:], in_values=sa_sb[:], imm_value=0.0))
        wt = p.step("vector", [mr], lambda v: v.tensor_sub(
            out=wte_sb[:], in0=sa_sb[:], in1=mk_sb[:]))
        last[("wte", i)] = wt
        tp = p.step("tensor", [wt, const_done], lambda t, i=i: t.transpose(out=sc_ps[0:NE1, 0:128], in_=wte_sb[:], identity=s_iden[:]))
        ev = p.step("vector", [tp], lambda v, i=i: v.tensor_copy(
            out=wteT[:, 128 * i:128 * (i + 1)], in_=sc_ps[0:NE1, 0:128]))
        last[("wte", i)] = ev
    wcast = p.step("vector", [ev], lambda v: v.tensor_copy(out=wteTb[:], in_=wteT[:]))
    phA_done = wcast

    y_stores = []
    for hh in range(2):
        kl = []
        for h4 in range(4):
            h = hh * 4 + h4
            for e in range(NE1):
                deps = [last[("ymm", None)]] if (hh and e == 0 and h4 == 0) else NONE
                kl.append(p.dma("gpsimd", deps, lambda g, h=h, h4=h4, e=e: g.dma_start(
                    out=kf_sb[:, (h4 * NE1 + e) * 256:(h4 * NE1 + e) * 256 + 256],
                    in_=kfT[h, e])))
                kl.append(p.dma("gpsimd", deps, lambda g, h=h, h4=h4, e=e: g.dma_start(
                    out=vf_sb[:, (h4 * NE1 + e) * 256:(h4 * NE1 + e) * 256 + 256
                              ].rearrange("p (sc c) -> p sc c", sc=2),
                    in_=vf[h, e].rearrange("(sc p) c -> p sc c", p=128))))
        kv_done = kl[-1]

        for tc in range(NCH):
            kk = hh * NCH + tc
            deps = [const_done, phA_done]
            if kk >= 2:
                deps.append(last[("qmm", kk - 2)])
            xl = p.dma("sync", deps, lambda q, tc=tc, kk=kk: q.dma_start(
                out=xnb_sb[kk % 2].rearrange("p (g c) -> p g c", g=G),
                in_=xn_fmb.rearrange("(g p) t -> p g t", p=128)[:, :, 512 * tc:512 * (tc + 1)]))
            for og in range(G):
                bank = og % 2
                deps = [xl]
                if (kk, og) > (0, 1):
                    deps.append(last.get(("qev", bank)))
                def qmm(t, og=og, kk=kk, bank=bank):
                    for ig in range(G):
                        li = t.matmul(out=P[bank][:],
                                      lhsT=s_wmh[:, ig * D + 128 * og:ig * D + 128 * (og + 1)],
                                      rhs=xnb_sb[kk % 2][:, ig * 512:(ig + 1) * 512],
                                      start=(ig == 0), stop=(ig == G - 1))
                    return li
                qm = p.step("tensor", deps, qmm)
                last[("qmm", kk)] = qm
                qe = p.step("vector", [qm], lambda v, og=og, bank=bank: v.tensor_copy(
                    out=q_sb[:, og * 512:(og + 1) * 512], in_=P[bank][:]))
                last[("qev", bank)] = qe
            q_done = last[("qev", (G - 1) % 2)]

            wb_done = None
            for ep in range(NE1 // 2):
                deps2 = [phA_done]
                if ep >= 1:
                    deps2.append(last[("wbev", None)])
                elif kk >= 1:
                    deps2.append(last[("actm", None)])
                def wbmm2(t, tc=tc, ep=ep):
                    t.matmul(out=P[2][:], lhsT=s_esel[:, (2 * ep) * 128:(2 * ep + 1) * 128],
                             rhs=wteTb[:, 512 * tc:512 * (tc + 1)], start=True, stop=True)
                    return t.matmul(out=P[3][:], lhsT=s_esel[:, (2 * ep + 1) * 128:(2 * ep + 2) * 128],
                                    rhs=wteTb[:, 512 * tc:512 * (tc + 1)], start=True, stop=True)
                wm = p.step("tensor", deps2, wbmm2)
                def wbev(s, ep=ep):
                    s.copy(out=wb_sb[:, (2 * ep) * 512:(2 * ep + 1) * 512], in_=P[2][:])
                    return s.copy(out=wb_sb[:, (2 * ep + 1) * 512:(2 * ep + 2) * 512], in_=P[3][:])
                wb_done = p.step("scalar", [wm], wbev)
                last[("wbev", None)] = wb_done

            for h4 in range(4):
                h = hh * 4 + h4
                for e in range(NE1):
                    deps = [kv_done, q_done, wb_done]
                    if last.get(("gelu", None)) is not None:
                        deps.append(last[("gelu", None)])
                    def scmm(t, h4=h4, e=e, h=h):
                        base = (h4 * NE1 + e) * 256
                        for sc in range(2):
                            li = t.matmul(out=sc_ps[:, 512 * sc:512 * (sc + 1)],
                                          lhsT=kf_sb[:, base + 128 * sc:base + 128 * (sc + 1)],
                                          rhs=q_sb[:, h * 512:(h + 1) * 512],
                                          start=True, stop=True)
                        return li
                    sm = p.step("tensor", deps, scmm)
                    deps = [sm]
                    if last.get(("ymm", None)) is not None:
                        deps.append(last[("ymm", None)])
                    ge = p.step("scalar", deps, lambda s: s.activation(
                        out=act_sb[:], in_=sc_ps[:], func=AF.Gelu))
                    last[("gelu", None)] = ge
                    def actm(v, e=e):
                        v.tensor_mul(out=act_sb[:, 0:512], in0=act_sb[:, 0:512],
                                     in1=wb_sb[:, e * 512:(e + 1) * 512])
                        return v.tensor_mul(out=act_sb[:, 512:1024], in0=act_sb[:, 512:1024],
                                            in1=wb_sb[:, e * 512:(e + 1) * 512])
                    am = p.step("vector", [ge], actm)
                    last[("actm", None)] = am

                    deps = [am]
                    if e == 0 and last.get(("yev", None)) is not None:
                        deps.append(last[("yev", None)])
                    def ymm(t, h4=h4, e=e):
                        for sc in range(2):
                            li = t.matmul(out=P[4][:],
                                          lhsT=vf_sb[:, (h4 * NE1 + e) * 256 + 128 * sc:
                                                     (h4 * NE1 + e) * 256 + 128 * (sc + 1)],
                                          rhs=act_sb[:, 512 * sc:512 * (sc + 1)],
                                          start=(e == 0 and sc == 0), stop=(e == NE1 - 1 and sc == 1))
                        return li
                    ym = p.step("tensor", deps, ymm)
                    last[("ymm", None)] = ym
                ye = p.step("vector", [ym], lambda v, h4=h4: v.tensor_copy(
                    out=y_sb[:, h4 * 512:(h4 + 1) * 512], in_=P[4][:]))
                last[("yev", None)] = ye
            y_stores.append(p.dma("sync", [ye], lambda q, hh=hh, tc=tc: q.dma_start(
                out=y_d.rearrange("(g p) t -> p g t", p=128)[:, 4 * hh:4 * (hh + 1),
                                                            512 * tc:512 * (tc + 1)],
                in_=y_sb.rearrange("p (g c) -> p g c", g=4))))
    phB_done = y_stores[-1]

    st3 = []
    for tc in range(NCH):
        deps = [phB_done]
        if tc >= 2:
            deps.append(last[("gmm", tc - 2)])
        yl = p.dma("sync", deps, lambda q, tc=tc: q.dma_start(
            out=xnb_sb[tc % 2].rearrange("p (g c) -> p g c", g=G),
            in_=y_d.rearrange("(g p) t -> p g t", p=128)[:, :, 512 * tc:512 * (tc + 1)]))
        for tt in range(4):
            i = tc * 4 + tt
            deps = [yl]
            if i >= 1:
                deps.append(last[("gev", None)])
            def gmm(t, tc=tc, tt=tt):
                for od in range(2):
                    for ig in range(G):
                        li = t.matmul(out=P[od][:],
                                      lhsT=xnb_sb[tc % 2][:, ig * 512 + 128 * tt:ig * 512 + 128 * (tt + 1)],
                                      rhs=s_wmg[:, ig * D + 512 * od:ig * D + 512 * (od + 1)],
                                      start=(ig == 0), stop=(ig == G - 1))
                return li
            gm = p.step("tensor", deps, gmm)
            last[("gmm", tc)] = gm
            deps = [gm]
            if i >= 2:
                deps.append(st3[i - 2])
            gevs = p.step("scalar", deps, lambda s, i=i: (
                s.activation(out=go_sb[i % 2][:, 0:512], in_=P[0][:], func=AF.Silu),
                s.activation(out=go_sb[i % 2][:, 512:1024], in_=P[1][:], func=AF.Silu))[-1])
            last[("gev", None)] = gevs
            st3.append(p.dma("sync", [gevs], lambda q, i=i: q.dma_start(
                out=g1_tok[128 * i:128 * (i + 1), :], in_=go_sb[i % 2][:])))
    p.step("sync", [st3[-1]], lambda q: q.nop())
    return p.build()


_IDEN = np.eye(128, dtype=np.float32)
_PROGS = {}


def _prog(name, builder):
    if name not in _PROGS:
        _PROGS[name] = builder()
    return _PROGS[name]


def _run(name, builder, maps):
    nc = _prog(name, builder)
    return run_bass_kernel_spmd(nc, maps, core_ids=list(range(len(maps)))).results


def _f32c(a):
    return np.ascontiguousarray(np.asarray(a, dtype=np.float32))


def kernel(**inp):
    xs = _f32c(inp["L0_attn_x"])
    NCORE = xs.shape[0]

    a0 = {}
    for nm, key in [("WpT", "L0_attn_Wp"), ("WqT", "L0_attn_Wq"), ("WkT", "L0_attn_Wk"),
                    ("WvT", "L0_attn_Wv"), ("WoT", "L0_attn_Wo")]:
        a0[nm] = _f32c(np.asarray(inp[key]).T)
    for nm, key in [("bpc", "L0_attn_bp"), ("bqc", "L0_attn_bq"), ("bkc", "L0_attn_bk"),
                    ("bvc", "L0_attn_bv")]:
        a0[nm] = _f32c(_f32c(inp[key]).reshape(8, 128).T)
    a0["bo_row"] = _f32c(inp["L0_attn_bo"]).reshape(1, 1024)
    wsel = np.zeros((8, 128, 16), np.float32)
    wbc = np.zeros((8, 16, 128), np.float32)
    for g in range(8):
        for pp in range(128):
            h = 2 * g + pp // 64
            wsel[g, pp, h] = 1.0
            wbc[g, h, pp] = 1.0
    a0["wsel"] = wsel
    a0["wbc"] = wbc
    a0["ones1"] = np.ones((1, 128), np.float32)

    cen = _f32c(inp["L0_ffn_centroids"])
    lng = _f32c(inp["L0_ffn_ln_g"])
    lnb = _f32c(inp["L0_ffn_ln_b"])
    w1 = _f32c(inp["L0_ffn_w1"])
    b1 = _f32c(inp["L0_ffn_b1"])
    w2 = _f32c(inp["L0_ffn_w2"])
    b2 = _f32c(inp["L0_ffn_b2"])
    f0_base = {"cenT": _f32c(cen.T), "ones1": a0["ones1"]}
    f0_e = []
    for e in range(8):
        f0_e.append({
            "lg": _f32c(lng[e].reshape(8, 128).T),
            "lb": _f32c(lnb[e].reshape(8, 128).T),
            "b1c": _f32c(b1[e].reshape(32, 128).T),
            "b2_row": b2[e].reshape(1, 1024).copy(),
            "w1Te": _f32c(w1[e].T),
            "w2Te": _f32c(w2[e].T),
            "econst": np.full((128, 1), float(e), np.float32),
        })

    a1 = {
        "WrvkT": _f32c(np.asarray(inp["L1_attn_W_rvk"]).T),
        "WoutT": _f32c(np.asarray(inp["L1_attn_W_out"]).T),
        "tdc": _f32c(_f32c(inp["L1_attn_time_decay"]).reshape(8, 128).T),
        "tfc": _f32c(_f32c(inp["L1_attn_time_first"]).reshape(8, 128).T),
    }

    esel = np.zeros((16, 16 * 128), np.float32)
    for e in range(16):
        esel[e, e * 128:(e + 1) * 128] = 1.0
    f1 = {
        "wrT": _f32c(np.asarray(inp["L1_ffn_w_router"]).T),
        "wmhT": _f32c(np.asarray(inp["L1_ffn_w_mh"]).T),
        "wmgT": _f32c(np.asarray(inp["L1_ffn_w_mg"]).T),
        "kfT": _f32c(np.asarray(inp["L1_ffn_k_ffwd"]).transpose(0, 1, 3, 2)),
        "vf": _f32c(inp["L1_ffn_v_ffwd"]),
        "iota16": np.broadcast_to(np.arange(16, dtype=np.float32), (128, 16)).copy(),
        "esel": esel,
        "iden": _IDEN,
    }

    r = _run("ln0", lambda: build_ln(with_xhat=True),
             [{"x": np.ascontiguousarray(xs[c]), "iden": _IDEN} for c in range(NCORE)])
    xn_fmb = [r[c]["xn_fmb"] for c in range(NCORE)]
    xn_fm32 = [r[c]["xn_fm32"] for c in range(NCORE)]
    xn_tok = [r[c]["xn_tok"] for c in range(NCORE)]
    xh_tok = [r[c]["xh_tok"] for c in range(NCORE)]

    r = _run("attn0", build_attn0,
             [dict(a0, xn_fmb=xn_fmb[c]) for c in range(NCORE)])
    s_tok = [r[c]["s_tok"] for c in range(NCORE)]

    stril = np.zeros((128, 128), np.float32)
    for _p in range(128):
        stril[_p, _p + 1:] = 1.0
    rt = {"cenT": f0_base["cenT"], "strilT": stril,
          "iota8": np.broadcast_to(np.arange(8, dtype=np.float32), (128, 8)).copy()}
    r = _run("route0", build_route0,
             [dict(rt, xn_fm32=xn_fm32[c], xh_tok=xh_tok[c]) for c in range(NCORE)])
    xg = [r[c]["xg"] for c in range(NCORE)]
    slot_d = [r[c]["slot_d"] for c in range(NCORE)]
    a_d = [r[c]["a_d"] for c in range(NCORE)]

    lgt = np.zeros((128, 64), np.float32)
    lbt = np.zeros((128, 64), np.float32)
    b1t = np.zeros((128, 256), np.float32)
    for e in range(8):
        lgt[:, e * 8:(e + 1) * 8] = lng[e].reshape(8, 128).T
        lbt[:, e * 8:(e + 1) * 8] = lnb[e].reshape(8, 128).T
        b1t[:, e * 32:(e + 1) * 32] = b1[e].reshape(32, 128).T
    fs = {"w1T": _f32c(w1.transpose(0, 2, 1)), "w2T": _f32c(w2.transpose(0, 2, 1)),
          "lgt": lgt, "lbt": lbt, "b1t": b1t, "b2t": b2,
          "ones1": a0["ones1"], "idenb": _IDEN.astype(NBF)}
    r = _run("ffn0s", build_ffn0s, [dict(fs, xg=xg[c]) for c in range(NCORE)])
    z_slots = [r[c]["z_slots"] for c in range(NCORE)]

    r = _run("comb0s", build_comb0s,
             [{"z_slots": z_slots[c], "slot_d": slot_d[c], "a_d": a_d[c],
               "xn_tok": xn_tok[c], "s_tok": s_tok[c],
               "x": np.ascontiguousarray(xs[c])} for c in range(NCORE)])
    x1 = [r[c]["out"] for c in range(NCORE)]

    r = _run("ln1", lambda: build_ln(with_xhat=False),
             [{"x": x1[c], "iden": _IDEN} for c in range(NCORE)])
    xn1_fmb = [r[c]["xn_fmb"] for c in range(NCORE)]
    xn1_fm32 = [r[c]["xn_fm32"] for c in range(NCORE)]

    r = _run("attn1", build_attn1,
             [dict(a1, xn_fmb=xn1_fmb[c]) for c in range(NCORE)])
    s1_tok = [r[c]["s1_tok"] for c in range(NCORE)]

    r = _run("ffn1", build_ffn1,
             [dict(f1, xn_fmb=xn1_fmb[c], xn_fm32=xn1_fm32[c]) for c in range(NCORE)])
    g1_tok = [r[c]["g1_tok"] for c in range(NCORE)]

    r = _run("comb1", lambda: build_combine("final"),
             [{"g1_tok": g1_tok[c], "s_tok": s1_tok[c], "x": x1[c]} for c in range(NCORE)])
    out = np.stack([r[c]["out"] for c in range(NCORE)]).astype(np.float32)
    return out


CAP = 768
NST = 8 * CAP // 128
BIGIDX = 1 << 20


def build_route0():
    nc = bass.Bass("TRN2", target_bir_lowering=False, detect_race_conditions=False)
    xn_fm32 = nc.dram_tensor("xn_fm32", [D, T], F32, kind="ExternalInput").ap()
    xh_tok = nc.dram_tensor("xh_tok", [T, D], BF16, kind="ExternalInput").ap()
    cenT = nc.dram_tensor("cenT", [D, G], F32, kind="ExternalInput").ap()
    strilT = nc.dram_tensor("strilT", [128, 128], F32, kind="ExternalInput").ap()
    iota8 = nc.dram_tensor("iota8", [128, G], F32, kind="ExternalInput").ap()
    xg = nc.dram_tensor("xg", [8 * CAP, D], BF16, kind="ExternalOutput").ap()
    slot_d = nc.dram_tensor("slot_d", [128, NT], U32, kind="ExternalOutput").ap()
    a_d = nc.dram_tensor("a_d", [128, NT], F32, kind="ExternalOutput").ap()
    dbg_m = nc.dram_tensor("dbg_m", [128, G * NT], F32, kind="ExternalOutput").ap()
    dbg_c = nc.dram_tensor("dbg_c", [128, 2 * G], F32, kind="ExternalOutput").ap()

    s_cen = nc.alloc_sbuf_tensor("s_cen", [128, G * G], F32)
    s_stril = nc.alloc_sbuf_tensor("s_stril", [128, 128], F32)
    s_io8 = nc.alloc_sbuf_tensor("s_io8", [128, G], F32)
    xn32 = [nc.alloc_sbuf_tensor(f"xn32_{i}", [128, D], F32) for i in range(2)]
    lg_sb = nc.alloc_sbuf_tensor("lg_sb", [128, G], F32)
    sa_sb = nc.alloc_sbuf_tensor("sa_sb", [128, G], F32)
    mx_sb = nc.alloc_sbuf_tensor("mx_sb", [128, 8], F32)
    mr_sb = nc.alloc_sbuf_tensor("mr_sb", [128, G], F32)
    am_sb = nc.alloc_sbuf_tensor("am_sb", [128, G], F32)
    a_all = nc.alloc_sbuf_tensor("a_all", [128, NT], F32)
    mall = nc.alloc_sbuf_tensor("mall", [128, G * NT], F32)
    ones32 = nc.alloc_sbuf_tensor("ones32", [128, NT], F32)
    incl = nc.alloc_sbuf_tensor("incl", [128, G * NT], F32)
    cnt_sb = nc.alloc_sbuf_tensor("cnt_sb", [128, G], F32)
    offs_sb = nc.alloc_sbuf_tensor("offs_sb", [128, G], F32)
    rank = nc.alloc_sbuf_tensor("rank", [128, NT], F32)
    slot_f = nc.alloc_sbuf_tensor("slot_f", [128, NT], F32)
    slot_u = nc.alloc_sbuf_tensor("slot_u", [128, NT], U32)
    tmp_sb = nc.alloc_sbuf_tensor("tmp_sb", [128, NT], F32)
    xh_sb = [nc.alloc_sbuf_tensor(f"xh_sb{i}", [128, D], BF16) for i in range(2)]
    idxc = [nc.alloc_sbuf_tensor(f"idxc{i}", [128, 1], U32) for i in range(2)]
    Pq = nc.alloc_psum_tensor("Pq", [128, G], F32)
    Po = nc.alloc_psum_tensor("Po", [128, G], F32)

    p = Prog(nc)
    NONE = []
    cl = []
    cl.append(p.dma("gpsimd", NONE, lambda g: g.dma_start(
        out=s_cen.rearrange("p (i c) -> p i c", i=G), in_=cenT.rearrange("(i p) c -> p i c", p=128))))
    cl.append(p.dma("gpsimd", NONE, lambda g: g.dma_start(out=s_stril[:], in_=strilT)))
    cl.append(p.dma("gpsimd", NONE, lambda g: g.dma_start(out=s_io8[:], in_=iota8)))
    const_done = cl[-1]

    last = {}
    for i in range(NT):
        deps = [cl[0]]
        if i >= 2:
            deps.append(last[("aff", i - 2)])
        ld = p.dma("sync", deps, lambda q, i=i: q.dma_start(
            out=xn32[i % 2].rearrange("p (g c) -> p g c", g=G),
            in_=xn_fm32.rearrange("(g p) t -> p g t", p=128)[:, :, 128 * i:128 * (i + 1)]))
        def affmm(t, i=i):
            for ig in range(G):
                li = t.matmul(out=Pq[:], lhsT=xn32[i % 2][:, ig * 128:(ig + 1) * 128],
                              rhs=s_cen[:, ig * G:(ig + 1) * G], start=(ig == 0), stop=(ig == G - 1))
            return li
        deps = [ld]
        if i >= 1:
            deps.append(last[("lcp", i - 1)])
        aff = p.step("tensor", deps, affmm)
        last[("aff", i)] = aff
        lcp = p.step("scalar", [aff], lambda s: s.copy(out=lg_sb[:], in_=Pq[:]))
        last[("lcp", i)] = lcp
        sga = p.step("scalar", [lcp], lambda s: s.activation(out=sa_sb[:], in_=lg_sb[:], func=AF.Sigmoid))
        mx = p.step("vector", [sga] + ([last[("eid", i - 1)]] if i else []),
                    lambda v: v.max(out=mx_sb[:], in_=sa_sb[:]))
        ms0 = p.step("vector", [mx], lambda v: v.memset(mx_sb[:, 1:8], -1.0))
        mr = p.step("vector", [ms0], lambda v: v.match_replace(
            out=mr_sb[:], in_to_replace=mx_sb[:], in_values=sa_sb[:], imm_value=0.0))
        am = p.step("vector", [mr], lambda v, i=i: v.tensor_sub(out=am_sb[:], in0=sa_sb[:], in1=mr_sb[:]))
        a1 = p.step("vector", [am], lambda v, i=i: v.tensor_reduce(
            out=a_all[:, i:i + 1], in_=am_sb[:], op=OP.add, axis=AX.X))
        mk = p.step("vector", [a1], lambda v, i=i: v.tensor_scalar(
            out=mall[:, i:G * NT:NT], in0=am_sb[:], scalar1=0.0, scalar2=None, op0=OP.is_gt))
        eid = p.step("vector", [mk, const_done], lambda v, i=i: v.tensor_mul(
            out=mr_sb[:], in0=mall[:, i:G * NT:NT], in1=s_io8[:]))
        last[("eid", i)] = eid
    routing_done = last[("eid", NT - 1)]

    mo = p.step("vector", [routing_done], lambda v: v.memset(ones32[:], 1.0))
    scans = []
    for e in range(G):
        scans.append(p.step("vector", [mo], lambda v, e=e: v.tensor_tensor_scan(
            out=incl[:, e * NT:(e + 1) * NT], data0=ones32[:], data1=mall[:, e * NT:(e + 1) * NT],
            initial=0.0, op0=OP.mult, op1=OP.add)))
    cntc = p.step("vector", [scans[-1]], lambda v: v.tensor_copy(
        out=cnt_sb[:], in_=incl[:, NT - 1:G * NT:NT]))
    offm = p.step("tensor", [cntc, const_done], lambda t: t.matmul(
        out=Po[:], lhsT=s_stril[:], rhs=cnt_sb[:], start=True, stop=True))
    offe = p.step("scalar", [offm], lambda s: s.copy(out=offs_sb[:], in_=Po[:]))
    sl0 = p.step("vector", [offe], lambda v: v.memset(slot_f[:], 0.0))
    prev = sl0
    for e in range(G):
        s_a1 = p.step("vector", [prev], lambda v, e=e: v.scalar_tensor_tensor(
            out=rank[:], in0=incl[:, e * NT:(e + 1) * NT],
            scalar=offs_sb[:, e:e + 1], op0=OP.add, op1=OP.subtract,
            in1=mall[:, e * NT:(e + 1) * NT]))
        s_a2 = p.step("vector", [s_a1], lambda v, e=e: v.tensor_scalar_add(
            out=tmp_sb[:], in0=rank[:], scalar1=float(e * CAP)))
        s_a3 = p.step("vector", [s_a2], lambda v, e=e: v.tensor_scalar(
            out=ones32[:], in0=rank[:], scalar1=float(CAP),
            scalar2=float(BIGIDX), op0=OP.is_ge, op1=OP.mult))
        s_a4 = p.step("vector", [s_a3], lambda v, e=e: v.tensor_add(
            out=tmp_sb[:], in0=tmp_sb[:], in1=ones32[:]))
        s_a5 = p.step("vector", [s_a4], lambda v, e=e: v.tensor_mul(
            out=tmp_sb[:], in0=tmp_sb[:], in1=mall[:, e * NT:(e + 1) * NT]))
        prev = p.step("vector", [s_a5], lambda v, e=e: v.tensor_add(
            out=slot_f[:], in0=slot_f[:], in1=tmp_sb[:]))
    slu = p.step("vector", [prev], lambda v: v.tensor_copy(out=slot_u[:], in_=slot_f[:]))

    regcell = {}
    def mkreg(g):
        regcell["bc"] = g.to_reg(8 * CAP - 1)
        return g.dma_start(out=xh_sb[0][:], in_=xh_tok[0:128, :])
    first_ld = p.dma("gpsimd", [slu], mkreg)

    sc_prev = None
    for i in range(NT):
        if i == 0:
            xl = first_ld
        else:
            deps = [slu]
            if i >= 2:
                deps.append(last[("sct", i - 2)])
            xl = p.dma("gpsimd", deps, lambda g, i=i: g.dma_start(
                out=xh_sb[i % 2][:], in_=xh_tok[128 * i:128 * (i + 1), :]))
        cpi = p.step("vector", [xl, slu], lambda v, i=i: v.tensor_copy(
            out=idxc[i % 2][:], in_=slot_u[:, i:i + 1]))
        s1 = p.dma("gpsimd", [cpi], lambda g, i=i: g.indirect_dma_start(
            out=xg, out_offset=IndirectOffsetOnAxis(ap=idxc[i % 2][:], axis=0),
            in_=xh_sb[i % 2][:], in_offset=None,
            bounds_check=regcell["bc"], oob_is_err=False))
        last[("sct", i)] = s1
        sc_prev = s1

    ast = p.dma("sync", [routing_done], lambda q: q.dma_start(out=a_d, in_=a_all[:]))
    ast2 = p.dma("sync", [slu], lambda q: q.dma_start(out=slot_d, in_=slot_u[:]))
    d1 = p.dma("sync", [slu], lambda q: q.dma_start(out=dbg_m, in_=incl[:]))
    dcc = p.step("vector", [slu], lambda v: (v.tensor_copy(out=rank[:, 0:G], in_=cnt_sb[:]),
                                             v.tensor_copy(out=rank[:, G:2 * G], in_=offs_sb[:]))[-1])
    d2 = p.dma("sync", [dcc], lambda q: q.dma_start(out=dbg_c, in_=rank[:, 0:2 * G]))
    p.step("sync", [sc_prev, ast, ast2, d1, d2], lambda q: q.nop())
    return p.build()


def build_ffn0s():
    nc = bass.Bass("TRN2", target_bir_lowering=False, detect_race_conditions=False)
    xg = nc.dram_tensor("xg", [8 * CAP, D], BF16, kind="ExternalInput").ap()
    w1T = nc.dram_tensor("w1T", [G, D, 4096], F32, kind="ExternalInput").ap()
    w2T = nc.dram_tensor("w2T", [G, 4096, D], F32, kind="ExternalInput").ap()
    lgt = nc.dram_tensor("lgt", [128, G * G], F32, kind="ExternalInput").ap()
    lbt = nc.dram_tensor("lbt", [128, G * G], F32, kind="ExternalInput").ap()
    b1t = nc.dram_tensor("b1t", [128, G * 32], F32, kind="ExternalInput").ap()
    b2t = nc.dram_tensor("b2t", [G, D], F32, kind="ExternalInput").ap()
    ones1 = nc.dram_tensor("ones1", [1, 128], F32, kind="ExternalInput").ap()
    idenb = nc.dram_tensor("idenb", [128, 128], BF16, kind="ExternalInput").ap()
    z_slots = nc.dram_tensor("z_slots", [8 * CAP, D], BF16, kind="ExternalOutput").ap()

    s_lg = nc.alloc_sbuf_tensor("s_lg", [128, G * G], F32)
    s_lb = nc.alloc_sbuf_tensor("s_lb", [128, G * G], F32)
    s_b1 = nc.alloc_sbuf_tensor("s_b1", [128, G * 32], F32)
    s_b2 = nc.alloc_sbuf_tensor("s_b2", [1, G * D], BF16)
    s_ones = nc.alloc_sbuf_tensor("s_ones", [1, 128], BF16)
    s_idb = nc.alloc_sbuf_tensor("s_idb", [128, 128], BF16)
    w1_sb = nc.alloc_sbuf_tensor("w1_sb", [128, G * F_HALF], BF16)
    w2_sb = nc.alloc_sbuf_tensor("w2_sb", [128, 32 * D], BF16)
    xgt_sb = [nc.alloc_sbuf_tensor(f"xgt{i}", [128, D], BF16) for i in range(2)]
    xe_sb = nc.alloc_sbuf_tensor("xe_sb", [128, G * CAP], BF16)
    h_sb = nc.alloc_sbuf_tensor("h_sb", [128, 32 * CAP], BF16)
    z_sb = [nc.alloc_sbuf_tensor(f"z_sb{i}", [128, D], BF16) for i in range(2)]
    Ptp = [nc.alloc_psum_tensor(f"Ptp{i}", [128, 128], BF16) for i in range(2)]
    P = [nc.alloc_psum_tensor(f"Pz{i}", [128, 512], F32) for i in range(4)]

    p = Prog(nc)
    NONE = []
    cl = []
    for sb, dr in [(s_lg, lgt), (s_lb, lbt), (s_b1, b1t), (s_ones, ones1), (s_idb, idenb)]:
        cl.append(p.dma("gpsimd", NONE, lambda g, sb=sb, dr=dr: g.dma_start(out=sb[:], in_=dr)))
    cl.append(p.dma("gpsimd", NONE, lambda g: g.dma_start(
        out=s_b2.rearrange("p (e c) -> p e c", e=G), in_=b2t.rearrange("e c -> () e c"))))
    const_done = cl[-1]

    last = {}
    CHU = [(0, 512), (512, 256)]
    st_all = []
    for e in range(G):
        w2l = []
        for fk in range(32):
            w2l.append(p.dma("gpsimd", [last[("zmm", e - 1)]] if e else NONE,
                             lambda g, fk=fk, e=e: g.dma_start(
                out=w2_sb[:, fk * D:(fk + 1) * D], in_=w2T[e, 128 * fk:128 * (fk + 1), :])))
        w2_done = w2l[-1]

        tps = []
        for st in range(6):
            k = e * 6 + st
            deps = [const_done]
            if k >= 2:
                deps.append(last[("xemm", k - 2)])
            xl = p.dma("sync", deps, lambda q, e=e, st=st, k=k: q.dma_start(
                out=xgt_sb[k % 2][:], in_=xg[e * CAP + 128 * st:e * CAP + 128 * (st + 1), :]))
            for g_ in range(G):
                kk = k * G + g_
                deps = [xl, const_done]
                if kk >= 2:
                    deps.append(last[("xev", kk - 2)])
                tp = p.step("tensor", deps, lambda t, k=k, g_=g_, kk=kk: t.transpose(
                    out=Ptp[kk % 2][:], in_=xgt_sb[k % 2][:, 128 * g_:128 * (g_ + 1)],
                    identity=s_idb[:]))
                last[("xemm", k)] = tp
                ev = p.step("vector", [tp], lambda v, e=e, st=st, g_=g_, kk=kk: v.tensor_scalar(
                    out=xe_sb[:, g_ * CAP + 128 * st:g_ * CAP + 128 * (st + 1)], in0=Ptp[kk % 2][:],
                    scalar1=s_lg[:, e * G + g_:e * G + g_ + 1],
                    scalar2=s_lb[:, e * G + g_:e * G + g_ + 1], op0=OP.mult, op1=OP.add))
                last[("xev", kk)] = ev
                tps.append(ev)
        xe_done = tps[-1]

        hlast = None
        for fh in range(2):
            w1l = []
            for ig in range(G):
                deps = [last[("hmm", e, fh - 1)]] if fh else ([last[("hmm", e - 1, 1)]] if e else NONE)
                w1l.append(p.dma("gpsimd", deps, lambda g, ig=ig, fh=fh, e=e: g.dma_start(
                    out=w1_sb[:, ig * F_HALF:(ig + 1) * F_HALF],
                    in_=w1T[e, 128 * ig:128 * (ig + 1), F_HALF * fh:F_HALF * (fh + 1)])))
            w1_done = w1l[-1]
            for fg in range(NFG):
                fga = fh * NFG + fg
                for ci, (c0, cw) in enumerate(CHU):
                    bank = (fg * 2 + ci) % 2
                    deps = [xe_done, w1_done]
                    if (fh, fg, ci) != (0, 0, 0):
                        deps.append(last.get(("hev", bank)))
                    elif e > 0:
                        deps.append(last[("zev", None)])
                    def hmm(t, fg=fg, c0=c0, cw=cw, bank=bank):
                        for ig in range(G):
                            li = t.matmul(out=P[bank][:, 0:cw],
                                          lhsT=w1_sb[:, ig * F_HALF + 128 * fg:ig * F_HALF + 128 * (fg + 1)],
                                          rhs=xe_sb[:, ig * CAP + c0:ig * CAP + c0 + cw],
                                          start=(ig == 0), stop=(ig == G - 1))
                        return li
                    hm = p.step("tensor", deps, hmm)
                    last[("hmm", e, fh)] = hm
                    he = p.step("scalar", [hm], lambda s, fga=fga, c0=c0, cw=cw, bank=bank, e=e: s.activation(
                        out=h_sb[:, fga * CAP + c0:fga * CAP + c0 + cw], in_=P[bank][:, 0:cw],
                        func=AF.Relu, bias=s_b1[:, e * 32 + fga:e * 32 + fga + 1]))
                    last[("hev", bank)] = he
                    hlast = he

        for st in range(6):
            i = e * 6 + st
            deps = [hlast, w2_done]
            if st >= 1 or e > 0:
                deps.append(last[("zev", None)])
            def zmm(t, st=st, e=e):
                for od in range(2):
                    for fk in range(32):
                        t.matmul(out=P[2 + od][:],
                                 lhsT=h_sb[:, fk * CAP + 128 * st:fk * CAP + 128 * (st + 1)],
                                 rhs=w2_sb[:, fk * D + 512 * od:fk * D + 512 * (od + 1)],
                                 start=(fk == 0), stop=False)
                    li = t.matmul(out=P[2 + od][:], lhsT=s_ones[:],
                                  rhs=s_b2[:, e * D + 512 * od:e * D + 512 * (od + 1)],
                                  start=False, stop=True)
                return li
            zm = p.step("tensor", deps, zmm)
            last[("zmm", e)] = zm
            deps = [zm]
            if len(st_all) >= 2:
                deps.append(st_all[-2])
            ze = p.step("vector", deps, lambda v, i=i: (
                v.tensor_copy(out=z_sb[i % 2][:, 0:512], in_=P[2][:]),
                v.tensor_copy(out=z_sb[i % 2][:, 512:1024], in_=P[3][:]))[-1])
            last[("zev", None)] = ze
            st_all.append(p.dma("sync", [ze], lambda q, e=e, st=st, i=i: q.dma_start(
                out=z_slots[e * CAP + 128 * st:e * CAP + 128 * (st + 1), :], in_=z_sb[i % 2][:])))
    p.step("sync", [st_all[-1]], lambda q: q.nop())
    return p.build()


def build_comb0s():
    nc = bass.Bass("TRN2", target_bir_lowering=False, detect_race_conditions=False)
    z_slots = nc.dram_tensor("z_slots", [8 * CAP, D], BF16, kind="ExternalInput").ap()
    slot_d = nc.dram_tensor("slot_d", [128, NT], U32, kind="ExternalInput").ap()
    a_d = nc.dram_tensor("a_d", [128, NT], F32, kind="ExternalInput").ap()
    xn_tok = nc.dram_tensor("xn_tok", [T, D], F32, kind="ExternalInput").ap()
    s_tok = nc.dram_tensor("s_tok", [T, D], F32, kind="ExternalInput").ap()
    x = nc.dram_tensor("x", [T, D], F32, kind="ExternalInput").ap()
    out = nc.dram_tensor("out", [T, D], F32, kind="ExternalOutput").ap()

    s_slot = nc.alloc_sbuf_tensor("s_slot", [128, NT], U32)
    s_a = nc.alloc_sbuf_tensor("s_a", [128, NT], F32)
    idxc = [nc.alloc_sbuf_tensor(f"idxc{i}", [128, 1], U32) for i in range(2)]
    zb = [nc.alloc_sbuf_tensor(f"zb{i}", [128, D], BF16) for i in range(2)]
    xnb = [nc.alloc_sbuf_tensor(f"xnb{i}", [128, D], F32) for i in range(2)]
    sb_ = [nc.alloc_sbuf_tensor(f"sb_{i}", [128, D], F32) for i in range(2)]
    xb = [nc.alloc_sbuf_tensor(f"xb{i}", [128, D], F32) for i in range(2)]
    gb = [nc.alloc_sbuf_tensor(f"gb{i}", [128, D], F32) for i in range(2)]

    p = Prog(nc)
    NONE = []
    c1 = p.dma("sync", NONE, lambda q: q.dma_start(out=s_slot[:], in_=slot_d))
    c2 = p.dma("sync", NONE, lambda q: q.dma_start(out=s_a[:], in_=a_d))
    regcell = {}
    rg = p.dma("gpsimd", NONE, lambda g: (regcell.__setitem__("bc", g.to_reg(8 * CAP - 1)),
                                          g.dma_start(out=idxc[0][:], in_=slot_d[:, 0:1],
                                                      allow_slow_non_contiguous=True))[-1])
    last = {}
    st = []
    for i in range(NT):
        deps = [c1]
        if i >= 2:
            deps.append(last[("g", i - 2)])
        cp = p.step("vector", deps + [c2], lambda v, i=i: v.tensor_copy(
            out=idxc[i % 2][:], in_=s_slot[:, i:i + 1]))
        mz = p.step("vector", [cp], lambda v, i=i: v.memset(zb[i % 2][:], 0.0))
        gz = p.dma("gpsimd", [mz, rg], lambda g, i=i: g.indirect_dma_start(
            out=zb[i % 2][:], out_offset=None, in_=z_slots,
            in_offset=IndirectOffsetOnAxis(ap=idxc[i % 2][:], axis=0),
            bounds_check=regcell["bc"], oob_is_err=False))
        lds = []
        for nm, buf in [("xn", xnb), ("s", sb_), ("x", xb)]:
            dd = [st[i - 2]] if i >= 2 else []
            src_ = {"xn": xn_tok, "s": s_tok, "x": x}[nm]
            lds.append(p.dma("sync", dd, lambda q, i=i, src_=src_, buf=buf: q.dma_start(
                out=buf[i % 2][:], in_=src_[128 * i:128 * (i + 1), :])))
        def comb(v, i=i):
            v.scalar_tensor_tensor(out=gb[i % 2][:], in0=zb[i % 2][:], scalar=s_a[:, i:i + 1],
                                   op0=OP.mult, op1=OP.add, in1=xnb[i % 2][:])
            return v.engine_nop()
        cmb = p.step("vector", [gz] + lds, lambda v, i=i: v.scalar_tensor_tensor(
            out=gb[i % 2][:], in0=zb[i % 2][:], scalar=s_a[:, i:i + 1],
            op0=OP.mult, op1=OP.add, in1=xnb[i % 2][:]))
        last[("g", i)] = cmb
        sil = p.step("scalar", [cmb], lambda s, i=i: s.activation(
            out=gb[i % 2][:], in_=gb[i % 2][:], func=AF.Silu))
        fin = p.step("vector", [sil], lambda v, i=i: (
            v.tensor_mul(out=gb[i % 2][:], in0=gb[i % 2][:], in1=sb_[i % 2][:]),
            v.tensor_add(out=gb[i % 2][:], in0=gb[i % 2][:], in1=xb[i % 2][:]))[-1])
        st.append(p.dma("sync", [fin], lambda q, i=i: q.dma_start(
            out=out[128 * i:128 * (i + 1), :], in_=gb[i % 2][:])))
    p.step("sync", [st[-1]], lambda q: q.nop())
    return p.build()




class Arena:
    def __init__(self, nc, base, top):
        self.nc = nc
        self.base = base
        self.top = top
        self.off = base
        self.n = 0

    def reset(self):
        self.off = self.base

    def sb(self, name, shape, dtype):
        import math as _m
        bpe = {F32: 4, BF16: 2, U32: 4}[dtype]
        nbytes = shape[1] * bpe if len(shape) == 2 else int(np.prod(shape[1:])) * bpe
        off = (self.off + 31) // 32 * 32
        assert off + nbytes <= self.top, f"arena overflow {name} {off}+{nbytes}>{self.top}"
        self.n += 1
        t = self.nc.alloc_sbuf_tensor_at(f"A{self.n}_{name}", shape, dtype, offset=off)
        self.off = off + nbytes
        return t


def stage_fence(p, nc):
    lasts = {}
    for s, deps, emit in p.items:
        lasts[(s.eng, s.dma)] = s
    return p.step("sync", list(lasts.values()), lambda q: q.nop())


def emit_ln(nc, p, A, PS, base, *, x, iden_sb, xn_fm32, xn_fmb, xn_tok, with_xhat,
            xh_fmb=None, xh_tok=None, tag=""):
    sx = [A.sb(f"{tag}sx{i}", [128, D], F32) for i in range(2)]
    s_sq = A.sb(f"{tag}s_sq", [128, D], F32)
    smc = A.sb(f"{tag}smc", [128, NT], F32)
    ssc = A.sb(f"{tag}ssc", [128, NT], F32)
    mean = A.sb(f"{tag}mean", [128, NT], F32)
    rstd = A.sb(f"{tag}rstd", [128, NT], F32)
    cc = A.sb(f"{tag}cc", [128, NT], F32)
    tmp = [A.sb(f"{tag}tmp{i}", [128, NT], F32) for i in range(2)]
    eps_col = A.sb(f"{tag}eps", [128, 2], F32)
    sxn = [A.sb(f"{tag}sxn{i}", [128, D], F32) for i in range(2)]
    sxh = [A.sb(f"{tag}sxh{i}", [128, D], F32) for i in range(2)]
    tf32 = [A.sb(f"{tag}tf32_{i}", [128, D], F32) for i in range(2)]
    tfb = [A.sb(f"{tag}tfb_{i}", [128, D], BF16) for i in range(2)]
    thb = [A.sb(f"{tag}thb_{i}", [128, D], BF16) for i in range(2)]

    last = {}
    stat = []
    for i in range(NT):
        deps = list(base)
        if i >= 2:
            deps.append(last[("st", i - 2)])
        ld = p.dma("sync", deps, lambda q, i=i: q.dma_start(
            out=sx[i % 2][:], in_=x[128 * i:128 * (i + 1), :]))
        sq = p.step("scalar", [ld], lambda s, i=i: s.activation(
            out=s_sq[:], in_=sx[i % 2][:], func=AF.Square, accum_out=ssc[:, i:i + 1]))
        rd = p.step("vector", [ld], lambda v, i=i: v.tensor_reduce(
            out=smc[:, i:i + 1], in_=sx[i % 2][:], op=OP.add, axis=AX.X))
        f = p.step("sync", [sq, rd], lambda q: q.nop())
        last[("st", i)] = f
        stat.append(f)

    ms = p.step("vector", base, lambda v: v.memset(eps_col[:, 0:1], EPS))
    ms2 = p.step("vector", [ms], lambda v: v.memset(eps_col[:, 1:2], EPS * EPS))
    b1_ = p.step("vector", [stat[-1], ms2], lambda v: (
        v.tensor_scalar_mul(out=mean[:], in0=smc[:], scalar1=1.0 / D),
        v.tensor_scalar_mul(out=tmp[0][:], in0=ssc[:], scalar1=1.0 / D))[-1])
    b2_ = p.step("vector", [b1_], lambda v: v.tensor_mul(out=tmp[1][:], in0=mean[:], in1=mean[:]))
    b3_ = p.step("vector", [b2_], lambda v: v.tensor_sub(out=tmp[0][:], in0=tmp[0][:], in1=tmp[1][:]))
    b4_ = p.step("scalar", [b3_], lambda s: s.activation(
        out=tmp[1][:], in_=tmp[0][:], func=AF.Ln, bias=eps_col[:, 0:1]))
    b5_ = p.step("scalar", [b4_], lambda s: s.activation(out=rstd[:], in_=tmp[1][:], func=AF.Exp, scale=-0.5))
    if with_xhat:
        b6_ = p.step("scalar", [b5_], lambda s: s.activation(
            out=tmp[1][:], in_=tmp[0][:], func=AF.Ln, bias=eps_col[:, 1:2], scale=1.0 + EPS))
        b7_ = p.step("scalar", [b6_], lambda s: s.activation(out=cc[:], in_=tmp[1][:], func=AF.Exp, scale=-0.5))
        cols_done = b7_
    else:
        cols_done = b5_

    for i in range(NT):
        deps = [cols_done]
        if i >= 2:
            deps.append(last[("nrm", i - 2)])
        ld = p.dma("sync", deps, lambda q, i=i: q.dma_start(
            out=sx[i % 2][:], in_=x[128 * i:128 * (i + 1), :]))
        nr = p.step("vector", [ld] + ([last[("tpx", i - 2)], last[("tph", i - 2)],
                                       last[("tokst", i - 2)]] if i >= 2 else []),
                    lambda v, i=i: v.tensor_scalar(
            out=sxn[i % 2][:], in0=sx[i % 2][:], scalar1=mean[:, i:i + 1],
            scalar2=rstd[:, i:i + 1], op0=OP.subtract, op1=OP.mult))
        last[("nrm", i)] = nr
        if with_xhat:
            nh = p.step("vector", [nr], lambda v, i=i: v.tensor_scalar(
                out=sxh[i % 2][:], in0=sx[i % 2][:], scalar1=mean[:, i:i + 1],
                scalar2=cc[:, i:i + 1], op0=OP.subtract, op1=OP.mult))
        else:
            nh = nr
        st1 = p.dma("gpsimd", [nr], lambda q, i=i: q.dma_start(
            out=xn_tok[128 * i:128 * (i + 1), :], in_=sxn[i % 2][:]))
        if with_xhat:
            st2 = p.dma("gpsimd", [nh], lambda q, i=i: q.dma_start(
                out=xh_tok[128 * i:128 * (i + 1), :], in_=sxh[i % 2][:]))
        else:
            st2 = st1
        last[("tokst", i)] = st2
        evs = []
        for g in range(G):
            k = i * G + g
            deps = [nr, base[0]]
            if k >= 2:
                deps.append(last[("evx", k - 2)])
            tp = p.step("tensor", deps, lambda t, i=i, g=g, k=k: t.transpose(
                out=PS[k % 2][:, 0:128], in_=sxn[i % 2][:, 128 * g:128 * (g + 1)], identity=iden_sb[:]))
            last[("tpx", i)] = tp
            ev = p.step("vector", [tp] + ([last[("fmst", i - 2)]] if i >= 2 and g == 0 else []),
                        lambda v, i=i, g=g, k=k: (
                v.tensor_copy(out=tf32[i % 2][:, 128 * g:128 * (g + 1)], in_=PS[k % 2][:, 0:128]),
                v.tensor_copy(out=tfb[i % 2][:, 128 * g:128 * (g + 1)], in_=PS[k % 2][:, 0:128]))[-1])
            last[("evx", k)] = ev
            evs.append(ev)
            if with_xhat and xh_fmb is not None:
                deps = [nh, base[0]]
                if k >= 2:
                    deps.append(last[("evh", k - 2)])
                tph = p.step("tensor", deps, lambda t, i=i, g=g, k=k: t.transpose(
                    out=PS[2 + k % 2][:, 0:128], in_=sxh[i % 2][:, 128 * g:128 * (g + 1)],
                    identity=iden_sb[:]))
                last[("tph", i)] = tph
                evh = p.step("scalar", [tph] + ([last[("fmst", i - 2)]] if i >= 2 and g == 0 else []),
                             lambda s, i=i, g=g, k=k: s.copy(
                    out=thb[i % 2][:, 128 * g:128 * (g + 1)], in_=PS[2 + k % 2][:, 0:128]))
                last[("evh", k)] = evh
                evs.append(evh)
            else:
                last[("tph", i)] = tp
        f1 = p.dma("sync", [evs[-1]] + evs[-3:], lambda q, i=i: q.dma_start(
            out=xn_fm32.rearrange("(g p) t -> p g t", p=128)[:, :, 128 * i:128 * (i + 1)],
            in_=tf32[i % 2].rearrange("p (g c) -> p g c", g=G)))
        f2 = p.dma("sync", [f1], lambda q, i=i: q.dma_start(
            out=xn_fmb.rearrange("(g p) t -> p g t", p=128)[:, :, 128 * i:128 * (i + 1)],
            in_=tfb[i % 2].rearrange("p (g c) -> p g c", g=G)))
        if with_xhat and xh_fmb is not None:
            f3 = p.dma("sync", [f2], lambda q, i=i: q.dma_start(
                out=xh_fmb.rearrange("(g p) t -> p g t", p=128)[:, :, 128 * i:128 * (i + 1)],
                in_=thb[i % 2].rearrange("p (g c) -> p g c", g=G)))
        else:
            f3 = f2
        last[("fmst", i)] = f3


def emit_attn0(nc, p, A, PS, PSPA, PSPB, base, *, xn_fmb, wts, bcols, bo_row,
               wsel, wbc, ones1, s_tok, xp_d):

    W = {nm: A.sb("s_" + nm, [128, G * D], BF16) for nm in wts}
    B = {nm: A.sb("s_" + nm, [128, G], F32) for nm in bcols}
    s_bo = A.sb("s_bo", [1, D], BF16)
    s_ones = A.sb("s_ones", [1, 128], BF16)
    s_wsel = A.sb("s_wsel", [128, G * 16], BF16)
    s_wbc = A.sb("s_wbc", [16, G * 128], BF16)
    xn_sb = [A.sb(f"xn_sb{i}", [128, G * 512], BF16) for i in range(2)]
    xp_sb = [A.sb(f"xp_sb{i}", [128, G * 512], BF16) for i in range(2)]
    w_sb = [A.sb(f"w_sb{i}", [128, G * 514], BF16) for i in range(2)]
    q_sb = A.sb("q_sb", [128, G * 512], BF16)
    k_sb = A.sb("k_sb", [128, G * 514], BF16)
    v_sb = A.sb("v_sb", [128, G * 514], BF16)
    qk_sb = A.sb("qk_sb", [128, 3 * 512], BF16)
    sm_sb = A.sb("sm_sb", [16, 512], F32)
    e_sb = A.sb("e_sb", [16, 3 * 512], F32)
    ex_sb = A.sb("ex_sb", [16, 3 * 512], F32)
    den_sb = A.sb("den_sb", [16, 512], F32)
    lden_sb = A.sb("lden_sb", [16, 512], F32)
    a_sb = A.sb("a_sb", [16, 3 * 512], BF16)
    ab_sb = A.sb("ab_sb", [128, 3 * 512], BF16)
    av_sb = A.sb("av_sb", [128, G * 512], BF16)
    so_sb = [A.sb(f"so_sb{i}", [128, D], F32) for i in range(2)]


    wl = []
    for nm in ["WpT", "WqT", "WkT", "WvT", "WoT"]:
        for ig in range(G):
            wl.append(p.dma("gpsimd", list(base), lambda g, nm=nm, ig=ig: g.dma_start(
                out=W[nm][:, ig * D:(ig + 1) * D], in_=wts[nm][128 * ig:128 * (ig + 1), :])))
    for nm in ["bpc", "bqc", "bkc", "bvc"]:
        wl.append(p.dma("gpsimd", list(base), lambda g, nm=nm: g.dma_start(out=B[nm][:], in_=bcols[nm])))
    wl.append(p.dma("gpsimd", list(base), lambda g: g.dma_start(out=s_bo[:], in_=bo_row)))
    wl.append(p.dma("gpsimd", list(base), lambda g: g.dma_start(out=s_ones[:], in_=ones1)))
    wl.append(p.dma("gpsimd", list(base), lambda g: g.dma_start(
        out=s_wsel.rearrange("p (g c) -> p g c", g=G), in_=wsel.rearrange("g p c -> p g c"))))
    wl.append(p.dma("gpsimd", list(base), lambda g: g.dma_start(
        out=s_wbc.rearrange("p (g c) -> p g c", g=G), in_=wbc.rearrange("g p c -> p g c"))))
    wp_done = wl[7]
    w_all = wl[-1]

    xp_store = []
    ev_prev = {}
    xn_ld = []
    for tc in range(NCH):
        deps = [xp_store[tc - 2]] if tc >= 2 else []
        if tc >= 2:
            deps.append(ev_prev[("xn", tc - 2)])
        xn_ld.append(p.dma("sync", deps, lambda q, tc=tc: q.dma_start(
            out=xn_sb[tc % 2].rearrange("p (g c) -> p g c", g=G),
            in_=xn_fmb.rearrange("(g p) t -> p g t", p=128)[:, :, 512 * tc:512 * (tc + 1)])))
        evs = []
        for og in range(G):
            bank = og % 2
            deps = [xn_ld[tc], wp_done]
            if (tc, og) > (0, 1):
                deps.append(ev_prev[("p1", bank)])
            if og == 0 and tc >= 2:
                deps.append(xp_store[tc - 2])
            def mm(t, tc=tc, og=og, bank=bank):
                for ig in range(G):
                    last = t.matmul(out=PS[bank][:],
                                    lhsT=W["WpT"][:, ig * D + 128 * og:ig * D + 128 * (og + 1)],
                                    rhs=xn_sb[tc % 2][:, ig * 512:(ig + 1) * 512],
                                    start=(ig == 0), stop=(ig == G - 1))
                return last
            ms = p.step("tensor", deps, mm)
            ev = p.step("vector", [ms, w_all], lambda v, tc=tc, og=og, bank=bank: v.tensor_scalar_add(
                out=xp_sb[tc % 2][:, og * 512:(og + 1) * 512], in0=PS[bank][:],
                scalar1=B["bpc"][:, og:og + 1]))
            ev_prev[("p1", bank)] = ev
            evs.append(ev)
        ev_prev[("xn", tc)] = ms
        xp_store.append(p.dma("sync", [evs[-1]], lambda q, tc=tc: q.dma_start(
            out=xp_d.rearrange("(g p) t -> p g t", p=128)[:, :, 512 * tc:512 * (tc + 1)],
            in_=xp_sb[tc % 2].rearrange("p (g c) -> p g c", g=G))))
    xp_all = xp_store[-1]

    last = {}
    st_steps = []
    for tc in range(NCH):
        wnd = []
        deps = [xp_all]
        if tc >= 2:
            deps.append(last[("qkv_mm", tc - 2)])
        lo = 512 * tc - 1
        for g in range(G):
            if tc == 0:
                p.dma("sync", deps, lambda q, g=g: q.dma_start(
                    out=w_sb[0][:, g * 514:g * 514 + 1], in_=xp_d[128 * g:128 * (g + 1), T - 1:T], allow_slow_non_contiguous=True))
                wnd.append(p.dma("sync", deps, lambda q, g=g: q.dma_start(
                    out=w_sb[0][:, g * 514 + 1:(g + 1) * 514], in_=xp_d[128 * g:128 * (g + 1), 0:513])))
            elif tc == NCH - 1:
                p.dma("sync", deps, lambda q, g=g, lo=lo: q.dma_start(
                    out=w_sb[tc % 2][:, g * 514:(g + 1) * 514 - 1], in_=xp_d[128 * g:128 * (g + 1), lo:T]))
                wnd.append(p.dma("sync", deps, lambda q, g=g: q.dma_start(
                    out=w_sb[1][:, (g + 1) * 514 - 1:(g + 1) * 514], in_=xp_d[128 * g:128 * (g + 1), 0:1], allow_slow_non_contiguous=True)))
            else:
                wnd.append(p.dma("sync", deps, lambda q, g=g, lo=lo, tc=tc: q.dma_start(
                    out=w_sb[tc % 2][:, g * 514:(g + 1) * 514], in_=xp_d[128 * g:128 * (g + 1), lo:lo + 514])))
        wnd_done = wnd[-1]

        qkv_ev = []
        for og in range(G):
            deps = [wnd_done, w_all]
            if (tc, og) != (0, 0):
                deps.append(last[("qkv_ev", None)])
            else:
                deps.append(ev_prev[("p1", 0)])
                deps.append(ev_prev[("p1", 1)])
            if tc >= 1:
                deps.append(last[("abev_tc", tc - 1)])
                deps.append(last[("oev", tc - 1)])
            def qkvmm(t, tc=tc, og=og):
                for ig in range(G):
                    t.matmul(out=PS[0][:], lhsT=W["WqT"][:, ig * D + 128 * og:ig * D + 128 * (og + 1)],
                             rhs=w_sb[tc % 2][:, ig * 514 + 1:ig * 514 + 513],
                             start=(ig == 0), stop=(ig == G - 1))
                for ig in range(G):
                    t.matmul(out=PS[1][:], lhsT=W["WkT"][:, ig * D + 128 * og:ig * D + 128 * (og + 1)],
                             rhs=w_sb[tc % 2][:, ig * 514:ig * 514 + 512],
                             start=(ig == 0), stop=(ig == G - 1))
                for ig in range(G):
                    t.matmul(out=PS[2][:, 0:2], lhsT=W["WkT"][:, ig * D + 128 * og:ig * D + 128 * (og + 1)],
                             rhs=w_sb[tc % 2][:, ig * 514 + 512:ig * 514 + 514],
                             start=(ig == 0), stop=(ig == G - 1))
                for ig in range(G):
                    t.matmul(out=PS[3][:], lhsT=W["WvT"][:, ig * D + 128 * og:ig * D + 128 * (og + 1)],
                             rhs=w_sb[tc % 2][:, ig * 514:ig * 514 + 512],
                             start=(ig == 0), stop=(ig == G - 1))
                for ig in range(G):
                    last_i = t.matmul(out=PS[2][:, 4:6], lhsT=W["WvT"][:, ig * D + 128 * og:ig * D + 128 * (og + 1)],
                                      rhs=w_sb[tc % 2][:, ig * 514 + 512:ig * 514 + 514],
                                      start=(ig == 0), stop=(ig == G - 1))
                return last_i
            qm = p.step("tensor", deps, qkvmm)
            last[("qkv_mm", tc)] = qm

            def qkvev(v, tc=tc, og=og):
                v.tensor_scalar(out=q_sb[:, og * 512:(og + 1) * 512], in0=PS[0][:],
                                scalar1=B["bqc"][:, og:og + 1], scalar2=0.125,
                                op0=OP.add, op1=OP.mult)
                v.tensor_scalar_add(out=k_sb[:, og * 514:og * 514 + 512], in0=PS[1][:],
                                    scalar1=B["bkc"][:, og:og + 1])
                v.tensor_scalar_add(out=k_sb[:, og * 514 + 512:og * 514 + 514], in0=PS[2][:, 0:2],
                                    scalar1=B["bkc"][:, og:og + 1])
                v.tensor_scalar_add(out=v_sb[:, og * 514:og * 514 + 512], in0=PS[3][:],
                                    scalar1=B["bvc"][:, og:og + 1])
                return v.tensor_scalar_add(out=v_sb[:, og * 514 + 512:og * 514 + 514], in0=PS[2][:, 4:6],
                                           scalar1=B["bvc"][:, og:og + 1])
            qe = p.step("vector", [qm], qkvev)
            last[("qkv_ev", None)] = qe
            qkv_ev.append(qe)

        sc_mm = None
        for g in range(G):
            deps = [qkv_ev[g]]
            if g == 0:
                deps.append(qkv_ev[-1])
            if g >= 1:
                deps.append(sc_mm)
            elif tc >= 1:
                deps.append(last[("smax", tc - 1)])
            def qkprod(v, tc=tc, g=g):
                for sh in range(3):
                    li = v.tensor_mul(out=qk_sb[:, sh * 512:(sh + 1) * 512],
                                      in0=q_sb[:, g * 512:(g + 1) * 512],
                                      in1=k_sb[:, g * 514 + sh:g * 514 + sh + 512])
                return li
            qkp = p.step("vector", deps, qkprod)

            def scmm(t, tc=tc, g=g):
                for sh in range(3):
                    li = t.matmul(out=[PS[4], PS[5], PSPA][sh][0:16, :],
                                  lhsT=s_wsel[:, g * 16:(g + 1) * 16],
                                  rhs=qk_sb[:, sh * 512:(sh + 1) * 512],
                                  start=(g == 0), stop=(g == G - 1))
                return li
            sc_mm = p.step("tensor", [qkp], scmm)

        def smax1(v, tc=tc):
            v.tensor_copy(out=sm_sb[:], in_=PS[4][0:16, :])
            v.tensor_max(out=sm_sb[:], in0=sm_sb[:], in1=PS[5][0:16, :])
            v.tensor_max(out=sm_sb[:], in0=sm_sb[:], in1=PSPA[0:16, :])
            for sh in range(3):
                li = v.tensor_sub(out=e_sb[:, sh * 512:(sh + 1) * 512],
                                  in0=[PS[4], PS[5], PSPA][sh][0:16, :], in1=sm_sb[:])
            return li
        sm1 = p.step("vector", [sc_mm], smax1)
        last[("smax", tc)] = sm1

        def smax2(s, tc=tc):
            for sh in range(3):
                li = s.activation(out=ex_sb[:, sh * 512:(sh + 1) * 512],
                                  in_=e_sb[:, sh * 512:(sh + 1) * 512], func=AF.Exp)
            return li
        sm2 = p.step("scalar", [sm1], smax2)

        def smax3(v, tc=tc):
            v.tensor_add(out=den_sb[:], in0=ex_sb[:, 0:512], in1=ex_sb[:, 512:1024])
            return v.tensor_add(out=den_sb[:], in0=den_sb[:], in1=ex_sb[:, 1024:1536])
        sm3 = p.step("vector", [sm2], smax3)

        def smax4(s, tc=tc):
            return s.activation(out=lden_sb[:], in_=den_sb[:], func=AF.Ln)
        sm4 = p.step("scalar", [sm3], smax4)

        def smax5(v, tc=tc):
            for sh in range(3):
                li = v.tensor_sub(out=e_sb[:, sh * 512:(sh + 1) * 512],
                                  in0=e_sb[:, sh * 512:(sh + 1) * 512], in1=lden_sb[:])
            return li
        sm5 = p.step("vector", [sm4], smax5)

        def smax6(s, tc=tc):
            for sh in range(3):
                li = s.activation(out=a_sb[:, sh * 512:(sh + 1) * 512],
                                  in_=e_sb[:, sh * 512:(sh + 1) * 512], func=AF.Exp)
            return li
        sm6 = p.step("scalar", [sm5], smax6)
        last[("asb", tc)] = sm6

        av_st = None
        for g in range(G):
            deps = [sm6]
            if g >= 1:
                deps.append(last[("abev", None)])
            elif tc >= 1:
                deps.append(last[("av", tc - 1)])
            def bcmm(t, tc=tc, g=g):
                for sh in range(3):
                    li = t.matmul(out=PS[sh][:], lhsT=s_wbc[:, g * 128:(g + 1) * 128],
                                  rhs=a_sb[:, sh * 512:(sh + 1) * 512], start=True, stop=True)
                return li
            bq = p.step("tensor", deps, bcmm)
            last[("bcast_ev", tc)] = bq

            def abev(s, tc=tc, g=g):
                for sh in range(3):
                    li = s.copy(out=ab_sb[:, sh * 512:(sh + 1) * 512], in_=PS[sh][:])
                return li
            ae = p.step("scalar", [bq], abev)
            last[("abev", None)] = ae
            last[("abev_tc", tc)] = ae

            def avc(v, tc=tc, g=g):
                v.tensor_mul(out=av_sb[:, g * 512:(g + 1) * 512], in0=ab_sb[:, 0:512],
                             in1=v_sb[:, g * 514:g * 514 + 512])
                v.tensor_mul(out=qk_sb[:, 0:512], in0=ab_sb[:, 512:1024],
                             in1=v_sb[:, g * 514 + 1:g * 514 + 513])
                v.tensor_add(out=av_sb[:, g * 512:(g + 1) * 512], in0=av_sb[:, g * 512:(g + 1) * 512],
                             in1=qk_sb[:, 0:512])
                v.tensor_mul(out=qk_sb[:, 0:512], in0=ab_sb[:, 1024:1536],
                             in1=v_sb[:, g * 514 + 2:g * 514 + 514])
                return v.tensor_add(out=av_sb[:, g * 512:(g + 1) * 512],
                                    in0=av_sb[:, g * 512:(g + 1) * 512], in1=qk_sb[:, 0:512])
            av = p.step("vector", [ae], avc)
            last[("av", tc)] = av
        av_all = last[("av", tc)]

        oev_step = None
        for tt in range(4):
            deps = [av_all]
            if tt >= 1:
                deps.append(last[("oev", tc)])
            elif tc >= 1:
                deps.append(last[("oev", tc - 1)])
            def womm(t, tc=tc, tt=tt):
                for od in range(2):
                    ps = [PS[3], PSPB][od]
                    for ig in range(G):
                        t.matmul(out=ps[:], lhsT=av_sb[:, ig * 512 + 128 * tt:ig * 512 + 128 * (tt + 1)],
                                 rhs=W["WoT"][:, ig * D + 512 * od:ig * D + 512 * (od + 1)],
                                 start=(ig == 0), stop=False)
                    li = t.matmul(out=ps[:], lhsT=s_ones[:], rhs=s_bo[:, 512 * od:512 * (od + 1)],
                                  start=False, stop=True)
                return li
            wo = p.step("tensor", deps, womm)

            deps = [wo]
            if tc * 4 + tt >= 2:
                deps.append(st_steps[-2])
            def oev(v, tc=tc, tt=tt):
                v.tensor_copy(out=so_sb[tt % 2][:, 0:512], in_=PS[3][:])
                return v.tensor_copy(out=so_sb[tt % 2][:, 512:1024], in_=PSPB[:])
            oe = p.step("vector", deps, oev)
            last[("oev", tc)] = oe

            st_steps.append(p.dma("sync", [oe], lambda q, tc=tc, tt=tt: q.dma_start(
                out=s_tok[512 * tc + 128 * tt:512 * tc + 128 * (tt + 1), :], in_=so_sb[tt % 2][:])))

    p.step("sync", [st_steps[-1]], lambda q: q.nop())
    return [st_steps[-1]]


def emit_route0(nc, p, A, PS, base, regcell, *, xn_fm32, xh_tok, cenT, strilT, iota8,
                xg, slot_d, a_d):

    s_cen = A.sb("s_cen", [128, G * G], F32)
    s_stril = A.sb("s_stril", [128, 128], F32)
    s_io8 = A.sb("s_io8", [128, G], F32)
    xn32 = [A.sb(f"xn32_{i}", [128, D], F32) for i in range(2)]
    lg_sb = A.sb("lg_sb", [128, G], F32)
    sa_sb = A.sb("sa_sb", [128, G], F32)
    mx_sb = A.sb("mx_sb", [128, 8], F32)
    mr_sb = A.sb("mr_sb", [128, G], F32)
    am_sb = A.sb("am_sb", [128, G], F32)
    a_all = A.sb("a_all", [128, NT], F32)
    mall = A.sb("mall", [128, G * NT], F32)
    ones32 = A.sb("ones32", [128, NT], F32)
    incl = A.sb("incl", [128, G * NT], F32)
    cnt_sb = A.sb("cnt_sb", [128, G], F32)
    offs_sb = A.sb("offs_sb", [128, G], F32)
    rank = A.sb("rank", [128, NT], F32)
    slot_f = A.sb("slot_f", [128, NT], F32)
    slot_u = A.sb("slot_u", [128, NT], U32)
    tmp_sb = A.sb("tmp_sb", [128, NT], F32)
    xh_sb = A.sb("xh_stage", [128, NT * D], BF16)
    idxc = [A.sb(f"idxc{i}", [128, 1], U32) for i in range(2)]

    cl = []
    cl.append(p.dma("gpsimd", list(base), lambda g: g.dma_start(
        out=s_cen.rearrange("p (i c) -> p i c", i=G), in_=cenT.rearrange("(i p) c -> p i c", p=128))))
    cl.append(p.dma("gpsimd", list(base), lambda g: g.dma_start(out=s_stril[:], in_=strilT)))
    cl.append(p.dma("gpsimd", list(base), lambda g: g.dma_start(out=s_io8[:], in_=iota8)))
    const_done = cl[-1]

    last = {}
    for i in range(NT):
        deps = [cl[0]]
        if i >= 2:
            deps.append(last[("aff", i - 2)])
        ld = p.dma("sync", deps, lambda q, i=i: q.dma_start(
            out=xn32[i % 2].rearrange("p (g c) -> p g c", g=G),
            in_=xn_fm32.rearrange("(g p) t -> p g t", p=128)[:, :, 128 * i:128 * (i + 1)]))
        def affmm(t, i=i):
            for ig in range(G):
                li = t.matmul(out=PS[0][:, 0:G], lhsT=xn32[i % 2][:, ig * 128:(ig + 1) * 128],
                              rhs=s_cen[:, ig * G:(ig + 1) * G], start=(ig == 0), stop=(ig == G - 1))
            return li
        deps = [ld]
        if i >= 1:
            deps.append(last[("lcp", i - 1)])
        aff = p.step("tensor", deps, affmm)
        last[("aff", i)] = aff
        lcp = p.step("scalar", [aff], lambda s: s.copy(out=lg_sb[:], in_=PS[0][:, 0:G]))
        last[("lcp", i)] = lcp
        sga = p.step("scalar", [lcp], lambda s: s.activation(out=sa_sb[:], in_=lg_sb[:], func=AF.Sigmoid))
        mx = p.step("vector", [sga] + ([last[("eid", i - 1)]] if i else []),
                    lambda v: v.max(out=mx_sb[:], in_=sa_sb[:]))
        ms0 = p.step("vector", [mx], lambda v: v.memset(mx_sb[:, 1:8], -1.0))
        mr = p.step("vector", [ms0], lambda v: v.match_replace(
            out=mr_sb[:], in_to_replace=mx_sb[:], in_values=sa_sb[:], imm_value=0.0))
        am = p.step("vector", [mr], lambda v, i=i: v.tensor_sub(out=am_sb[:], in0=sa_sb[:], in1=mr_sb[:]))
        a1 = p.step("vector", [am], lambda v, i=i: v.tensor_reduce(
            out=a_all[:, i:i + 1], in_=am_sb[:], op=OP.add, axis=AX.X))
        mk = p.step("vector", [a1], lambda v, i=i: v.tensor_scalar(
            out=mall[:, i:G * NT:NT], in0=am_sb[:], scalar1=0.0, scalar2=None, op0=OP.is_gt))
        eid = p.step("vector", [mk, const_done], lambda v, i=i: v.tensor_mul(
            out=mr_sb[:], in0=mall[:, i:G * NT:NT], in1=s_io8[:]))
        last[("eid", i)] = eid
    routing_done = last[("eid", NT - 1)]

    mo = p.step("vector", [routing_done], lambda v: v.memset(ones32[:], 1.0))
    scans = []
    for e in range(G):
        scans.append(p.step("vector", [mo], lambda v, e=e: v.tensor_tensor_scan(
            out=incl[:, e * NT:(e + 1) * NT], data0=ones32[:], data1=mall[:, e * NT:(e + 1) * NT],
            initial=0.0, op0=OP.mult, op1=OP.add)))
    cntc = p.step("vector", [scans[-1]], lambda v: v.tensor_copy(
        out=cnt_sb[:], in_=incl[:, NT - 1:G * NT:NT]))
    offm = p.step("tensor", [cntc, const_done], lambda t: t.matmul(
        out=PS[1][:, 0:G], lhsT=s_stril[:], rhs=cnt_sb[:], start=True, stop=True))
    offe = p.step("scalar", [offm], lambda s: s.copy(out=offs_sb[:], in_=PS[1][:, 0:G]))
    sl0 = p.step("vector", [offe], lambda v: v.memset(slot_f[:], 0.0))
    prev = sl0
    for e in range(G):
        s_a1 = p.step("vector", [prev], lambda v, e=e: v.scalar_tensor_tensor(
            out=rank[:], in0=incl[:, e * NT:(e + 1) * NT],
            scalar=offs_sb[:, e:e + 1], op0=OP.add, op1=OP.subtract,
            in1=mall[:, e * NT:(e + 1) * NT]))
        s_a2 = p.step("vector", [s_a1], lambda v, e=e: v.tensor_scalar_add(
            out=tmp_sb[:], in0=rank[:], scalar1=float(e * CAP)))
        s_a3 = p.step("vector", [s_a2], lambda v, e=e: v.tensor_scalar(
            out=ones32[:], in0=rank[:], scalar1=float(CAP),
            scalar2=float(BIGIDX), op0=OP.is_ge, op1=OP.mult))
        s_a4 = p.step("vector", [s_a3], lambda v, e=e: v.tensor_add(
            out=tmp_sb[:], in0=tmp_sb[:], in1=ones32[:]))
        s_a5 = p.step("vector", [s_a4], lambda v, e=e: v.tensor_mul(
            out=tmp_sb[:], in0=tmp_sb[:], in1=mall[:, e * NT:(e + 1) * NT]))
        prev = p.step("vector", [s_a5], lambda v, e=e: v.tensor_add(
            out=slot_f[:], in0=slot_f[:], in1=tmp_sb[:]))
    slu = p.step("vector", [prev], lambda v: v.tensor_copy(out=slot_u[:], in_=slot_f[:]))

    zt = A.sb("zt", [128, D], BF16)
    ztm = p.step("vector", list(base), lambda v: v.memset(zt[:], 0.0))
    zf = None
    for st_ in range(NST):
        zf = p.dma("sync", [ztm], lambda q, st_=st_: q.dma_start(
            out=xg[128 * st_:128 * (st_ + 1), :], in_=zt[:]))
    def mkreg(g):
        if "bc" not in regcell:
            regcell["bc"] = g.to_reg(8 * CAP - 1)
        return g.dma_start(out=xh_sb[:, 0:D], in_=xh_tok[0:128, :])
    first_ld = p.dma("gpsimd", [slu, zf], mkreg)

    sc_prev = None
    for i in range(NT):
        if i == 0:
            xl = first_ld
        else:
            xl = p.dma("gpsimd", [slu, zf], lambda g, i=i: g.dma_start(
                out=xh_sb[:, i * D:(i + 1) * D], in_=xh_tok[128 * i:128 * (i + 1), :]))
        cpi = p.step("vector", [xl, slu] + ([last[("sct", i - 2)]] if i >= 2 else []),
                     lambda v, i=i: v.tensor_copy(
            out=idxc[i % 2][:], in_=slot_u[:, i:i + 1]))
        s1 = p.dma("gpsimd", [cpi],
                   lambda g, i=i: g.indirect_dma_start(
            out=xg, out_offset=IndirectOffsetOnAxis(ap=idxc[i % 2][:], axis=0),
            in_=xh_sb[:, i * D:(i + 1) * D], in_offset=None,
            bounds_check=regcell["bc"], oob_is_err=False))
        last[("sct", i)] = s1
        sc_prev = s1

    ast = p.dma("sync", [routing_done], lambda q: q.dma_start(out=a_d, in_=a_all[:]))
    ast2 = p.dma("sync", [slu], lambda q: q.dma_start(out=slot_d, in_=slot_u[:]))
    p.step("sync", [sc_prev, ast, ast2], lambda q: q.nop())
    return [sc_prev, ast, ast2]


def emit_ffn0s(nc, p, A, PS, base, iden_sb, regcell, *, xg, w1T, w2T, lgt, lbt, b1t, b2t,
               ones1, z_slots):

    s_lg = A.sb("s_lg", [128, G * G], F32)
    s_lb = A.sb("s_lb", [128, G * G], F32)
    s_b1 = A.sb("s_b1", [128, G * 32], F32)
    s_b2 = A.sb("s_b2", [1, G * D], BF16)
    s_ones = A.sb("s_ones", [1, 128], BF16)
    w1_sb = A.sb("w1_sb", [128, G * F_HALF], BF16)
    w2_sb = A.sb("w2_sb", [128, 32 * D], BF16)
    xgt_sb = [A.sb(f"xgt{i}", [128, D], F32) for i in range(2)]
    xe_sb = A.sb("xe_sb", [128, G * CAP], BF16)
    h_sb = A.sb("h_sb", [128, 32 * CAP], BF16)
    z_sb = [A.sb(f"z_sb{i}", [128, D], BF16) for i in range(2)]

    io_all = A.sb("io_all", [128, NST], U32)
    gidx = [A.sb(f"gidx{i}", [128, 1], U32) for i in range(2)]
    io48 = p.step("gpsimd", list(base), lambda g: g.iota(
        out=io_all[:], base=0, pattern=[[128, NST]], channel_multiplier=1))
    cl = []
    for sb, dr in [(s_lg, lgt), (s_lb, lbt), (s_b1, b1t), (s_ones, ones1)]:
        cl.append(p.dma("gpsimd", list(base), lambda g, sb=sb, dr=dr: g.dma_start(out=sb[:], in_=dr)))
    cl.append(p.dma("gpsimd", list(base), lambda g: g.dma_start(
        out=s_b2.rearrange("p (e c) -> p e c", e=G), in_=b2t.rearrange("e c -> () e c"))))
    const_done = cl[-1]

    last = {}
    CHU = [(0, 512), (512, 256)]
    st_all = []
    for e in range(G):
        w2l = []
        for fk in range(32):
            w2l.append(p.dma("gpsimd", [last[("zmm", e - 1)]] if e else list(base),
                             lambda g, fk=fk, e=e: g.dma_start(
                out=w2_sb[:, fk * D:(fk + 1) * D], in_=w2T[e, 128 * fk:128 * (fk + 1), :])))
        w2_done = w2l[-1]

        tps = []
        for st in range(6):
            k = e * 6 + st
            deps = [const_done]
            if k >= 2:
                deps.append(last[("xemm", k - 2)])
            cpg = p.step("vector", deps + [io48], lambda v, k=k: v.tensor_copy(
                out=gidx[k % 2][:], in_=io_all[:, (k):(k) + 1]))
            xl = p.dma("gpsimd", [cpg], lambda q, e=e, st=st, k=k: q.indirect_dma_start(
                out=xgt_sb[k % 2][:], out_offset=None, in_=xg,
                in_offset=IndirectOffsetOnAxis(ap=gidx[k % 2][:], axis=0),
                bounds_check=regcell["bc"], oob_is_err=False))
            for g_ in range(G):
                kk = k * G + g_
                deps = [xl, const_done]
                if kk >= 2:
                    deps.append(last[("xev", kk - 2)])
                tp = p.step("tensor", deps, lambda t, k=k, g_=g_, kk=kk: t.transpose(
                    out=PS[kk % 2][:, 0:128], in_=xgt_sb[k % 2][:, 128 * g_:128 * (g_ + 1)],
                    identity=iden_sb[:]))
                last[("xemm", k)] = tp
                ev = p.step("vector", [tp], lambda v, e=e, st=st, g_=g_, kk=kk: v.tensor_scalar(
                    out=xe_sb[:, g_ * CAP + 128 * st:g_ * CAP + 128 * (st + 1)], in0=PS[kk % 2][:, 0:128],
                    scalar1=s_lg[:, e * G + g_:e * G + g_ + 1],
                    scalar2=s_lb[:, e * G + g_:e * G + g_ + 1], op0=OP.mult, op1=OP.add))
                last[("xev", kk)] = ev
                tps.append(ev)
        xe_done = tps[-1]

        hlast = None
        for fh in range(2):
            w1l = []
            for ig in range(G):
                deps = [last[("hmm", e, fh - 1)]] if fh else ([last[("hmm", e - 1, 1)]] if e else list(base))
                w1l.append(p.dma("gpsimd", deps, lambda g, ig=ig, fh=fh, e=e: g.dma_start(
                    out=w1_sb[:, ig * F_HALF:(ig + 1) * F_HALF],
                    in_=w1T[e, 128 * ig:128 * (ig + 1), F_HALF * fh:F_HALF * (fh + 1)])))
            w1_done = w1l[-1]
            for fg in range(NFG):
                fga = fh * NFG + fg
                for ci, (c0, cw) in enumerate(CHU):
                    bank = (fg * 2 + ci) % 2
                    deps = [xe_done, w1_done]
                    if (fh, fg, ci) != (0, 0, 0):
                        deps.append(last.get(("hev", bank)))
                    elif e > 0:
                        deps.append(last[("zev", None)])
                    def hmm(t, fg=fg, c0=c0, cw=cw, bank=bank):
                        for ig in range(G):
                            li = t.matmul(out=PS[2 + bank][:, 0:cw],
                                          lhsT=w1_sb[:, ig * F_HALF + 128 * fg:ig * F_HALF + 128 * (fg + 1)],
                                          rhs=xe_sb[:, ig * CAP + c0:ig * CAP + c0 + cw],
                                          start=(ig == 0), stop=(ig == G - 1))
                        return li
                    hm = p.step("tensor", deps, hmm)
                    last[("hmm", e, fh)] = hm
                    he = p.step("scalar", [hm], lambda s, fga=fga, c0=c0, cw=cw, bank=bank, e=e: s.activation(
                        out=h_sb[:, fga * CAP + c0:fga * CAP + c0 + cw], in_=PS[2 + bank][:, 0:cw],
                        func=AF.Relu, bias=s_b1[:, e * 32 + fga:e * 32 + fga + 1]))
                    last[("hev", bank)] = he
                    hlast = he

        for st in range(6):
            i = e * 6 + st
            deps = [hlast, w2_done]
            if st >= 1 or e > 0:
                deps.append(last[("zev", None)])
            def zmm(t, st=st, e=e):
                for od in range(2):
                    for fk in range(32):
                        t.matmul(out=PS[4 + od][:],
                                 lhsT=h_sb[:, fk * CAP + 128 * st:fk * CAP + 128 * (st + 1)],
                                 rhs=w2_sb[:, fk * D + 512 * od:fk * D + 512 * (od + 1)],
                                 start=(fk == 0), stop=False)
                    li = t.matmul(out=PS[4 + od][:], lhsT=s_ones[:],
                                  rhs=s_b2[:, e * D + 512 * od:e * D + 512 * (od + 1)],
                                  start=False, stop=True)
                return li
            zm = p.step("tensor", deps, zmm)
            last[("zmm", e)] = zm
            deps = [zm]
            if len(st_all) >= 2:
                deps.append(st_all[-2])
            ze = p.step("vector", deps, lambda v, i=i: (
                v.tensor_copy(out=z_sb[i % 2][:, 0:512], in_=PS[4][:]),
                v.tensor_copy(out=z_sb[i % 2][:, 512:1024], in_=PS[5][:]))[-1])
            last[("zev", None)] = ze
            st_all.append(p.dma("sync", [ze], lambda q, e=e, st=st, i=i: q.dma_start(
                out=z_slots[e * CAP + 128 * st:e * CAP + 128 * (st + 1), :], in_=z_sb[i % 2][:])))
    p.step("sync", [st_all[-1]], lambda q: q.nop())
    return [st_all[-1]]


def emit_comb0s(nc, p, A, PS, base, regcell, *, z_slots, slot_d, a_d, xn_tok, s_tok, x, out):

    s_slot = A.sb("s_slot", [128, NT], U32)
    s_a = A.sb("s_a", [128, NT], F32)
    idxc = [A.sb(f"idxc{i}", [128, 1], U32) for i in range(2)]
    zb = [A.sb(f"zb{i}", [128, D], BF16) for i in range(2)]
    xnb = [A.sb(f"xnb{i}", [128, D], F32) for i in range(2)]
    sb_ = [A.sb(f"sb_{i}", [128, D], F32) for i in range(2)]
    xb = [A.sb(f"xb{i}", [128, D], F32) for i in range(2)]
    gb = [A.sb(f"gb{i}", [128, D], F32) for i in range(2)]

    c1 = p.dma("sync", list(base), lambda q: q.dma_start(out=s_slot[:], in_=slot_d))
    c2 = p.dma("sync", list(base), lambda q: q.dma_start(out=s_a[:], in_=a_d))
    def mkreg2(g):
        if "bc" not in regcell:
            regcell["bc"] = g.to_reg(8 * CAP - 1)
        return g.dma_start(out=idxc[0][:], in_=slot_d[:, 0:1], allow_slow_non_contiguous=True)
    rg = p.dma("gpsimd", list(base), mkreg2)
    last = {}
    st = []
    for i in range(NT):
        deps = [c1]
        if i >= 2:
            deps.append(last[("g", i - 2)])
        cp = p.step("vector", deps + [c2], lambda v, i=i: v.tensor_copy(
            out=idxc[i % 2][:], in_=s_slot[:, i:i + 1]))
        mz = p.step("vector", [cp], lambda v, i=i: v.memset(zb[i % 2][:], 0.0))
        gz = p.dma("gpsimd", [mz, rg], lambda g, i=i: g.indirect_dma_start(
            out=zb[i % 2][:], out_offset=None, in_=z_slots,
            in_offset=IndirectOffsetOnAxis(ap=idxc[i % 2][:], axis=0),
            bounds_check=regcell["bc"], oob_is_err=False))
        lds = []
        for nm, buf in [("xn", xnb), ("s", sb_), ("x", xb)]:
            dd = [st[i - 2]] if i >= 2 else []
            src_ = {"xn": xn_tok, "s": s_tok, "x": x}[nm]
            lds.append(p.dma("sync", dd, lambda q, i=i, src_=src_, buf=buf: q.dma_start(
                out=buf[i % 2][:], in_=src_[128 * i:128 * (i + 1), :])))
        def comb(v, i=i):
            v.scalar_tensor_tensor(out=gb[i % 2][:], in0=zb[i % 2][:], scalar=s_a[:, i:i + 1],
                                   op0=OP.mult, op1=OP.add, in1=xnb[i % 2][:])
            return v.engine_nop()
        cmb = p.step("vector", [gz] + lds, lambda v, i=i: v.scalar_tensor_tensor(
            out=gb[i % 2][:], in0=zb[i % 2][:], scalar=s_a[:, i:i + 1],
            op0=OP.mult, op1=OP.add, in1=xnb[i % 2][:]))
        last[("g", i)] = cmb
        sil = p.step("scalar", [cmb], lambda s, i=i: s.activation(
            out=gb[i % 2][:], in_=gb[i % 2][:], func=AF.Silu))
        fin = p.step("vector", [sil], lambda v, i=i: (
            v.tensor_mul(out=gb[i % 2][:], in0=gb[i % 2][:], in1=sb_[i % 2][:]),
            v.tensor_add(out=gb[i % 2][:], in0=gb[i % 2][:], in1=xb[i % 2][:]))[-1])
        st.append(p.dma("sync", [fin], lambda q, i=i: q.dma_start(
            out=out[128 * i:128 * (i + 1), :], in_=gb[i % 2][:])))
    p.step("sync", [st[-1]], lambda q: q.nop())
    return [st[-1]]


def emit_attn1(nc, p, A, PS, base, *, xn_fmb, WrvkT, WoutT, tdc, tfc, s1_tok,
               k_d, v_d, sr_d, rw_d):

    w_sb = A.sb("w_sb", [128, G * D], BF16)
    wo_sb = A.sb("wo_sb", [128, G * D], BF16)
    s_td = A.sb("s_td", [128, G], F32)
    s_tf = A.sb("s_tf", [128, G], F32)
    cols = A.sb("cols", [128, 8], F32)
    xn_sb = [A.sb(f"xn_sb{i}", [128, G * 512], BF16) for i in range(2)]
    o_sb = [A.sb(f"o_sb{i}", [128, G * 512], F32) for i in range(2)]
    ewr = A.sb("ewr", [128, T], F32)
    bufA = A.sb("bufA", [128, T], F32)
    bufB = A.sb("bufB", [128, T], F32)
    bufD = A.sb("bufD", [128, T], F32)
    bufE = A.sb("bufE", [128, T], F32)
    bufF = A.sb("bufF", [128, T], F32)
    sr_sb = A.sb("sr_sb", [128, T], BF16)
    y_sb = [A.sb(f"y_sb{i}", [128, D], F32) for i in range(2)]
    rwc_sb = xn_sb


    cl = p.dma("gpsimd", list(base), lambda g: g.dma_start(out=s_td[:], in_=tdc))
    cl2 = p.dma("gpsimd", list(base), lambda g: g.dma_start(out=s_tf[:], in_=tfc))
    wol = [p.dma("gpsimd", list(base), lambda g, ig=ig: g.dma_start(
        out=wo_sb[:, ig * D:(ig + 1) * D], in_=WoutT[128 * ig:128 * (ig + 1), :])) for ig in range(G)]

    last = {}
    stores = []
    for part in range(3):
        wld = [p.dma("gpsimd", [last[("mm", None)]] if part else list(base),
                     lambda g, ig=ig, part=part: g.dma_start(
            out=w_sb[:, ig * D:(ig + 1) * D],
            in_=WrvkT[128 * ig:128 * (ig + 1), D * part:D * (part + 1)])) for ig in range(G)]
        w_done = wld[-1]
        for tc in range(NCH):
            kk = part * NCH + tc
            deps = [w_done]
            if kk >= 2:
                deps.append(last[("mm", kk - 2)])
            xl = p.dma("sync", deps, lambda q, tc=tc, kk=kk: q.dma_start(
                out=xn_sb[kk % 2].rearrange("p (g c) -> p g c", g=G),
                in_=xn_fmb.rearrange("(g p) t -> p g t", p=128)[:, :, 512 * tc:512 * (tc + 1)]))
            evs = []
            for og in range(G):
                bank = og % 2
                deps = [xl, w_done]
                if (kk, og) > (0, 1):
                    deps.append(last.get(("ev", bank)))
                def mmf(t, og=og, kk=kk, bank=bank):
                    for ig in range(G):
                        li = t.matmul(out=PS[bank][:],
                                      lhsT=w_sb[:, ig * D + 128 * og:ig * D + 128 * (og + 1)],
                                      rhs=xn_sb[kk % 2][:, ig * 512:(ig + 1) * 512],
                                      start=(ig == 0), stop=(ig == G - 1))
                    return li
                ms = p.step("tensor", deps, mmf)
                last[("mm", None)] = ms
                last[("mm", kk)] = ms
                if part == 0:
                    ev = p.step("scalar", [ms] + ([stores[-2]] if len(stores) >= 2 else []),
                                lambda s, og=og, kk=kk, bank=bank: s.activation(
                        out=o_sb[kk % 2].bitcast(BF16)[:, og * 1024:og * 1024 + 512], in_=PS[bank][:],
                        func=AF.Sigmoid))
                else:
                    ev = p.step("vector", [ms] + ([stores[-2]] if len(stores) >= 2 else []),
                                lambda v, og=og, kk=kk, bank=bank: v.tensor_copy(
                        out=o_sb[kk % 2][:, og * 512:(og + 1) * 512], in_=PS[bank][:]))
                last[("ev", bank)] = ev
                evs.append(ev)
            dst = [sr_d, v_d, k_d][part]
            if part == 0:
                stores.append(p.dma("sync", [evs[-1]], lambda q, tc=tc, kk=kk: q.dma_start(
                    out=sr_d.rearrange("(g p) t -> p g t", p=128)[:, :, 512 * tc:512 * (tc + 1)],
                    in_=o_sb[kk % 2].bitcast(BF16).rearrange("p (g c) -> p g c", g=G)[:, :, 0:512])))
            else:
                stores.append(p.dma("sync", [evs[-1]], lambda q, tc=tc, kk=kk, dst=dst: q.dma_start(
                    out=dst.rearrange("(g p) t -> p g t", p=128)[:, :, 512 * tc:512 * (tc + 1)],
                    in_=o_sb[kk % 2].rearrange("p (g c) -> p g c", g=G))))
    ph1_done = stores[-1]


    rw_stores = []
    for g in range(G):
        deps = [ph1_done]
        if g >= 1:
            deps.append(last[("rw", g - 1)])
        lk = p.dma("sync", deps, lambda q, g=g: q.dma_start(out=bufA[:], in_=k_d[128 * g:128 * (g + 1), :]))
        lv = p.dma("sync", deps, lambda q, g=g: q.dma_start(out=bufB[:], in_=v_d[128 * g:128 * (g + 1), :]))
        lr = p.dma("sync", deps, lambda q, g=g: q.dma_start(out=sr_sb[:], in_=sr_d[128 * g:128 * (g + 1), :]))

        c1 = p.step("scalar", [cl, cl2] + ([last[("rw", g - 1)]] if g else []), lambda s, g=g: (
            s.activation(out=cols[:, 0:1], in_=s_td[:, g:g + 1], func=AF.Exp),
            s.activation(out=cols[:, 2:3], in_=s_tf[:, g:g + 1], func=AF.Exp))[-1])
        c2 = p.step("scalar", [c1], lambda s, g=g: s.activation(
            out=cols[:, 1:2], in_=cols[:, 0:1], func=AF.Exp, scale=-1.0))
        msetg = p.step("vector", [ph1_done] + ([last[("rw", g - 1)]] if g else []),
                       lambda v: v.memset(ewr[:], 1.0))
        ewt = p.step("vector", [c2, msetg], lambda v, g=g: v.tensor_scalar_mul(
            out=ewr[:], in0=ewr[:], scalar1=cols[:, 1:2]))
        ek = p.step("scalar", [lk, c2], lambda s, g=g: s.activation(out=bufD[:], in_=bufA[:], func=AF.Exp))
        ekv = p.step("vector", [ek, lv], lambda v, g=g: v.tensor_mul(out=bufE[:], in0=bufD[:], in1=bufB[:]))
        sa = p.step("vector", [ekv, ewt], lambda v, g=g: v.tensor_tensor_scan(
            out=bufF[:], data0=ewr[:], data1=bufE[:], initial=0.0, op0=OP.mult, op1=OP.add))
        sb_ = p.step("vector", [sa], lambda v, g=g: v.tensor_tensor_scan(
            out=bufA[:], data0=ewr[:], data1=bufD[:], initial=0.0, op0=OP.mult, op1=OP.add))
        num = p.step("vector", [sb_], lambda v, g=g: (
            v.tensor_scalar_mul(out=bufB[:, 0:1], in0=bufE[:, 0:1], scalar1=cols[:, 2:3]),
            v.scalar_tensor_tensor(out=bufB[:, 1:T], in0=bufE[:, 1:T], scalar=cols[:, 2:3],
                                   op0=OP.mult, op1=OP.add, in1=bufF[:, 0:T - 1]))[-1])
        den = p.step("vector", [num], lambda v, g=g: (
            v.tensor_scalar_mul(out=bufD[:, 0:1], in0=bufD[:, 0:1], scalar1=cols[:, 2:3]),
            v.scalar_tensor_tensor(out=bufD[:, 1:T], in0=bufD[:, 1:T], scalar=cols[:, 2:3],
                                   op0=OP.mult, op1=OP.add, in1=bufA[:, 0:T - 1]))[-1])
        rd = p.step("vector", [den], lambda v, g=g: v.reciprocal(out=bufE[:], in_=bufD[:]))
        wkv = p.step("vector", [rd], lambda v, g=g: v.tensor_mul(out=bufF[:], in0=bufB[:], in1=bufE[:]))
        rw = p.step("vector", [wkv, lr], lambda v, g=g: v.tensor_mul(out=sr_sb[:], in0=bufF[:], in1=sr_sb[:]))
        last[("rw", g)] = rw
        rw_stores.append(p.dma("sync", [rw], lambda q, g=g: q.dma_start(
            out=rw_d[128 * g:128 * (g + 1), :], in_=sr_sb[:])))
    ph2_done = rw_stores[-1]

    st3 = []
    for tc in range(NCH):
        deps = [ph2_done]
        if tc >= 2:
            deps.append(last[("omm", tc - 2)])
        rl = p.dma("sync", deps, lambda q, tc=tc: q.dma_start(
            out=rwc_sb[tc % 2].rearrange("p (g c) -> p g c", g=G),
            in_=rw_d.rearrange("(g p) t -> p g t", p=128)[:, :, 512 * tc:512 * (tc + 1)]))
        for tt in range(4):
            i = tc * 4 + tt
            deps = [rl, wol[-1]]
            if i >= 1:
                deps.append(last[("oev", None)])
            def omm(t, tc=tc, tt=tt):
                for od in range(2):
                    for ig in range(G):
                        li = t.matmul(out=PS[od][:],
                                      lhsT=rwc_sb[tc % 2][:, ig * 512 + 128 * tt:ig * 512 + 128 * (tt + 1)],
                                      rhs=wo_sb[:, ig * D + 512 * od:ig * D + 512 * (od + 1)],
                                      start=(ig == 0), stop=(ig == G - 1))
                return li
            om = p.step("tensor", deps, omm)
            last[("omm", tc)] = om
            deps = [om]
            if i >= 2:
                deps.append(st3[i - 2])
            oe = p.step("vector", deps, lambda v, i=i: (
                v.tensor_copy(out=y_sb[i % 2][:, 0:512], in_=PS[0][:]),
                v.tensor_copy(out=y_sb[i % 2][:, 512:1024], in_=PS[1][:]))[-1])
            last[("oev", None)] = oe
            st3.append(p.dma("sync", [oe], lambda q, i=i: q.dma_start(
                out=s1_tok[128 * i:128 * (i + 1), :], in_=y_sb[i % 2][:])))
    p.step("sync", [st3[-1]], lambda q: q.nop())
    return [st3[-1]]


def emit_ffn1(nc, p, A, PS, PSPAIR, base, iden_sb, *, xn_fmb, xn_fm32, wrT, wmhT,
              wmgT, kfT, vf, iota16, esel, g1_tok, y_d):

    s_wr = A.sb("s_wr", [128, G * NE1], F32)
    s_wmh = A.sb("s_wmh", [128, G * D], BF16)
    s_wmg = A.sb("s_wmg", [128, G * D], BF16)
    s_io16 = A.sb("s_io16", [128, NE1], F32)
    s_esel = A.sb("s_esel", [NE1, NE1 * 128], BF16)
    xn32 = [A.sb(f"xn32_{i}", [128, D], F32) for i in range(2)]
    mx_sb = A.sb("mx_sb", [128, 8], F32)
    mi_sb = A.sb("mi_sb", [128, 8], U32)
    gt_sb = A.sb("gt_sb", [128, 2], F32)
    lg_sb = A.sb("lg_sb", [128, NE1], F32)
    sa_sb = A.sb("sa_sb", [128, NE1], F32)
    mk_sb = A.sb("mk_sb", [128, NE1], F32)
    mx2_col = A.sb("mx2_col", [128, 1], F32)
    wte_sb = A.sb("wte_sb", [128, NE1], F32)
    wteT = A.sb("wteT", [NE1, T], F32)
    wteTb = A.sb("wteTb", [NE1, T], BF16)
    kf_sb = A.sb("kf_sb", [128, 4 * NE1 * 256], BF16)
    vf_sb = A.sb("vf_sb", [128, 4 * NE1 * 2 * 128], BF16)
    xnb_sb = [A.sb(f"xnb_sb{i}", [128, G * 512], BF16) for i in range(2)]
    q_sb = A.sb("q_sb", [128, G * 512], BF16)
    wb_sb = A.sb("wb_sb", [128, NE1 * 512], BF16)
    act_sb = A.sb("act_sb", [128, 2 * 512], BF16)
    y_sb = A.sb("y_sb", [128, 4 * 512], BF16)
    go_sb = [A.sb(f"go_sb{i}", [128, D], F32) for i in range(2)]


    cl = []
    cl.append(p.dma("gpsimd", list(base), lambda g: g.dma_start(
        out=s_wr.rearrange("p (i c) -> p i c", i=G), in_=wrT.rearrange("(i p) c -> p i c", p=128))))
    for ig in range(G):
        cl.append(p.dma("gpsimd", list(base), lambda g, ig=ig: g.dma_start(
            out=s_wmh[:, ig * D:(ig + 1) * D], in_=wmhT[128 * ig:128 * (ig + 1), :])))
        cl.append(p.dma("gpsimd", list(base), lambda g, ig=ig: g.dma_start(
            out=s_wmg[:, ig * D:(ig + 1) * D], in_=wmgT[128 * ig:128 * (ig + 1), :])))
    cl.append(p.dma("gpsimd", list(base), lambda g: g.dma_start(out=s_io16[:], in_=iota16)))
    cl.append(p.dma("gpsimd", list(base), lambda g: g.dma_start(out=s_esel[:], in_=esel)))
    const_done = cl[-1]

    last = {}
    for i in range(NT):
        deps = [cl[0]]
        if i >= 2:
            deps.append(last[("aff", i - 2)])
        ld = p.dma("sync", deps, lambda q, i=i: q.dma_start(
            out=xn32[i % 2].rearrange("p (g c) -> p g c", g=G),
            in_=xn_fm32.rearrange("(g p) t -> p g t", p=128)[:, :, 128 * i:128 * (i + 1)]))
        def affmm(t, i=i):
            for ig in range(G):
                li = t.matmul(out=PS[5][:, 0:NE1], lhsT=xn32[i % 2][:, ig * 128:(ig + 1) * 128],
                              rhs=s_wr[:, ig * NE1:(ig + 1) * NE1], start=(ig == 0), stop=(ig == G - 1))
            return li
        deps = [ld]
        if i >= 1:
            deps.append(last[("wte", i - 1)])
            deps.append(last[("sg", i - 1)])
            deps.append(last[("lcp", i - 1)])
        aff = p.step("tensor", deps, affmm)
        last[("aff", i)] = aff
        lcp = p.step("scalar", [aff], lambda s: s.copy(out=lg_sb[:], in_=PS[5][:, 0:NE1]))
        last[("lcp", i)] = lcp
        sga = p.step("scalar", [lcp], lambda s: s.activation(out=sa_sb[:], in_=lg_sb[:],
                                                             func=AF.Sigmoid))
        last[("sg", i)] = sga
        mx = p.step("vector", [sga], lambda v: v.max(out=mx_sb[:], in_=sa_sb[:]))
        ms0 = p.step("vector", [mx], lambda v: v.memset(mx_sb[:, 2:8], -1.0))
        mr = p.step("vector", [ms0], lambda v: v.match_replace(
            out=mk_sb[:], in_to_replace=mx_sb[:], in_values=sa_sb[:], imm_value=0.0))
        wt = p.step("vector", [mr], lambda v: v.tensor_sub(
            out=wte_sb[:], in0=sa_sb[:], in1=mk_sb[:]))
        last[("wte", i)] = wt
        tp = p.step("tensor", [wt, const_done], lambda t, i=i: t.transpose(out=PSPAIR[0:NE1, 0:128], in_=wte_sb[:], identity=iden_sb[:]))
        ev = p.step("vector", [tp], lambda v, i=i: v.tensor_copy(
            out=wteT[:, 128 * i:128 * (i + 1)], in_=PSPAIR[0:NE1, 0:128]))
        last[("wte", i)] = ev
    wcast = p.step("vector", [ev], lambda v: v.tensor_copy(out=wteTb[:], in_=wteT[:]))
    phA_done = wcast

    y_stores = []
    for hh in range(2):
        kl = []
        for h4 in range(4):
            h = hh * 4 + h4
            for e in range(NE1):
                deps = [last[("ymm", None)]] if (hh and e == 0 and h4 == 0) else list(base)
                kl.append(p.dma("gpsimd", deps, lambda g, h=h, h4=h4, e=e: g.dma_start(
                    out=kf_sb[:, (h4 * NE1 + e) * 256:(h4 * NE1 + e) * 256 + 256],
                    in_=kfT[h, e])))
                kl.append(p.dma("gpsimd", deps, lambda g, h=h, h4=h4, e=e: g.dma_start(
                    out=vf_sb[:, (h4 * NE1 + e) * 256:(h4 * NE1 + e) * 256 + 256
                              ].rearrange("p (sc c) -> p sc c", sc=2),
                    in_=vf[h, e].rearrange("(sc p) c -> p sc c", p=128))))
        kv_done = kl[-1]

        for tc in range(NCH):
            kk = hh * NCH + tc
            deps = [const_done, phA_done]
            if kk >= 2:
                deps.append(last[("qmm", kk - 2)])
            xl = p.dma("sync", deps, lambda q, tc=tc, kk=kk: q.dma_start(
                out=xnb_sb[kk % 2].rearrange("p (g c) -> p g c", g=G),
                in_=xn_fmb.rearrange("(g p) t -> p g t", p=128)[:, :, 512 * tc:512 * (tc + 1)]))
            for og in range(G):
                bank = og % 2
                deps = [xl]
                if (kk, og) > (0, 1):
                    deps.append(last.get(("qev", bank)))
                def qmm(t, og=og, kk=kk, bank=bank):
                    for ig in range(G):
                        li = t.matmul(out=PS[bank][:],
                                      lhsT=s_wmh[:, ig * D + 128 * og:ig * D + 128 * (og + 1)],
                                      rhs=xnb_sb[kk % 2][:, ig * 512:(ig + 1) * 512],
                                      start=(ig == 0), stop=(ig == G - 1))
                    return li
                qm = p.step("tensor", deps, qmm)
                last[("qmm", kk)] = qm
                qe = p.step("vector", [qm], lambda v, og=og, bank=bank: v.tensor_copy(
                    out=q_sb[:, og * 512:(og + 1) * 512], in_=PS[bank][:]))
                last[("qev", bank)] = qe
            q_done = last[("qev", (G - 1) % 2)]

            wb_done = None
            for ep in range(NE1 // 2):
                deps2 = [phA_done]
                if ep >= 1:
                    deps2.append(last[("wbev", None)])
                elif kk >= 1:
                    deps2.append(last[("actm", None)])
                def wbmm2(t, tc=tc, ep=ep):
                    t.matmul(out=PS[2][:], lhsT=s_esel[:, (2 * ep) * 128:(2 * ep + 1) * 128],
                             rhs=wteTb[:, 512 * tc:512 * (tc + 1)], start=True, stop=True)
                    return t.matmul(out=PS[3][:], lhsT=s_esel[:, (2 * ep + 1) * 128:(2 * ep + 2) * 128],
                                    rhs=wteTb[:, 512 * tc:512 * (tc + 1)], start=True, stop=True)
                wm = p.step("tensor", deps2, wbmm2)
                def wbev(s, ep=ep):
                    s.copy(out=wb_sb[:, (2 * ep) * 512:(2 * ep + 1) * 512], in_=PS[2][:])
                    return s.copy(out=wb_sb[:, (2 * ep + 1) * 512:(2 * ep + 2) * 512], in_=PS[3][:])
                wb_done = p.step("scalar", [wm], wbev)
                last[("wbev", None)] = wb_done

            for h4 in range(4):
                h = hh * 4 + h4
                for e in range(NE1):
                    deps = [kv_done, q_done, wb_done]
                    if last.get(("gelu", None)) is not None:
                        deps.append(last[("gelu", None)])
                    def scmm(t, h4=h4, e=e, h=h):
                        base = (h4 * NE1 + e) * 256
                        for sc in range(2):
                            li = t.matmul(out=PSPAIR[:, 512 * sc:512 * (sc + 1)],
                                          lhsT=kf_sb[:, base + 128 * sc:base + 128 * (sc + 1)],
                                          rhs=q_sb[:, h * 512:(h + 1) * 512],
                                          start=True, stop=True)
                        return li
                    sm = p.step("tensor", deps, scmm)
                    deps = [sm]
                    if last.get(("ymm", None)) is not None:
                        deps.append(last[("ymm", None)])
                    ge = p.step("scalar", deps, lambda s: s.activation(
                        out=act_sb[:], in_=PSPAIR[:], func=AF.Gelu))
                    last[("gelu", None)] = ge
                    def actm(v, e=e):
                        v.tensor_mul(out=act_sb[:, 0:512], in0=act_sb[:, 0:512],
                                     in1=wb_sb[:, e * 512:(e + 1) * 512])
                        return v.tensor_mul(out=act_sb[:, 512:1024], in0=act_sb[:, 512:1024],
                                            in1=wb_sb[:, e * 512:(e + 1) * 512])
                    am = p.step("vector", [ge], actm)
                    last[("actm", None)] = am

                    deps = [am]
                    if e == 0 and last.get(("yev", None)) is not None:
                        deps.append(last[("yev", None)])
                    def ymm(t, h4=h4, e=e):
                        for sc in range(2):
                            li = t.matmul(out=PS[4][:],
                                          lhsT=vf_sb[:, (h4 * NE1 + e) * 256 + 128 * sc:
                                                     (h4 * NE1 + e) * 256 + 128 * (sc + 1)],
                                          rhs=act_sb[:, 512 * sc:512 * (sc + 1)],
                                          start=(e == 0 and sc == 0), stop=(e == NE1 - 1 and sc == 1))
                        return li
                    ym = p.step("tensor", deps, ymm)
                    last[("ymm", None)] = ym
                ye = p.step("vector", [ym], lambda v, h4=h4: v.tensor_copy(
                    out=y_sb[:, h4 * 512:(h4 + 1) * 512], in_=PS[4][:]))
                last[("yev", None)] = ye
            y_stores.append(p.dma("sync", [ye], lambda q, hh=hh, tc=tc: q.dma_start(
                out=y_d.rearrange("(g p) t -> p g t", p=128)[:, 4 * hh:4 * (hh + 1),
                                                            512 * tc:512 * (tc + 1)],
                in_=y_sb.rearrange("p (g c) -> p g c", g=4))))
    phB_done = y_stores[-1]

    st3 = []
    for tc in range(NCH):
        deps = [phB_done]
        if tc >= 2:
            deps.append(last[("gmm", tc - 2)])
        yl = p.dma("sync", deps, lambda q, tc=tc: q.dma_start(
            out=xnb_sb[tc % 2].rearrange("p (g c) -> p g c", g=G),
            in_=y_d.rearrange("(g p) t -> p g t", p=128)[:, :, 512 * tc:512 * (tc + 1)]))
        for tt in range(4):
            i = tc * 4 + tt
            deps = [yl]
            if i >= 1:
                deps.append(last[("gev", None)])
            def gmm(t, tc=tc, tt=tt):
                for od in range(2):
                    for ig in range(G):
                        li = t.matmul(out=PS[od][:],
                                      lhsT=xnb_sb[tc % 2][:, ig * 512 + 128 * tt:ig * 512 + 128 * (tt + 1)],
                                      rhs=s_wmg[:, ig * D + 512 * od:ig * D + 512 * (od + 1)],
                                      start=(ig == 0), stop=(ig == G - 1))
                return li
            gm = p.step("tensor", deps, gmm)
            last[("gmm", tc)] = gm
            deps = [gm]
            if i >= 2:
                deps.append(st3[i - 2])
            gevs = p.step("scalar", deps, lambda s, i=i: (
                s.activation(out=go_sb[i % 2][:, 0:512], in_=PS[0][:], func=AF.Silu),
                s.activation(out=go_sb[i % 2][:, 512:1024], in_=PS[1][:], func=AF.Silu))[-1])
            last[("gev", None)] = gevs
            st3.append(p.dma("sync", [gevs], lambda q, i=i: q.dma_start(
                out=g1_tok[128 * i:128 * (i + 1), :], in_=go_sb[i % 2][:])))
    p.step("sync", [st3[-1]], lambda q: q.nop())
    return [st3[-1]]


def emit_comb1(nc, p, A, PS, base, *, g1_tok, s_tok, x, out):
    bufs = {nm: [A.sb(f"cb_{nm}{i}", [128, D], F32) for i in range(2)]
            for nm in ["g1_tok", "s_tok", "x"]}
    srcs = {"g1_tok": g1_tok, "s_tok": s_tok, "x": x}
    o_sb = [A.sb(f"co_sb{i}", [128, D], F32) for i in range(2)]
    st = []
    for i in range(NT):
        lds = []
        for nm in srcs:
            deps = list(base) + ([st[i - 2]] if i >= 2 else [])
            lds.append(p.dma("sync", deps, lambda q, nm=nm, i=i: q.dma_start(
                out=bufs[nm][i % 2][:], in_=srcs[nm][128 * i:128 * (i + 1), :])))
        def comb(v, i=i):
            v.tensor_mul(out=o_sb[i % 2][:], in0=bufs["s_tok"][i % 2][:], in1=bufs["g1_tok"][i % 2][:])
            return v.tensor_add(out=o_sb[i % 2][:], in0=o_sb[i % 2][:], in1=bufs["x"][i % 2][:])
        c3 = p.step("vector", lds, comb)
        st.append(p.dma("sync", [c3], lambda q, i=i: q.dma_start(
            out=out[128 * i:128 * (i + 1), :], in_=o_sb[i % 2][:])))
    return [st[-1]]


def build_fused(dbg=False):
    nc = bass.Bass("TRN2", target_bir_lowering=False, detect_race_conditions=False)
    E = {}
    def ein(name, shape, dtype=F32):
        E[name] = nc.dram_tensor(name, shape, dtype, kind="ExternalInput").ap()
        return E[name]
    ein("x", [T, D]); ein("iden", [128, 128])
    for nm in ["WpT", "WqT", "WkT", "WvT", "WoT"]:
        ein(nm, [D, D])
    for nm in ["bpc", "bqc", "bkc", "bvc"]:
        ein(nm, [128, G])
    ein("bo_row", [1, D]); ein("wsel", [G, 128, 16]); ein("wbc", [G, 16, 128])
    ein("ones1", [1, 128])
    ein("cenT", [D, G]); ein("strilT", [128, 128]); ein("iota8", [128, G])
    ein("w1T", [G, D, 4096]); ein("w2T", [G, 4096, D])
    ein("lgt", [128, G * G]); ein("lbt", [128, G * G]); ein("b1t", [128, G * 32])
    ein("b2t", [G, D])
    ein("WrvkT", [D, 3 * D]); ein("WoutT", [D, D]); ein("tdc", [128, G]); ein("tfc", [128, G])
    ein("wrT", [D, NE1]); ein("wmhT", [D, D]); ein("wmgT", [D, D])
    ein("kfT", [NH, NE1, 128, 256]); ein("vf", [NH, NE1, 256, 128])
    ein("iota16", [128, NE1]); ein("esel", [NE1, NE1 * 128])
    out = nc.dram_tensor("out", [T, D], F32, kind="ExternalOutput").ap()
    I = {}
    _dbg_names = set(os.environ.get("KDBG", "").split(","))
    def din(name, shape, dtype):
        kind = {"kind": "ExternalOutput"} if (dbg and name in _dbg_names) else {}
        I[name] = nc.dram_tensor(name, shape, dtype, **kind).ap()
        return I[name]
    din("xn_fm32", [D, T], F32); din("xn_fmb", [D, T], BF16); din("xn_tok", [T, D], F32)
    din("xh_tok", [T, D], BF16); din("s_tok", [T, D], F32); din("xp_d", [D, T], BF16)
    din("xg", [8 * CAP, D], BF16); din("slot_d", [128, NT], U32); din("a_d", [128, NT], F32)
    din("z_slots", [8 * CAP, D], BF16); din("x1_d", [T, D], F32)
    din("xn1_fm32", [D, T], F32); din("xn1_fmb", [D, T], BF16); din("xn1_tok", [T, D], F32)
    din("k_d", [D, T], F32); din("v_d", [D, T], F32); din("sr_d", [D, T], BF16)
    din("rw_d", [D, T], BF16); din("s1_tok", [T, D], F32)
    din("y_d", [D, T], BF16); din("g1_tok", [T, D], F32)

    PSPAIR = nc.alloc_psum_tensor("PSPAIR", [128, 1024], F32)
    PS = [nc.alloc_psum_tensor(f"PSB{i}", [128, 512], F32) for i in range(6)]
    PSPA = PSPAIR[:, 0:512]
    PSPB = PSPAIR[:, 512:1024]

    iden_sb = nc.alloc_sbuf_tensor("iden_sb", [128, 128], F32)
    A = Arena(nc, ((nc.sbuf_base + 31) // 32) * 32, 229344)

    p = Prog(nc)
    f0 = p.dma("sync", [], lambda q: q.dma_start(out=iden_sb[:], in_=E["iden"]))
    base = [f0]
    regcell = {}

    emit_ln(nc, p, A, PS, base, x=E["x"], iden_sb=iden_sb, xn_fm32=I["xn_fm32"],
            xn_fmb=I["xn_fmb"], xn_tok=I["xn_tok"], with_xhat=True,
            xh_fmb=None, xh_tok=I["xh_tok"], tag="l0")
    base = [stage_fence(p, nc)]; A.reset()

    emit_attn0(nc, p, A, PS, PSPA, PSPB, base, xn_fmb=I["xn_fmb"],
               wts={nm: E[nm] for nm in ["WpT", "WqT", "WkT", "WvT", "WoT"]},
               bcols={nm: E[nm] for nm in ["bpc", "bqc", "bkc", "bvc"]},
               bo_row=E["bo_row"], wsel=E["wsel"], wbc=E["wbc"], ones1=E["ones1"],
               s_tok=I["s_tok"], xp_d=I["xp_d"])
    base = [stage_fence(p, nc)]; A.reset()

    emit_route0(nc, p, A, PS, base, regcell, xn_fm32=I["xn_fm32"], xh_tok=I["xh_tok"],
                cenT=E["cenT"], strilT=E["strilT"], iota8=E["iota8"],
                xg=I["xg"], slot_d=I["slot_d"], a_d=I["a_d"])
    base = [stage_fence(p, nc)]; A.reset()

    emit_ffn0s(nc, p, A, PS, base, iden_sb, regcell, xg=I["xg"], w1T=E["w1T"], w2T=E["w2T"],
               lgt=E["lgt"], lbt=E["lbt"], b1t=E["b1t"], b2t=E["b2t"],
               ones1=E["ones1"], z_slots=I["z_slots"])
    base = [stage_fence(p, nc)]; A.reset()

    emit_comb0s(nc, p, A, PS, base, regcell, z_slots=I["z_slots"], slot_d=I["slot_d"],
                a_d=I["a_d"], xn_tok=I["xn_tok"], s_tok=I["s_tok"], x=E["x"], out=I["x1_d"])
    base = [stage_fence(p, nc)]; A.reset()

    emit_ln(nc, p, A, PS, base, x=I["x1_d"], iden_sb=iden_sb, xn_fm32=I["xn1_fm32"],
            xn_fmb=I["xn1_fmb"], xn_tok=I["xn1_tok"], with_xhat=False, tag="l1")
    base = [stage_fence(p, nc)]; A.reset()

    emit_attn1(nc, p, A, PS, base, xn_fmb=I["xn1_fmb"], WrvkT=E["WrvkT"], WoutT=E["WoutT"],
               tdc=E["tdc"], tfc=E["tfc"], s1_tok=I["s1_tok"],
               k_d=I["k_d"], v_d=I["v_d"], sr_d=I["sr_d"], rw_d=I["rw_d"])
    base = [stage_fence(p, nc)]; A.reset()

    emit_ffn1(nc, p, A, PS, PSPAIR, base, iden_sb, xn_fmb=I["xn1_fmb"], xn_fm32=I["xn1_fm32"],
              wrT=E["wrT"], wmhT=E["wmhT"], wmgT=E["wmgT"], kfT=E["kfT"], vf=E["vf"],
              iota16=E["iota16"], esel=E["esel"], g1_tok=I["g1_tok"], y_d=I["y_d"])
    base = [stage_fence(p, nc)]; A.reset()

    emit_comb1(nc, p, A, PS, base, g1_tok=I["g1_tok"], s_tok=I["s1_tok"], x=I["x1_d"], out=out)
    stage_fence(p, nc)
    return p.build()


def kernel_fused(**inp):
    xs = _f32c(inp["L0_attn_x"])
    NCORE = xs.shape[0]
    lng = _f32c(inp["L0_ffn_ln_g"]); lnb = _f32c(inp["L0_ffn_ln_b"])
    w1 = _f32c(inp["L0_ffn_w1"]); b1 = _f32c(inp["L0_ffn_b1"])
    w2 = _f32c(inp["L0_ffn_w2"]); b2 = _f32c(inp["L0_ffn_b2"])
    wsel = np.zeros((8, 128, 16), np.float32)
    wbc = np.zeros((8, 16, 128), np.float32)
    for g in range(8):
        for pp in range(128):
            h = 2 * g + pp // 64
            wsel[g, pp, h] = 1.0
            wbc[g, h, pp] = 1.0
    stril = np.zeros((128, 128), np.float32)
    for _p in range(128):
        stril[_p, _p + 1:] = 1.0
    lgt = np.zeros((128, 64), np.float32)
    lbt = np.zeros((128, 64), np.float32)
    b1t = np.zeros((128, 256), np.float32)
    for e in range(8):
        lgt[:, e * 8:(e + 1) * 8] = lng[e].reshape(8, 128).T
        lbt[:, e * 8:(e + 1) * 8] = lnb[e].reshape(8, 128).T
        b1t[:, e * 32:(e + 1) * 32] = b1[e].reshape(32, 128).T
    esel = np.zeros((16, 16 * 128), np.float32)
    for e in range(16):
        esel[e, e * 128:(e + 1) * 128] = 1.0
    shared = {
        "iden": _IDEN,
        "WpT": _f32c(np.asarray(inp["L0_attn_Wp"]).T), "WqT": _f32c(np.asarray(inp["L0_attn_Wq"]).T),
        "WkT": _f32c(np.asarray(inp["L0_attn_Wk"]).T), "WvT": _f32c(np.asarray(inp["L0_attn_Wv"]).T),
        "WoT": _f32c(np.asarray(inp["L0_attn_Wo"]).T),
        "bpc": _f32c(_f32c(inp["L0_attn_bp"]).reshape(8, 128).T),
        "bqc": _f32c(_f32c(inp["L0_attn_bq"]).reshape(8, 128).T),
        "bkc": _f32c(_f32c(inp["L0_attn_bk"]).reshape(8, 128).T),
        "bvc": _f32c(_f32c(inp["L0_attn_bv"]).reshape(8, 128).T),
        "bo_row": _f32c(inp["L0_attn_bo"]).reshape(1, 1024),
        "wsel": wsel, "wbc": wbc, "ones1": np.ones((1, 128), np.float32),
        "cenT": _f32c(np.asarray(inp["L0_ffn_centroids"]).T), "strilT": stril,
        "iota8": np.broadcast_to(np.arange(8, dtype=np.float32), (128, 8)).copy(),
        "w1T": _f32c(w1.transpose(0, 2, 1)), "w2T": _f32c(w2.transpose(0, 2, 1)),
        "lgt": lgt, "lbt": lbt, "b1t": b1t, "b2t": b2,
        "WrvkT": _f32c(np.asarray(inp["L1_attn_W_rvk"]).T),
        "WoutT": _f32c(np.asarray(inp["L1_attn_W_out"]).T),
        "tdc": _f32c(_f32c(inp["L1_attn_time_decay"]).reshape(8, 128).T),
        "tfc": _f32c(_f32c(inp["L1_attn_time_first"]).reshape(8, 128).T),
        "wrT": _f32c(np.asarray(inp["L1_ffn_w_router"]).T),
        "wmhT": _f32c(np.asarray(inp["L1_ffn_w_mh"]).T),
        "wmgT": _f32c(np.asarray(inp["L1_ffn_w_mg"]).T),
        "kfT": _f32c(np.asarray(inp["L1_ffn_k_ffwd"]).transpose(0, 1, 3, 2)),
        "vf": _f32c(inp["L1_ffn_v_ffwd"]),
        "iota16": np.broadcast_to(np.arange(16, dtype=np.float32), (128, 16)).copy(),
        "esel": esel,
    }
    maps = [dict(shared, x=np.ascontiguousarray(xs[c])) for c in range(NCORE)]
    nc = _prog("fused", build_fused)
    r = run_bass_kernel_spmd(nc, maps, core_ids=list(range(NCORE))).results
    return np.stack([r[c]["out"] for c in range(NCORE)]).astype(np.float32)
